# version 16
# baseline (speedup 1.0000x reference)
import numpy as np
import ml_dtypes
from contextlib import ExitStack

import concourse.bass as bass
import concourse.mybir as mybir
from concourse.bass_utils import run_bass_kernel_spmd

F32 = mybir.dt.float32
BF16 = mybir.dt.bfloat16
I32 = mybir.dt.int32
AF = mybir.ActivationFunctionType
ALU = mybir.AluOpType
BFNP = ml_dtypes.bfloat16

B, T, D, H, DH = 2, 8192, 1024, 16, 64
NCORE = 8
TILE = 512
NT = T // TILE
CH = 128
CPT = TILE // CH
SLICE = T // 4
NEG = -1.0e6


def _build(dbg_taps=None):
    nc = bass.Bass("TRN2", target_bir_lowering=False, debug=False)

    xt = nc.dram_tensor("xt", [D, T], F32, kind="ExternalInput").ap()
    xsl = nc.dram_tensor("xsl", [D, SLICE + 1], F32, kind="ExternalInput").ap()
    wp1 = nc.dram_tensor("wp1", [D, 640], F32, kind="ExternalInput").ap()
    wtm = nc.dram_tensor("wtm", [D, 512], F32, kind="ExternalInput").ap()
    wffn = nc.dram_tensor("wffn", [D, 3072], F32, kind="ExternalInput").ap()
    wout = nc.dram_tensor("wout", [D, D], F32, kind="ExternalInput").ap()
    muin = nc.dram_tensor("mu", [128, 8, 2], F32, kind="ExternalInput").ap()
    gbin = nc.dram_tensor("gb", [98, 1], F32, kind="ExternalInput").ap()
    ridx = nc.dram_tensor("ridx", [1, 1], I32, kind="ExternalInput").ap()
    c_in = {}
    for nm, sh, dt in (("c_maskS", [128, 128], F32), ("c_maskI", [128, 128], F32),
                       ("c_maskSR", [128, 128], F32),
                       ("c_sel0", [128, 128], F32), ("c_sel1", [128, 128], F32),
                       ("c_halfF", [128, 128], F32), ("c_halfB", [128, 128], BF16),
                       ("c_idB", [128, 128], BF16), ("c_idF", [128, 128], F32),
                       ("c_ss", [128, 4, 128], BF16), ("c_o1b", [128, 1], BF16),
                       ("c_o1f", [1, 128], F32), ("c_eps", [128, 2], F32)):
        c_in[nm] = nc.dram_tensor(nm, sh, dt, kind="ExternalInput").ap()

    outp = nc.dram_tensor("outp", [D, SLICE], F32, kind="ExternalOutput").ap()
    agin = nc.dram_tensor("agin", [256, T], BF16, kind="Internal")
    agout = nc.dram_tensor("agout", [4, 256, T], BF16, kind="Internal")
    dresid = nc.dram_tensor("dresid", [D, SLICE], F32, kind="Internal")
    dbg_out = {}
    if dbg_taps:
        for nm, sh, dt in dbg_taps:
            dbg_out[nm] = nc.dram_tensor(nm, sh, dt, kind="ExternalOutput").ap()

    ctx = ExitStack()
    SB = lambda n, sh, dt: ctx.enter_context(nc.sbuf_tensor(n, sh, dt))
    PS_ = lambda n: ctx.enter_context(nc.psum_tensor(n, [128, 512], F32))

    xtile = [SB(f"xtile{i}", [128, 8, TILE], F32) for i in range(2)]
    _ybuf = SB("ybuf0", [128, TILE], BF16)
    ybuf = [_ybuf, _ybuf]
    _y2buf = SB("y2buf0", [128, TILE], BF16)
    y2buf = [_y2buf, _y2buf]
    xnb = SB("xnb", [128, 8, TILE], BF16)
    t1b = [SB(f"t1b{i}", [128, TILE], F32) for i in range(2)]
    rrep = SB("rrep", [128, TILE], F32)
    mrep = SB("mrep", [128, TILE], F32)
    rowq = [SB(f"rowq{i}", [1, TILE], F32) for i in range(5)]
    mrrow = SB("mrrow", [1, TILE + 1], F32)
    hcol = [SB(f"hcol{i}", [128, 1], F32) for i in range(2)]
    wstg = SB("wstg", [128, 8, 128], F32)
    ebuf = SB("ebuf", [128, TILE], F32)
    tbb = SB("tbb", [128, TILE], F32)
    sqb = SB("sqb", [128, TILE], BF16)
    fmt = sqb
    mub = SB("mub", [128, 8, 2], F32)
    mu1b = SB("mu1b", [128, 8, 2], F32)
    kc_ = {}
    for nm, sh, dt in (("kmask2", [128, 256], F32),
                       ("kmaskSR", [128, 128], F32),
                       ("ksel0", [128, 128], F32), ("ksel1", [128, 128], F32),
                       ("khalfF", [128, 128], F32), ("khalfB", [128, 128], BF16),
                       ("kidB", [128, 128], BF16), ("kidF", [128, 128], F32),
                       ("kss", [128, 4, 128], BF16), ("ko1b", [128, 1], BF16),
                       ("ko1f", [1, 128], F32), ("keps", [128, 2], F32)):
        kc_[nm] = SB(nm, sh, dt)
    ksel = [kc_["ksel0"], kc_["ksel1"]]
    kselb = [SB(f"kselb{i}", [128, 128], BF16) for i in range(2)]
    kgb = SB("kgb", [128, 1], F32)
    ngb = SB("ngb", [128, 1], F32)
    sridx = SB("sridx", [1, 1], I32)

    pb = [PS_(f"pb{i}") for i in range(8)]

    sctx = ExitStack()
    SBs = lambda n, sh, dt: sctx.enter_context(nc.sbuf_tensor(n, sh, dt))
    wp1b = SBs("wp1b", [128, 8, 640], BF16)
    wtmb = SBs("wtmb", [128, 8, 512], BF16)
    qk01 = [SBs(f"qk01s{i}", [128, CPT, 256], BF16) for i in range(2)]
    qk23 = [SBs(f"qk23s{i}", [128, CPT, 256], BF16) for i in range(2)]
    ktmv = [SBs(f"ktmvs{i}", [128, CPT, 512], BF16) for i in range(2)]
    abb = SBs("abb", [128, TILE], F32)
    lntb = SBs("lntb", [128, TILE], F32)
    sscb = SBs("sscb", [128, TILE], F32)
    t1r = SBs("t1r", [128, TILE], F32)
    t2r = SBs("t2r", [128, TILE], F32)
    t3r = sscb
    lnssb = lntb
    _gneg = SBs("gnegs", [128, TILE], F32)
    _colsrc = SBs("colsrcs", [128, TILE], F32)
    gneg = [_gneg, SBs("gnegs1", [128, TILE], F32)]
    _rows3 = SBs("rows3_0", [128, 3 * TILE], F32)
    rows3 = [_rows3, _rows3]
    r3A = lambda i, lc: rows3[i][:, lc * 384:lc * 384 + 128]
    r3M = lambda i, lc: rows3[i][:, lc * 384 + 128:lc * 384 + 256]
    r3C = lambda i, lc: rows3[i][:, lc * 384 + 256:lc * 384 + 384]
    rows3b = [SBs(f"rows3b{i}", [128, 3, 3 * TILE], BF16) for i in range(2)]
    colsrc = [_colsrc, SBs("colsrcs1", [128, TILE], F32)]
    escr = SBs("escr", [128, 128], F32)
    expAq = [SBs(f"expAq{i}", [128, CPT, 128], BF16) for i in range(2)]
    gdif = SBs("gdif", [128, 1], F32)
    ngend = SBs("ngend", [128, 1], F32)
    colsA = SBs("colsA", [128, 128], F32)
    _colsBq = SBs("colsBq0", [128, CPT, 128], F32)
    colsBq = [_colsBq, _colsBq]
    nmcols = SBs("nmcols", [128, 128], F32)
    _t3b = SBs("t3b0", [128, 384], F32)
    t3b = [_t3b, _t3b]
    _dab3 = SBs("dab30", [128, 384], BF16)
    dab3 = [_dab3, _dab3]
    attnq = [[SBs(f"attnq{i}_{s}", [128, CPT, 128], BF16) for s in range(4)]
             for i in range(2)]
    _mnegq = [SBs(f"mnegq_{s}", [128, CPT, 128], BF16) for s in range(4)]
    mnegq = [_mnegq, _mnegq]
    _mrawq = [SBs(f"mrawq_{s}", [128, CPT, 128], BF16) for s in range(4)]
    mrawq = [_mrawq, _mrawq]
    _rhsq = [SBs(f"rhsq_{s}", [128, CPT, 128], BF16) for s in range(4)]
    rhsq = [_rhsq, _rhsq]
    xq = [SBs(f"xq{s}", [128, CPT, 128], BF16) for s in range(4)]
    pq = [[SBs(f"pq{s}_{i}", [128, CPT, 128], BF16) for i in range(2)]
          for s in range(4)]
    prawq = [SBs(f"prawq{s}", [128, CPT, 128], BF16) for s in range(4)]
    _uvw = [SBs(f"uvw_{s}", [128, CPT, 128], BF16) for s in range(4)]
    uvw = [_uvw, _uvw]
    wfm = [SBs(f"wfm{p}", [128, CPT, 128], BF16) for p in range(2)]
    ubuf = [SBs(f"ubuf{s}", [128, 64], BF16) for s in range(4)]
    utb = [SBs(f"utb{s}", [128, 64], BF16) for s in range(4)]
    qgb = [SBs(f"qgb{p}", [128, 128], BF16) for p in range(2)]
    gCcq = [SBs(f"gCcq{i}", [128, CPT], F32) for i in range(2)]
    gCcol = [SBs(f"gCcol{p}", [128, 1], F32) for p in range(2)]
    ps32 = [SBs(f"ps32{p}", [128, 64], F32) for p in range(2)]
    psb = [SBs(f"psb{p}", [128, 64], BF16) for p in range(2)]
    _oseg = SBs("osegs", [128, 2, TILE], BF16)
    oseg = [_oseg, _oseg]

    sems = {}
    for sname in ("dx", "dg", "cc", "t", "v", "s", "g"):
        sems[sname] = ctx.enter_context(nc.semaphore(sname))

    steps = {e: [] for e in ("sync", "tensor", "vector", "scalar", "gpsimd")}
    cnt = {k: 0 for k in sems}
    RES = {}
    HWM = {}

    def _res(obj):
        key = id(obj)
        if key not in RES:
            RES[key] = {"w": None, "r": {}}
        return RES[key]

    deferq = []
    defer_on = [False]

    def emit(q, fn, reads=(), writes=(), sem=None, inc=1, queue=None):
        if defer_on[0]:
            deferq.append((q, fn, reads, writes, sem, inc, queue))
            return
        if sem is None:
            sem = {"sync": "dx", "tensor": "t", "vector": "v",
                   "scalar": "s", "gpsimd": "g"}[q]
        queue = queue or (q if sem not in ("dx", "dg") else "q_" + sem)
        waits = {}

        def addw(ent):
            if ent is None:
                return
            eq, es, ec = ent
            if eq == queue:
                return
            if waits.get(es, 0) < ec:
                waits[es] = ec
        for b in reads:
            addw(_res(b)["w"])
        for b in writes:
            r = _res(b)
            addw(r["w"])
            for eq, (es, ec) in r["r"].items():
                addw((eq, es, ec))
        for s2, v2 in sorted(waits.items()):
            if HWM.get((q, s2), -1) >= v2:
                continue
            HWM[(q, s2)] = v2
            steps[q].append(lambda e, s=s2, v=v2: e.wait_ge(sems[s], v))
        cnt[sem] += inc
        me = (queue, sem, cnt[sem])
        steps[q].append(lambda e, f=fn, s=sem, k=inc: f(e).then_inc(sems[s], k))
        for b in reads:
            _res(b)["r"][queue] = (sem, cnt[sem])
        for b in writes:
            r = _res(b)
            r["w"] = me
            r["r"] = {}
        return cnt[sem]

    def drain(n=1 << 30):
        for _ in range(min(n, len(deferq))):
            q, fn, reads, writes, sem, inc, queue = deferq.pop(0)
            emit(q, fn, reads, writes, sem, inc, queue)

    def fence():
        vals = dict(cnt)
        for q2 in ("sync", "tensor", "vector", "scalar", "gpsimd"):
            for sname, v in vals.items():
                if v > 0 and HWM.get((q2, sname), -1) < v:
                    HWM[(q2, sname)] = v
                    steps[q2].append(
                        lambda e, s=sname, vv=v: e.wait_ge(sems[s], vv))
        RES.clear()

    xtv = xt.rearrange("(c p) t -> p c t", p=128)
    xslv = xsl.rearrange("(c p) t -> p c t", p=128)
    wp1v = wp1.rearrange("(c p) m -> p c m", p=128)
    wtmv = wtm.rearrange("(c p) m -> p c m", p=128)
    wffnv = wffn.rearrange("(c p) m -> p c m", p=128)
    woutv = wout.rearrange("(c p) m -> p c m", p=128)
    aginv = agin.ap().rearrange("(r p) t -> p r t", p=128)
    agoutv = agout.ap().rearrange("r (h p) t -> (r h) p t", p=128)
    dresv = dresid.ap().rearrange("(c p) t -> p c t", p=128)
    outpv = outp.rearrange("(c p) t -> p c t", p=128)

    DMA = "dma_marker"

    emit("sync", lambda e: e.dma_start(out=kc_["kmask2"][:, 0:128],
         in_=c_in["c_maskI"]), writes=[kc_["kmask2"]], inc=16)
    emit("sync", lambda e: e.dma_start(out=kc_["kmask2"][:, 128:256],
         in_=c_in["c_maskS"]), writes=[kc_["kmask2"]], inc=16)
    for nm, dst in (("c_maskSR", kc_["kmaskSR"]),
                    ("c_eps", kc_["keps"]),
                    ("c_sel0", ksel[0]), ("c_sel1", ksel[1]),
                    ("c_halfF", kc_["khalfF"]), ("c_halfB", kc_["khalfB"]),
                    ("c_idB", kc_["kidB"]), ("c_idF", kc_["kidF"]),
                    ("c_ss", kc_["kss"]), ("c_o1b", kc_["ko1b"]),
                    ("c_o1f", kc_["ko1f"])):
        emit("sync", lambda e, d=dst, s=c_in[nm]: e.dma_start(out=d[:], in_=s),
             writes=[dst], inc=16)
    for i in range(2):
        emit("vector", lambda e, i=i: e.tensor_copy(kselb[i][:], ksel[i][:]),
             reads=[ksel[i]], writes=[kselb[i]])
    emit("sync", lambda e: e.dma_start(out=mub[:], in_=muin), writes=[mub], inc=16)
    emit("sync", lambda e: e.dma_start(out=sridx[:], in_=ridx), writes=[sridx], inc=16)
    emit("vector", lambda e: e.memset(kgb[:], 0.0), writes=[kgb])
    emit("sync", lambda e: e.dma_start(out=kgb[0:98, :], in_=gbin),
         writes=[kgb], inc=16)
    for lo in range(0, 640, 128):
        emit("sync", lambda e, lo=lo: e.dma_start(
            out=wstg[:], in_=wp1v[:, :, lo:lo + 128]),
            writes=[wstg], inc=16)
        emit("vector", lambda e, lo=lo: e.tensor_copy(
            wp1b[:, :, lo:lo + 128], wstg[:]),
            reads=[wstg], writes=[wp1b])
    for lo in (0, 128, 256, 384):
        emit("sync", lambda e, lo=lo: e.dma_start(
            out=wstg[:], in_=wtmv[:, :, lo:lo + 128]), writes=[wstg], inc=16)
        emit("vector", lambda e, lo=lo: e.tensor_copy(
            wtmb[:, :, lo:lo + 128], wstg[:]), reads=[wstg], writes=[wtmb])
    emit("vector", lambda e: e.tensor_scalar(ngb[:], kgb[:], -1.0, None, ALU.mult),
         reads=[kgb], writes=[ngb])
    for p in range(2):
        emit("vector", lambda e, p=p: e.memset(ps32[p][:], 0.0), writes=[ps32[p]])
        emit("vector", lambda e, p=p: e.memset(psb[p][:], 0.0), writes=[psb[p]])
    emit("vector", lambda e: e.tensor_scalar(
        mu1b[:], mub[:], -1.0, 1.0, ALU.mult, ALU.add),
        reads=[mub], writes=[mu1b])

    ridx_reg = nc.gpsimd.alloc_register("ridxreg")
    rank_sv = [None]
    emit("gpsimd", lambda e: e.reg_load(ridx_reg, sridx[0:1, 0:1]),
         reads=[sridx], writes=[], sem="g")

    def XL(t):
        xb = xtile[t % 2]
        emit("sync", lambda e, t=t, xb=xb: e.dma_start(
            out=xb[:], in_=xtv[:, :, t * TILE:(t + 1) * TILE]),
            writes=[xb], inc=16)

    def P1(t):
        pt = t % 2
        xb = xtile[pt]
        for kc in range(8):
            emit("vector", lambda e, kc=kc: e.tensor_copy(
                ybuf[kc % 2][:], xb[:, kc]), reads=[xb], writes=[ybuf[kc % 2]])
            emit("scalar", lambda e, kc=kc: e.activation(
                y2buf[kc % 2][:], xb[:, kc], AF.Square),
                reads=[xb], writes=[y2buf[kc % 2]])
            emit("tensor", lambda e, kc=kc: e.matmul(
                pb[2][0:1, :], kc_["ko1b"][:], ybuf[kc % 2][:],
                start=(kc == 0), stop=(kc == 7)),
                reads=[ybuf[kc % 2], kc_["ko1b"]], writes=[pb[2]])
            emit("tensor", lambda e, kc=kc: e.matmul(
                pb[3][0:1, :], kc_["ko1b"][:], y2buf[kc % 2][:],
                start=(kc == 0), stop=(kc == 7)),
                reads=[y2buf[kc % 2], kc_["ko1b"]], writes=[pb[3]])
        emit("vector", lambda e: e.tensor_scalar(
            rowq[0][:], pb[2][0:1, :], 1.0 / D, None, ALU.mult),
            reads=[pb[2]], writes=[rowq[0]])
        emit("vector", lambda e: e.tensor_tensor(
            rowq[1][:], rowq[0][:], rowq[0][:], ALU.mult),
            reads=[rowq[0]], writes=[rowq[1]])
        emit("vector", lambda e: e.scalar_tensor_tensor(
            rowq[2][:], pb[3][0:1, :], 1.0 / D, rowq[1][:], ALU.mult,
            ALU.subtract), reads=[pb[3], rowq[1]], writes=[rowq[2]])
        emit("scalar", lambda e: e.activation(rowq[3][:], rowq[2][:], AF.Ln,
                                              bias=kc_["keps"][0:1, 0:1]),
             reads=[rowq[2], kc_["keps"]], writes=[rowq[3]])
        emit("scalar", lambda e: e.activation(rowq[4][:], rowq[3][:], AF.Exp,
                                              scale=-0.5),
             reads=[rowq[3]], writes=[rowq[4]])
        emit("vector", lambda e: e.tensor_tensor(
            mrrow[:, 0:TILE], rowq[0][:], rowq[4][:], ALU.mult),
            reads=[rowq[0], rowq[4]], writes=[mrrow])
        emit("tensor", lambda e: e.matmul(pb[2][:, :], kc_["ko1f"][:],
                                          rowq[4][:], start=True, stop=True),
             reads=[rowq[4], kc_["ko1f"]], writes=[pb[2]])
        emit("tensor", lambda e: e.matmul(pb[3][:, :], kc_["ko1f"][:],
                                          mrrow[:, 0:TILE], start=True, stop=True),
             reads=[mrrow, kc_["ko1f"]], writes=[pb[3]])
        emit("scalar", lambda e: e.copy(rrep[:], pb[2][:, :]),
             reads=[pb[2]], writes=[rrep])
        emit("scalar", lambda e: e.copy(mrep[:], pb[3][:, :]),
             reads=[pb[3]], writes=[mrep])
        for kc in range(8):
            emit("gpsimd", lambda e, kc=kc: e.tensor_tensor(
                t1b[kc % 2][:], xb[:, kc], rrep[:], ALU.mult),
                reads=[xb, rrep], writes=[t1b[kc % 2]])
            emit("vector", lambda e, kc=kc: e.scalar_tensor_tensor(
                xnb[:, kc], mrep[:], -1.0, t1b[kc % 2][:], ALU.mult, ALU.add),
                reads=[mrep, t1b[kc % 2]], writes=[xnb])
        for mi in range(5):
            bk = pb[mi % 2]
            for kc in range(8):
                emit("tensor", lambda e, kc=kc, mi=mi, bk=bk: e.matmul(
                    bk[:, :], wp1b[:, kc, mi * 128:(mi + 1) * 128], xnb[:, kc],
                    start=(kc == 0), stop=(kc == 7)),
                    reads=[wp1b, xnb], writes=[bk])
            if mi < 4:
                qkb = [qk01, qk23][mi % 2][pt]
                csl = slice(0, 128) if mi < 2 else slice(128, 256)
                emit("scalar", lambda e, qkb=qkb, csl=csl, bk=bk: e.copy(
                    qkb[:, :, csl], bk[:, :].rearrange("p (c x) -> p c x", x=128)),
                    reads=[bk], writes=[qkb])
            else:
                emit("scalar", lambda e, bk=bk: e.activation(
                    ebuf[:], bk[:, :], AF.Exp, bias=ngb[:], scale=-1.0),
                    reads=[bk, ngb], writes=[ebuf])
                emit("vector", lambda e: e.tensor_scalar(
                    tbb[:], ebuf[:], 1.0, None, ALU.add),
                    reads=[ebuf], writes=[tbb])
                emit("vector", lambda e: e.reciprocal(abb[:], tbb[:]),
                     reads=[tbb], writes=[abb])
                emit("scalar", lambda e: e.activation(lntb[:], tbb[:], AF.Ln),
                     reads=[tbb], writes=[lntb])
                emit("vector", lambda e, pt=pt: e.tensor_tensor_scan(
                    gneg[pt][:], lntb[:], lntb[:], 0.0, ALU.add, ALU.bypass),
                    reads=[lntb], writes=[gneg[pt]])
                emit("vector", lambda e: e.tensor_copy(t1r[0:2, :], lntb[64:66, :]),
                     reads=[lntb], writes=[t1r])
                emit("vector", lambda e: e.tensor_copy(t1r[32:34, :], lntb[96:98, :]),
                     reads=[lntb], writes=[t1r])
        for tg in range(4):
            bk = pb[2 + tg % 2]
            for kc in range(8):
                emit("tensor", lambda e, kc=kc, tg=tg, bk=bk: e.matmul(
                    bk[:, :], xnb[:, kc, tg * 128:(tg + 1) * 128],
                    wtmb[:, kc, :], start=(kc == 0), stop=(kc == 7)),
                    reads=[xnb, wtmb], writes=[bk])
            emit("scalar", lambda e, tg=tg, pt=pt, bk=bk: e.copy(
                ktmv[pt][:, tg, :], bk[:, :]), reads=[bk], writes=[ktmv[pt]])
        for part in range(4):
            qkb = [qk01, qk23][part % 2][pt]
            csl = slice(0, 128) if part < 2 else slice(128, 256)
            emit("scalar", lambda e, qkb=qkb, csl=csl: e.activation(
                sqb[:].rearrange("p (c x) -> p c x", x=128), qkb[:, :, csl],
                AF.Square), reads=[qkb], writes=[sqb])
            emit("tensor", lambda e, part=part: e.matmul(
                pb[0][:, :], kc_["kss"][:, part, :], sqb[:],
                start=(part == 0), stop=(part == 3)),
                reads=[sqb, kc_["kss"]], writes=[pb[0]])
        emit("vector", lambda e: e.tensor_scalar(
            sscb[:], pb[0][:, :], 1e-24, None, ALU.max),
            reads=[pb[0]], writes=[sscb])
        emit("scalar", lambda e: e.activation(lnssb[:], sscb[:], AF.Ln),
             reads=[sscb], writes=[lnssb])
        emit("vector", lambda e: e.tensor_scalar(
            lnssb[:], lnssb[:], 0.5, None, ALU.mult),
            reads=[lnssb], writes=[lnssb])
        emit("vector", lambda e: e.tensor_copy(t2r[0:2, :], lnssb[64:66, :]),
             reads=[lnssb], writes=[t2r])
        emit("vector", lambda e: e.tensor_copy(t2r[32:34, :], lnssb[96:98, :]),
             reads=[lnssb], writes=[t2r])
        emit("gpsimd", lambda e: e.tensor_tensor(
            t3r[:], t2r[:], t1r[:], ALU.add),
            reads=[t2r, t1r], writes=[t3r])
        for lc4 in range(CPT):
            cs4 = slice(lc4 * 128, (lc4 + 1) * 128)
            emit("gpsimd", lambda e, pt=pt, lc4=lc4, cs4=cs4: e.tensor_tensor(
                r3A(pt, lc4), lnssb[:, cs4], gneg[pt][:, cs4], ALU.add),
                reads=[lnssb, gneg[pt]], writes=[rows3[pt]])
            emit("gpsimd", lambda e, pt=pt, lc4=lc4, cs4=cs4: e.tensor_tensor(
                r3M(pt, lc4), t3r[:, cs4], gneg[pt][:, cs4], ALU.add),
                reads=[t3r, gneg[pt]], writes=[rows3[pt]])
            emit("gpsimd", lambda e, pt=pt, lc4=lc4, cs4=cs4: e.tensor_tensor(
                r3C(pt, lc4), t2r[:, cs4], gneg[pt][:, cs4], ALU.subtract),
                reads=[t2r, gneg[pt]], writes=[rows3[pt]])
        for lc4 in range(CPT):
            c34 = slice(lc4 * 384, (lc4 + 1) * 384)
            emit("scalar", lambda e, pt=pt, c34=c34: e.copy(
                rows3b[pt][:, 0, c34], rows3[pt][:, c34]),
                reads=[rows3[pt]], writes=[rows3b[pt]])
            emit("scalar", lambda e, pt=pt, c34=c34: e.copy(
                lnssb[:, 0:384], rows3b[pt][:, 0, c34]),
                reads=[rows3b[pt]], writes=[lnssb])
            emit("vector", lambda e, pt=pt, c34=c34: e.tensor_tensor(
                t1r[:, 0:384], rows3[pt][:, c34], lnssb[:, 0:384],
                ALU.subtract), reads=[rows3[pt], lnssb], writes=[t1r])
            emit("scalar", lambda e, pt=pt, c34=c34: e.copy(
                rows3b[pt][:, 1, c34], t1r[:, 0:384]),
                reads=[t1r], writes=[rows3b[pt]])
            emit("scalar", lambda e, pt=pt, c34=c34: e.copy(
                lnssb[:, 0:384], rows3b[pt][:, 1, c34]),
                reads=[rows3b[pt]], writes=[lnssb])
            emit("vector", lambda e: e.tensor_tensor(
                t1r[:, 0:384], t1r[:, 0:384], lnssb[:, 0:384], ALU.subtract),
                reads=[t1r, lnssb], writes=[t1r])
            emit("scalar", lambda e, pt=pt, c34=c34: e.copy(
                rows3b[pt][:, 2, c34], t1r[:, 0:384]),
                reads=[t1r], writes=[rows3b[pt]])
        emit("gpsimd", lambda e, pt=pt: e.tensor_tensor(
            colsrc[pt][:], gneg[pt][:], t2r[:], ALU.subtract),
            reads=[t2r, gneg[pt]], writes=[colsrc[pt]])
        emit("vector", lambda e, pt=pt: e.tensor_copy(
            colsrc[pt][64:66, :], abb[64:66, :]),
            reads=[abb], writes=[colsrc[pt]])
        emit("vector", lambda e, pt=pt: e.tensor_copy(
            colsrc[pt][96:98, :], abb[96:98, :]),
            reads=[abb], writes=[colsrc[pt]])

    def P23(t, nxt=None):
        if nxt is not None:
            XL(nxt)
        pt = t % 2
        for lc in range(CPT):
            csl = slice(lc * 128, (lc + 1) * 128)
            if lc == 0:
                emit("vector", lambda e, pt=pt, lc=lc: e.tensor_scalar(
                    gdif[:], gneg[pt][:, (lc + 1) * 128 - 1:(lc + 1) * 128],
                    -1.0, None, ALU.mult), reads=[gneg[pt]], writes=[gdif])
            else:
                emit("vector", lambda e, pt=pt, lc=lc: e.tensor_tensor(
                    gdif[:], gneg[pt][:, lc * 128 - 1:lc * 128],
                    gneg[pt][:, (lc + 1) * 128 - 1:(lc + 1) * 128],
                    ALU.subtract), reads=[gneg[pt]], writes=[gdif])
            emit("vector", lambda e, pt=pt, lc=lc: e.tensor_scalar(
                ngend[:], gneg[pt][:, (lc + 1) * 128 - 1:(lc + 1) * 128],
                -1.0, None, ALU.mult), reads=[gneg[pt]], writes=[ngend])
            if lc == 0:
                emit("scalar", lambda e, pt=pt, lc=lc: e.activation(
                    escr[0:64, :], r3M(pt, lc)[0:64, :], AF.Exp,
                    scale=-1.0), reads=[rows3[pt]], writes=[escr])
                emit("scalar", lambda e, pt=pt, lc=lc: e.activation(
                    expAq[pt][0:64, lc, :], r3A(pt, lc)[0:64, :], AF.Exp,
                    scale=-1.0), reads=[rows3[pt]], writes=[expAq[pt]])
            else:
                emit("scalar", lambda e, pt=pt, lc=lc: e.activation(
                    escr[0:64, :], r3M(pt, lc)[0:64, :], AF.Exp, scale=-1.0,
                    bias=gneg[pt][0:64, lc * 128 - 1:lc * 128]),
                    reads=[rows3[pt], gneg[pt]], writes=[escr])
                emit("scalar", lambda e, pt=pt, lc=lc: e.activation(
                    expAq[pt][0:64, lc, :], r3A(pt, lc)[0:64, :], AF.Exp,
                    scale=-1.0,
                    bias=gneg[pt][0:64, lc * 128 - 1:lc * 128]),
                    reads=[rows3[pt], gneg[pt]], writes=[expAq[pt]])
            emit("scalar", lambda e, pt=pt, csl=csl: e.activation(
                escr[64:128, :], colsrc[pt][0:64, csl],
                AF.Exp, bias=ngend[0:64, :]),
                reads=[colsrc[pt], ngend], writes=[escr])
            emit("scalar", lambda e, pt=pt, lc=lc: e.activation(
                gCcq[pt][0:64, lc:lc + 1], gdif[0:64, :], AF.Exp),
                reads=[gdif], writes=[gCcq[pt]])
            emit("tensor", lambda e, pt=pt, csl=csl: e.transpose(
                pb[4][:, 384:512], colsrc[pt][:, csl], kc_["kidF"][:]),
                reads=[colsrc[pt], kc_["kidF"]], writes=[pb[4]])
            emit("vector", lambda e: e.tensor_copy(colsA[:], pb[4][:, 384:512]),
                 reads=[pb[4]], writes=[colsA])
            emit("tensor", lambda e: e.transpose(
                pb[4][:, 384:512], escr[:], kc_["kidF"][:]),
                reads=[escr, kc_["kidF"]], writes=[pb[4]])
            emit("vector", lambda e, lc=lc: e.tensor_copy(
                colsBq[pt][:, lc, :], pb[4][:, 384:512]),
                reads=[pb[4]], writes=[colsBq[pt]])
            emit("tensor", lambda e, pt=pt, lc=lc: e.transpose(
                pb[5][:, 384:512], r3M(pt, lc), kc_["kidF"][:]),
                reads=[rows3[pt], kc_["kidF"]], writes=[pb[5]])
            emit("vector", lambda e: e.tensor_scalar(
                nmcols[:], pb[5][:, 384:512], -1.0, None, ALU.mult),
                reads=[pb[5]], writes=[nmcols])

            for s in range(4):
                p, hf = s // 2, s % 2
                bq = 32 * p
                Bh = 64 * hf
                qkb = [qk01, qk23][p][pt]
                bk = pb[2 + s % 2]
                bk2 = pb[6 + s % 2]
                emit("tensor", lambda e, qkb=qkb, lc=lc, Bh=Bh, bk=bk: e.matmul(
                    bk[:, 0:256], qkb[Bh:Bh + 64, lc, 128:256],
                    qkb[Bh:Bh + 64, lc, :], start=True, stop=True),
                    reads=[qkb], writes=[bk])
                for j3 in range(3):
                    emit("tensor", lambda e, pt=pt, bq=bq, hf=hf, bk2=bk2,
                         lc=lc, j3=j3: e.matmul(
                             bk2[:, 0:384], kselb[hf][bq:bq + 2, :],
                             rows3b[pt][bq:bq + 2, j3,
                                        lc * 384:(lc + 1) * 384],
                             start=(j3 == 0), stop=(j3 == 2)),
                         reads=[kselb[hf], rows3b[pt]], writes=[bk2])
                emit("scalar", lambda e, s=s, bq=bq, hf=hf: e.activation(
                    t3b[s % 2][:, 0:256], kc_["kmask2"][:], AF.Identity,
                    bias=colsA[:, bq + hf:bq + hf + 1]),
                    reads=[kc_["kmask2"], colsA], writes=[t3b[s % 2]])
                emit("scalar", lambda e, s=s, bq=bq, hf=hf: e.activation(
                    t3b[s % 2][:, 256:384], kc_["kmaskSR"][:], AF.Identity,
                    bias=nmcols[:, bq + hf:bq + hf + 1]),
                    reads=[kc_["kmaskSR"], nmcols], writes=[t3b[s % 2]])
                emit("vector", lambda e, s=s, bk2=bk2: e.scalar_tensor_tensor(
                    t3b[s % 2][:], bk2[:, 0:384], -1.0, t3b[s % 2][:],
                    ALU.mult, ALU.add), reads=[bk2, t3b[s % 2]],
                    writes=[t3b[s % 2]])
                emit("scalar", lambda e, s=s: e.activation(
                    dab3[s % 2][:], t3b[s % 2][:], AF.Exp),
                    reads=[t3b[s % 2]], writes=[dab3[s % 2]])
                emit("vector", lambda e, s=s, lc=lc, bk=bk: e.scalar_tensor_tensor(
                    attnq[pt][s][:, lc, :], bk[:, 0:128], 1.0,
                    dab3[s % 2][:, 0:128], ALU.mult, ALU.mult),
                    reads=[bk, dab3[s % 2]], writes=[attnq[pt][s]])
                emit("vector", lambda e, s=s, lc=lc, bk=bk: e.scalar_tensor_tensor(
                    mnegq[pt][s][:, lc, :], bk[:, 128:256], -1.0,
                    dab3[s % 2][:, 128:256], ALU.mult, ALU.mult),
                    reads=[bk, dab3[s % 2]], writes=[mnegq[pt][s]])
                emit("vector", lambda e, s=s, lc=lc, bk=bk: e.scalar_tensor_tensor(
                    mrawq[pt][s][:, lc, :], bk[:, 128:256], -1.0,
                    dab3[s % 2][:, 256:384], ALU.mult, ALU.mult),
                    reads=[bk, dab3[s % 2]], writes=[mrawq[pt][s]])
                emit("vector", lambda e, s=s, pt=pt, lc=lc, bq=bq, hf=hf:
                     e.tensor_scalar(
                         rhsq[pt][s][:, lc, 0:64],
                         ktmv[pt][:, lc, 256 + s * 64:256 + s * 64 + 64],
                         colsA[:, 64 + bq + hf:64 + bq + hf + 1], None,
                         ALU.mult),
                     reads=[ktmv[pt], colsA], writes=[rhsq[pt][s]])
                emit("vector", lambda e, s=s, pt=pt, lc=lc, bq=bq, hf=hf:
                     e.tensor_scalar(
                         rhsq[pt][s][:, lc, 64:128],
                         ktmv[pt][:, lc, s * 64:s * 64 + 64],
                         colsBq[pt][:, lc, bq + hf:bq + hf + 1], -1.0,
                         ALU.mult, ALU.mult),
                     reads=[ktmv[pt], colsBq[pt]], writes=[rhsq[pt][s]])

        if nxt is not None:
            defer_on[0] = True
            P1(nxt)
            defer_on[0] = False

        for lev in range(7):
            for s in range(4):
                bk = pb[4 + s]
                Xs = rhsq[pt][s] if lev == 0 else xq[s]
                Ps = mnegq[pt][s] if lev == 0 else pq[s][(lev - 1) % 2]
                for l4 in range(CPT):
                    emit("tensor", lambda e, bk=bk, Ps=Ps, Xs=Xs, l4=l4:
                         e.matmul(bk[:, l4 * 128:(l4 + 1) * 128],
                                  Ps[:, l4, :], Xs[:, l4, :],
                                  start=True, stop=True),
                         reads=[Ps, Xs], writes=[bk])
            for s in range(4):
                bk = pb[4 + s]
                Xs = rhsq[pt][s] if lev == 0 else xq[s]
                dest = uvw[pt][s] if lev == 6 else xq[s]
                emit("vector", lambda e, bk=bk, Xs=Xs, d=dest:
                     e.scalar_tensor_tensor(
                         d[:].rearrange("p c x -> p (c x)"), bk[:, :], 1.0,
                         Xs[:].rearrange("p c x -> p (c x)"), ALU.mult,
                         ALU.add),
                     reads=[bk, Xs], writes=[dest])
            drain(5)
            if lev < 6:
                for s in range(4):
                    bk = pb[4 + s]
                    Ps = mnegq[pt][s] if lev == 0 else pq[s][(lev - 1) % 2]
                    Rs = mrawq[pt][s] if lev == 0 else prawq[s]
                    for l4 in range(CPT):
                        emit("tensor", lambda e, bk=bk, Ps=Ps, Rs=Rs, l4=l4:
                             e.matmul(bk[:, l4 * 128:(l4 + 1) * 128],
                                      Rs[:, l4, :], Ps[:, l4, :],
                                      start=True, stop=True),
                             reads=[Ps, Rs], writes=[bk])
                drain(5)
                for s in range(4):
                    bk = pb[4 + s]
                    emit("scalar", lambda e, bk=bk, s=s, lev=lev: e.copy(
                        pq[s][lev % 2][:].rearrange("p c x -> p (c x)"),
                        bk[:, :]), reads=[bk], writes=[pq[s][lev % 2]])
                for s in range(4):
                    bk = pb[4 + s]
                    Ps = mnegq[pt][s] if lev == 0 else pq[s][(lev - 1) % 2]
                    Rs = mrawq[pt][s] if lev == 0 else prawq[s]
                    for l4 in range(CPT):
                        emit("tensor", lambda e, bk=bk, Ps=Ps, Rs=Rs, l4=l4:
                             e.matmul(bk[:, l4 * 128:(l4 + 1) * 128],
                                      Ps[:, l4, :], Rs[:, l4, :],
                                      start=True, stop=True),
                             reads=[Ps, Rs], writes=[bk])
                for s in range(4):
                    bk = pb[4 + s]
                    emit("vector", lambda e, bk=bk, s=s: e.tensor_copy(
                        prawq[s][:].rearrange("p c x -> p (c x)"),
                        bk[:, :]), reads=[bk], writes=[prawq[s]])
            drain(8)

        for lc in range(CPT):
            for p in range(2):
                bk = pb[4 + 2 * p]
                for hf in range(2):
                    s = 2 * p + hf
                    emit("tensor", lambda e, s=s, hf=hf, lc=lc, bk=bk:
                         e.transpose(bk[64 * hf:64 * hf + 64, 320:384]
                                     .bitcast(BF16), uvw[pt][s][:, lc, 64:128],
                                     kc_["kidB"][:]),
                         reads=[uvw[pt][s], kc_["kidB"]], writes=[bk])
                emit("scalar", lambda e, p=p, lc=lc, bk=bk: e.copy(
                    wfm[p][:, lc, :], bk[:, 320:384].bitcast(BF16)),
                    reads=[bk], writes=[wfm[p]])
            for p in range(2):
                bq = 32 * p
                bk = pb[4 + 2 * p]
                qkb = [qk01, qk23][p][pt]
                emit("tensor", lambda e, bq=bq, bk=bk, lc=lc: e.matmul(
                    bk[:, 448:449], kc_["khalfF"][bq:bq + 2, :],
                    gCcq[pt][bq:bq + 2, lc:lc + 1], start=True, stop=True),
                    reads=[kc_["khalfF"], gCcq[pt]], writes=[bk])
                emit("vector", lambda e, p=p, bk=bk: e.tensor_copy(
                    gCcol[p][:], bk[:, 448:449]), reads=[bk], writes=[gCcol[p]])
                for hf in range(2):
                    s = 2 * p + hf
                    Bh = 64 * hf
                    emit("tensor", lambda e, s=s, lc=lc, bk=bk: e.matmul(
                        bk[:, 0:64], kc_["kidB"][:], uvw[pt][s][:, lc, 0:64],
                        start=True, stop=False),
                        reads=[kc_["kidB"], uvw[pt][s]], writes=[bk])
                    emit("tensor", lambda e, s=s, p=p, lc=lc, Bh=Bh,
                         bk=bk: e.matmul(
                             bk[:, 0:64], wfm[p][Bh:Bh + 64, lc, :],
                             psb[p][Bh:Bh + 64, :], start=False, stop=True),
                         reads=[wfm[p], psb[p]], writes=[bk])
                    emit("scalar", lambda e, s=s, bk=bk: e.copy(
                        ubuf[s][:], bk[:, 0:64]), reads=[bk], writes=[ubuf[s]])
                    emit("vector", lambda e, s=s, bq=bq, hf=hf, lc=lc:
                         e.tensor_scalar(
                             utb[s][:], ubuf[s][:],
                             colsBq[pt][:, lc, 64 + bq + hf:64 + bq + hf + 1],
                             None, ALU.mult),
                         reads=[ubuf[s], colsBq[pt]], writes=[utb[s]])
                emit("tensor", lambda e, bq=bq, bk=bk, lc=lc: e.matmul(
                    bk[:, 64:192], kc_["khalfB"][bq:bq + 2, :],
                    expAq[pt][bq:bq + 2, lc, :], start=True, stop=True),
                    reads=[kc_["khalfB"], expAq[pt]], writes=[bk])
                emit("vector", lambda e, p=p, lc=lc, qkb=qkb, bk=bk:
                     e.tensor_tensor(qgb[p][:], qkb[:, lc, 0:128],
                                     bk[:, 64:192], ALU.mult),
                     reads=[qkb, bk], writes=[qgb[p]])
                for hf in range(2):
                    s = 2 * p + hf
                    Bh = 64 * hf
                    emit("tensor", lambda e, s=s, hf=hf, bk=bk, lc=lc,
                         fs=(hf == 0): e.matmul(
                             bk[64 * hf:64 * hf + 64, 192:320],
                             ubuf[s][:], attnq[pt][s][:, lc, :], start=fs,
                             stop=False),
                         reads=[ubuf[s], attnq[pt][s]], writes=[bk])
                    emit("tensor", lambda e, s=s, p=p, hf=hf, Bh=Bh, bk=bk,
                         lt=(hf == 1): e.matmul(
                             bk[64 * hf:64 * hf + 64, 192:320],
                             psb[p][Bh:Bh + 64, :], qgb[p][Bh:Bh + 64, :],
                             start=False, stop=lt),
                         reads=[psb[p], qgb[p]], writes=[bk])
                emit("vector", lambda e, p=p, pt=pt, lc=lc, bk=bk: e.tensor_copy(
                    oseg[pt][:, p, lc * 128:(lc + 1) * 128], bk[:, 192:320]),
                    reads=[bk], writes=[oseg[pt]])
                for hf in range(2):
                    s = 2 * p + hf
                    emit("tensor", lambda e, s=s, hf=hf, pt=pt, lc=lc, bk=bk:
                         e.matmul(bk[64 * hf:64 * hf + 64, 320:384],
                                  ktmv[pt][:, lc, s * 64:s * 64 + 64],
                                  utb[s][:], start=(hf == 0), stop=(hf == 1)),
                         reads=[ktmv[pt], utb[s]], writes=[bk])
                emit("vector", lambda e, p=p, bk=bk: e.scalar_tensor_tensor(
                    ps32[p][:], ps32[p][:], gCcol[p][:], bk[:, 320:384],
                    ALU.mult, ALU.add),
                    reads=[ps32[p], gCcol[p], bk], writes=[ps32[p]])
                emit("scalar", lambda e, p=p: e.copy(psb[p][:], ps32[p][:]),
                     reads=[ps32[p]], writes=[psb[p]])
            drain(8)
        drain()
        emit("gpsimd", lambda e, t=t, pt=pt: e.dma_start(
            out=aginv[:, :, t * TILE:(t + 1) * TILE], in_=oseg[pt][:]),
            reads=[oseg[pt]], writes=[agin], sem="dg", inc=16, queue="gpsq")

    def FFN(fp):
        base = fp * 512
        emit("sync", lambda e, base=base: e.dma_start(
            out=fxb[:], in_=xslv[:, :, base:base + 513]), writes=[fxb], inc=16)
        for kc in range(8):
            emit("vector", lambda e, kc=kc: e.tensor_copy(
                ybuf[kc % 2][:], fxb[:, kc, 0:512]),
                reads=[fxb], writes=[ybuf[kc % 2]])
            emit("scalar", lambda e, kc=kc: e.activation(
                y2buf[kc % 2][:], fxb[:, kc, 0:512], AF.Square),
                reads=[fxb], writes=[y2buf[kc % 2]])
            emit("tensor", lambda e, kc=kc: e.matmul(
                pb[2][0:1, :], kc_["ko1b"][:], ybuf[kc % 2][:],
                start=(kc == 0), stop=(kc == 7)),
                reads=[ybuf[kc % 2], kc_["ko1b"]], writes=[pb[2]])
            emit("tensor", lambda e, kc=kc: e.matmul(
                pb[3][0:1, :], kc_["ko1b"][:], y2buf[kc % 2][:],
                start=(kc == 0), stop=(kc == 7)),
                reads=[y2buf[kc % 2], kc_["ko1b"]], writes=[pb[3]])
        for kc in range(8):
            emit("vector", lambda e, kc=kc: e.tensor_copy(
                ybuf[kc % 2][:, 0:1], fxb[:, kc, 512:513]),
                reads=[fxb], writes=[ybuf[kc % 2]])
            emit("scalar", lambda e, kc=kc: e.activation(
                y2buf[kc % 2][:, 0:1], fxb[:, kc, 512:513], AF.Square),
                reads=[fxb], writes=[y2buf[kc % 2]])
            emit("tensor", lambda e, kc=kc: e.matmul(
                pb[2][32:33, 0:1], kc_["ko1b"][:], ybuf[kc % 2][:, 0:1],
                start=(kc == 0), stop=(kc == 7)),
                reads=[ybuf[kc % 2], kc_["ko1b"]], writes=[pb[2]])
            emit("tensor", lambda e, kc=kc: e.matmul(
                pb[3][32:33, 0:1], kc_["ko1b"][:], y2buf[kc % 2][:, 0:1],
                start=(kc == 0), stop=(kc == 7)),
                reads=[y2buf[kc % 2], kc_["ko1b"]], writes=[pb[3]])
        emit("vector", lambda e: e.tensor_scalar(
            rowq[0][:, 0:512], pb[2][0:1, :], 1.0 / D, None, ALU.mult),
            reads=[pb[2]], writes=[rowq[0]])
        emit("vector", lambda e: e.tensor_scalar(
            mrrow[:, 0:1], pb[2][32:33, 0:1], 1.0 / D, None, ALU.mult),
            reads=[pb[2]], writes=[mrrow])
        emit("vector", lambda e: e.tensor_tensor(
            rowq[1][:, 0:512], rowq[0][:, 0:512], rowq[0][:, 0:512], ALU.mult),
            reads=[rowq[0]], writes=[rowq[1]])
        emit("vector", lambda e: e.scalar_tensor_tensor(
            rowq[2][:, 0:512], pb[3][0:1, :], 1.0 / D, rowq[1][:, 0:512],
            ALU.mult, ALU.subtract), reads=[pb[3], rowq[1]], writes=[rowq[2]])
        emit("scalar", lambda e: e.activation(
            rowq[3][:, 0:512], rowq[2][:, 0:512], AF.Ln,
            bias=kc_["keps"][0:1, 0:1]),
            reads=[rowq[2], kc_["keps"]], writes=[rowq[3]])
        emit("scalar", lambda e: e.activation(
            rowq[4][:, 0:512], rowq[3][:, 0:512], AF.Exp, scale=-0.5),
            reads=[rowq[3]], writes=[rowq[4]])
        emit("vector", lambda e: e.tensor_tensor(
            mrrow[:, 1:513], rowq[0][:, 0:512], rowq[4][:, 0:512], ALU.mult),
            reads=[rowq[0], rowq[4]], writes=[mrrow])
        emit("vector", lambda e: e.tensor_tensor(
            hcol[0][0:1, :], mrrow[0:1, 0:1], mrrow[0:1, 0:1], ALU.mult),
            reads=[mrrow], writes=[hcol[0]])
        emit("vector", lambda e: e.scalar_tensor_tensor(
            hcol[0][0:1, :], pb[3][32:33, 0:1], 1.0 / D, hcol[0][0:1, :],
            ALU.mult, ALU.subtract), reads=[pb[3], hcol[0]], writes=[hcol[0]])
        emit("scalar", lambda e: e.activation(
            hcol[0][0:1, :], hcol[0][0:1, :], AF.Ln,
            bias=kc_["keps"][0:1, 0:1]),
            reads=[hcol[0], kc_["keps"]], writes=[hcol[0]])
        emit("scalar", lambda e: e.activation(
            hcol[0][0:1, :], hcol[0][0:1, :], AF.Exp, scale=-0.5),
            reads=[hcol[0]], writes=[hcol[0]])
        emit("vector", lambda e: e.tensor_tensor(
            hcol[1][0:1, :], mrrow[0:1, 0:1], hcol[0][0:1, :], ALU.mult),
            reads=[mrrow, hcol[0]], writes=[hcol[1]])
        emit("tensor", lambda e: e.matmul(
            pb[2][:, :], kc_["ko1f"][:], rowq[4][:, 0:512], start=True,
            stop=True), reads=[kc_["ko1f"], rowq[4]], writes=[pb[2]])
        emit("tensor", lambda e: e.matmul(
            pb[3][:, :], kc_["ko1f"][:], mrrow[:, 1:513], start=True,
            stop=True), reads=[kc_["ko1f"], mrrow], writes=[pb[3]])
        emit("scalar", lambda e: e.copy(rrep[:], pb[2][:, :]),
             reads=[pb[2]], writes=[rrep])
        emit("scalar", lambda e: e.copy(mrep[:], pb[3][:, :]),
             reads=[pb[3]], writes=[mrep])
        emit("tensor", lambda e: e.matmul(
            pb[2][:, 0:1], kc_["ko1f"][:], hcol[0][0:1, :], start=True,
            stop=True), reads=[kc_["ko1f"], hcol[0]], writes=[pb[2]])
        emit("tensor", lambda e: e.matmul(
            pb[3][:, 0:1], kc_["ko1f"][:], hcol[1][0:1, :], start=True,
            stop=True), reads=[kc_["ko1f"], hcol[1]], writes=[pb[3]])
        emit("scalar", lambda e: e.copy(hcol[0][:], pb[2][:, 0:1]),
             reads=[pb[2]], writes=[hcol[0]])
        emit("scalar", lambda e: e.copy(hcol[1][:], pb[3][:, 0:1]),
             reads=[pb[3]], writes=[hcol[1]])
        for kc in range(8):
            emit("gpsimd", lambda e, kc=kc: e.tensor_tensor(
                t1b[kc % 2][:], fxb[:, kc, 0:512], rrep[:], ALU.mult),
                reads=[fxb, rrep], writes=[t1b[kc % 2]])
            emit("vector", lambda e, kc=kc: e.scalar_tensor_tensor(
                fxn[:, kc, 0:512], mrep[:], -1.0, t1b[kc % 2][:], ALU.mult,
                ALU.add), reads=[mrep, t1b[kc % 2]], writes=[fxn])
        for kc in range(8):
            emit("gpsimd", lambda e, kc=kc: e.tensor_tensor(
                t1b[kc % 2][:, 0:1], fxb[:, kc, 512:513], hcol[0][:], ALU.mult),
                reads=[fxb, hcol[0]], writes=[t1b[kc % 2]])
            emit("vector", lambda e, kc=kc: e.scalar_tensor_tensor(
                fxn[:, kc, 512:513], hcol[1][:], -1.0, t1b[kc % 2][:, 0:1],
                ALU.mult, ALU.add), reads=[hcol[1], t1b[kc % 2]], writes=[fxn])
        for stage in range(3):
            for q4 in range(8):
                emit("sync", lambda e, stage=stage, q4=q4: e.dma_start(
                    out=wstg[:], in_=wffnv[:, :, stage * 1024 + q4 * 128:
                                           stage * 1024 + (q4 + 1) * 128]),
                    writes=[wstg], inc=16)
                emit("vector", lambda e, q4=q4: e.tensor_copy(
                    wbf[:, :, q4 * 128:(q4 + 1) * 128], wstg[:]),
                    reads=[wstg], writes=[wbf])
            if stage < 2:
                for kc in range(8):
                    emit("vector", lambda e, kc=kc, stage=stage: e.tensor_scalar(
                        fmt[:], fxn[:, kc, 1:513], mub[:, kc, stage:stage + 1],
                        None, ALU.mult), reads=[fxn, mub], writes=[fmt])
                    emit("vector", lambda e, kc=kc, stage=stage:
                         e.scalar_tensor_tensor(
                             fmx[:, kc], fxn[:, kc, 0:512],
                             mu1b[:, kc, stage:stage + 1], fmt[:],
                             ALU.mult, ALU.add),
                         reads=[fxn, mu1b, fmt], writes=[fmx])
                rhs = fmx
            else:
                rhs = kab
            for mi in range(8):
                bk = pb[mi % 2]
                for kc in range(8):
                    emit("tensor", lambda e, kc=kc, mi=mi, bk=bk, rhs=rhs:
                         e.matmul(bk[:, :], wbf[:, kc, mi * 128:(mi + 1) * 128],
                                  rhs[:, kc], start=(kc == 0), stop=(kc == 7)),
                         reads=[wbf, rhs], writes=[bk])
                if stage == 0:
                    emit("scalar", lambda e, bk=bk: e.activation(
                        ebuf[:], bk[:, :], AF.Exp, scale=-1.0),
                        reads=[bk], writes=[ebuf])
                    emit("vector", lambda e: e.tensor_scalar(
                        tbb[:], ebuf[:], 1.0, None, ALU.add),
                        reads=[ebuf], writes=[tbb])
                    emit("vector", lambda e: e.reciprocal(
                        ftmp[:], tbb[:]), reads=[tbb], writes=[ftmp])
                    emit("vector", lambda e, mi=mi: e.tensor_copy(
                        rfb[:, mi], ftmp[:]), reads=[ftmp], writes=[rfb])
                elif stage == 1:
                    emit("vector", lambda e, bk=bk: e.tensor_scalar(
                        ftmp[:], bk[:, :], 0.0, None, ALU.max),
                        reads=[bk], writes=[ftmp])
                    emit("scalar", lambda e, mi=mi: e.activation(
                        kab[:, mi], ftmp[:], AF.Square),
                        reads=[ftmp], writes=[kab])
                else:
                    emit("vector", lambda e, mi=mi, bk=bk: e.scalar_tensor_tensor(
                        ftmp[:], bk[:, :], 1.0, rfb[:, mi], ALU.mult, ALU.mult),
                        reads=[bk, rfb], writes=[ftmp])
                    emit("vector", lambda e, mi=mi: e.tensor_tensor(
                        fresp[mi % 2][:], ftmp[:], fxb[:, mi, 1:513], ALU.add),
                        reads=[ftmp, fxb], writes=[fresp[mi % 2]])
                    emit("gpsimd", lambda e, base=base, mi=mi: e.dma_start(
                        out=dresv[:, mi, base:base + 512], in_=fresp[mi % 2][:]),
                        reads=[fresp[mi % 2]], writes=[dresid], sem="dg",
                        inc=16, queue="gpsq")

    XL(0)
    P1(0)
    for t in range(NT):
        P23(t, nxt=t + 1 if t + 1 < NT else None)

    emit("gpsimd", lambda e: e.collective_compute(
        "AllGather", ALU.bypass, replica_groups=[[0, 1, 2, 3], [4, 5, 6, 7]],
        ins=[agin.ap().opt()], outs=[agout.ap().opt()]),
        reads=[agin], writes=[agout], sem="cc", inc=1, queue="gpsq")

    fence()
    sctx.close()
    wbf = SB("wbf", [128, 8, 1024], BF16)
    wotb = SB("wotb", [128, 8, 1024], BF16)
    fxb = SB("fxb", [128, 8, 513], F32)
    fxn = SB("fxn", [128, 8, 513], BF16)
    rfb = SB("rfb", [128, 8, 512], BF16)
    kab = SB("kab", [128, 8, 512], BF16)
    ftmp = SB("ftmp", [128, 512], F32)
    fmx = xnb
    fresp = t1b
    otile = [xtile[0][:, :, 0:256].bitcast(BF16),
             xtile[0][:, :, 256:512].bitcast(BF16)]
    _resb = xtile[1][:]
    resb = [_resb, _resb]
    _outsb = SB("outsb", [128, 8, 512], F32)[:]
    outsb = [_outsb, _outsb]


    for q4 in range(8):
        emit("sync", lambda e, q4=q4: e.dma_start(
            out=wstg[:], in_=woutv[:, :, q4 * 128:(q4 + 1) * 128]),
            writes=[wstg], inc=16)
        emit("vector", lambda e, q4=q4: e.tensor_copy(
            wotb[:, :, q4 * 128:(q4 + 1) * 128], wstg[:]),
            reads=[wstg], writes=[wotb])

    def OUT(ot):
        def oload(e, ot=ot):
            if rank_sv[0] is None:
                rank_sv[0] = e.snap(ridx_reg)
            return e.dma_start(
                out=otile[ot % 2][:],
                in_=agoutv.transpose([1, 0, 2])[:, :, bass.ds(rank_sv[0] * SLICE + ot * 512, 512)])
        emit("gpsimd", oload, reads=[agout], writes=[otile[ot % 2]],
             sem="dg", inc=16, queue="gpsq")
        emit("sync", lambda e, ot=ot: e.dma_start(
            out=resb[ot % 2][:], in_=dresv[:, :, ot * 512:(ot + 1) * 512]),
            reads=[dresid], writes=[resb[ot % 2]], inc=16)
        for mi in range(8):
            bk = pb[2 + mi % 2]
            for kc in range(8):
                emit("tensor", lambda e, kc=kc, mi=mi, ot=ot, bk=bk: e.matmul(
                    bk[:, :], wotb[:, kc, mi * 128:(mi + 1) * 128],
                    otile[ot % 2][:, kc], start=(kc == 0), stop=(kc == 7)),
                    reads=[wotb, otile[ot % 2]], writes=[bk])
            emit("vector", lambda e, mi=mi, ot=ot, bk=bk: e.scalar_tensor_tensor(
                outsb[ot % 2][:, mi], bk[:, :], 1.0, resb[ot % 2][:, mi],
                ALU.mult, ALU.add), reads=[bk, resb[ot % 2]],
                writes=[outsb[ot % 2]])
        emit("gpsimd", lambda e, ot=ot: e.dma_start(
            out=outpv[:, :, ot * 512:(ot + 1) * 512], in_=outsb[ot % 2][:]),
            reads=[outsb[ot % 2]], writes=[outp], sem="dg", inc=16,
            queue="gpsq")


    for fp in range(4):
        FFN(fp)
    for ot in range(4):
        OUT(ot)

    fin_dg = cnt["dg"]
    fin_dx = cnt["dx"]
    steps["gpsimd"].append(lambda e: e.wait_ge(sems["dg"], fin_dg))
    steps["gpsimd"].append(lambda e: e.wait_ge(sems["dx"], fin_dx))

    with ctx:
        with nc.Block() as block:
            @block.sync
            def _(e):
                for fn in steps["sync"]:
                    fn(e)

            @block.tensor
            def _(e):
                for fn in steps["tensor"]:
                    fn(e)

            @block.vector
            def _(e):
                for fn in steps["vector"]:
                    fn(e)

            @block.scalar
            def _(e):
                for fn in steps["scalar"]:
                    fn(e)

            @block.gpsimd
            def _(e):
                for fn in steps["gpsimd"]:
                    fn(e)
    return nc


dbg_taps_emit = None


def _host_inputs(inputs):
    x = np.asarray(inputs["L0_attn_x"], np.float32)
    Wqkv = np.asarray(inputs["L0_attn_Wqkv"], np.float32)
    Wg = np.asarray(inputs["L0_attn_Wg"], np.float32)
    bg = np.asarray(inputs["L0_attn_bg"], np.float32)
    Wb = np.asarray(inputs["L0_attn_Wb"], np.float32)
    bb = np.asarray(inputs["L0_attn_bb"], np.float32)
    Wout = np.asarray(inputs["L0_attn_Wout"], np.float32)
    W_r = np.asarray(inputs["L0_ffn_W_r"], np.float32)
    W_k = np.asarray(inputs["L0_ffn_W_k"], np.float32)
    W_v = np.asarray(inputs["L0_ffn_W_v"], np.float32)
    mu_r = np.asarray(inputs["L0_ffn_mu_r"], np.float32)
    mu_k = np.asarray(inputs["L0_ffn_mu_k"], np.float32)

    jj, ii = np.meshgrid(np.arange(128), np.arange(128), indexing="ij")
    maskS = np.where(jj < ii, 0.0, NEG).astype(np.float32)
    maskI = np.where(jj <= ii, 0.0, NEG).astype(np.float32)
    sel0 = np.zeros((128, 128), np.float32)
    sel0[np.arange(128) % 32 == 0, :] = 1.0
    sel1 = np.zeros((128, 128), np.float32)
    sel1[np.arange(128) % 32 == 1, :] = 1.0
    halfF = np.zeros((128, 128), np.float32)
    halfF[np.arange(128) % 32 == 0, 0:64] = 1.0
    halfF[np.arange(128) % 32 == 1, 64:128] = 1.0
    css = np.zeros((128, 4, 128), np.float32)
    for part in range(4):
        c0 = [0, 32, 64, 96][part]
        css[0:64, part, c0] = 1.0
        css[64:128, part, c0 + 1] = 1.0
    ident = np.eye(128, dtype=np.float32)
    mu = np.stack([mu_r.reshape(8, 128).T, mu_k.reshape(8, 128).T], axis=-1)
    wffn = np.hstack([W_r.T, W_k.T, W_v.T]).astype(np.float32)
    wout_t = np.ascontiguousarray(Wout.T)

    ins = []
    for c in range(NCORE):
        b, s = c // 4, c % 4
        xslice = np.empty((D, SLICE + 1), np.float32)
        xslice[:, 1:] = x[b, s * SLICE:(s + 1) * SLICE].T
        xslice[:, 0] = x[b, s * SLICE - 1] if s > 0 else 0.0
        h0 = 4 * s
        wp1c = np.zeros((D, 640), np.float32)
        wp1c[:, 0:128] = Wqkv[h0 * 64:(h0 + 2) * 64].T
        wp1c[:, 128:256] = Wqkv[(h0 + 2) * 64:(h0 + 4) * 64].T
        wp1c[:, 256:384] = Wqkv[D + h0 * 64:D + (h0 + 2) * 64].T
        wp1c[:, 384:512] = Wqkv[D + (h0 + 2) * 64:D + (h0 + 4) * 64].T
        for j in range(2):
            wp1c[:, 512 + j] = Wg[h0 + j]
            wp1c[:, 512 + 32 + j] = Wg[h0 + 2 + j]
            wp1c[:, 512 + 64 + j] = Wb[h0 + j]
            wp1c[:, 512 + 96 + j] = Wb[h0 + 2 + j]
        wtmc = np.hstack([Wqkv[D + h0 * 64:D + h0 * 64 + 256].T,
                          Wqkv[2 * D + h0 * 64:2 * D + h0 * 64 + 256].T])
        gb = np.zeros((98, 1), np.float32)
        for j in range(2):
            gb[j, 0] = bg[h0 + j]
            gb[32 + j, 0] = bg[h0 + 2 + j]
            gb[64 + j, 0] = bb[h0 + j]
            gb[96 + j, 0] = bb[h0 + 2 + j]
        ins.append(dict(
            xt=np.ascontiguousarray(x[b].T), xsl=xslice, wp1=wp1c,
            wtm=np.ascontiguousarray(wtmc), wffn=wffn, wout=wout_t,
            mu=np.ascontiguousarray(mu), gb=gb,
            ridx=np.array([[s]], np.int32),
            c_maskS=maskS, c_maskI=maskI, c_sel0=sel0, c_sel1=sel1,
            c_maskSR=np.ascontiguousarray(maskS.T),
            c_halfF=halfF, c_halfB=halfF.astype(BFNP),
            c_idB=ident.astype(BFNP), c_idF=ident,
            c_ss=css.astype(BFNP),
            c_o1b=np.ones((128, 1), BFNP),
            c_o1f=np.ones((1, 128), np.float32),
            c_eps=np.stack([np.full(128, 1e-5, np.float32),
                            np.zeros(128, np.float32)], axis=1),
        ))
    return ins


def kernel(**inputs):
    ins = _host_inputs(inputs)
    nc = _build()
    res = run_bass_kernel_spmd(nc, ins, core_ids=list(range(NCORE)))
    out = np.empty((B, T, D), np.float32)
    for c in range(NCORE):
        b, s = c // 4, c % 4
        out[b, s * SLICE:(s + 1) * SLICE, :] = np.asarray(
            res.results[c]["outp"], np.float32).T
    return out


# version 17
# speedup vs baseline: 1.1044x; 1.1044x over previous
import numpy as np
import ml_dtypes
from contextlib import ExitStack

import concourse.bass as bass
import concourse.mybir as mybir
from concourse.bass_utils import run_bass_kernel_spmd

F32 = mybir.dt.float32
BF16 = mybir.dt.bfloat16
I32 = mybir.dt.int32
AF = mybir.ActivationFunctionType
ALU = mybir.AluOpType
BFNP = ml_dtypes.bfloat16

B, T, D, H, DH = 2, 8192, 1024, 16, 64
NCORE = 8
TILE = 512
NT = T // TILE
CH = 128
CPT = TILE // CH
SLICE = T // 4
NEG = -1.0e6


def _build(dbg_taps=None):
    nc = bass.Bass("TRN2", target_bir_lowering=False, debug=False)

    xt = nc.dram_tensor("xt", [D, T], F32, kind="ExternalInput").ap()
    xsl = nc.dram_tensor("xsl", [D, SLICE + 1], F32, kind="ExternalInput").ap()
    wp1 = nc.dram_tensor("wp1", [D, 640], F32, kind="ExternalInput").ap()
    wtm = nc.dram_tensor("wtm", [D, 512], F32, kind="ExternalInput").ap()
    wffn = nc.dram_tensor("wffn", [D, 3072], F32, kind="ExternalInput").ap()
    wout = nc.dram_tensor("wout", [D, D], F32, kind="ExternalInput").ap()
    muin = nc.dram_tensor("mu", [128, 8, 2], F32, kind="ExternalInput").ap()
    gbin = nc.dram_tensor("gb", [98, 1], F32, kind="ExternalInput").ap()
    ridx = nc.dram_tensor("ridx", [1, 1], I32, kind="ExternalInput").ap()
    c_in = {}
    for nm, sh, dt in (("c_maskS", [128, 128], F32), ("c_maskI", [128, 128], F32),
                       ("c_maskSR", [128, 128], F32),
                       ("c_sel0", [128, 128], F32), ("c_sel1", [128, 128], F32),
                       ("c_halfF", [128, 128], F32), ("c_halfB", [128, 128], BF16),
                       ("c_idB", [128, 128], BF16), ("c_idF", [128, 128], F32),
                       ("c_ss", [128, 4, 128], BF16), ("c_o1b", [128, 1], BF16),
                       ("c_o1f", [1, 128], F32), ("c_eps", [128, 2], F32)):
        c_in[nm] = nc.dram_tensor(nm, sh, dt, kind="ExternalInput").ap()

    outp = nc.dram_tensor("outp", [D, SLICE], F32, kind="ExternalOutput").ap()
    agin = nc.dram_tensor("agin", [256, T], BF16, kind="Internal")
    agout = nc.dram_tensor("agout", [4, 256, T], BF16, kind="Internal")
    dresid = nc.dram_tensor("dresid", [D, SLICE], F32, kind="Internal")
    dbg_out = {}
    if dbg_taps:
        for nm, sh, dt in dbg_taps:
            dbg_out[nm] = nc.dram_tensor(nm, sh, dt, kind="ExternalOutput").ap()

    ctx = ExitStack()
    SB = lambda n, sh, dt: ctx.enter_context(nc.sbuf_tensor(n, sh, dt))
    PS_ = lambda n: ctx.enter_context(nc.psum_tensor(n, [128, 512], F32))

    xtile = [SB(f"xtile{i}", [128, 8, TILE], F32) for i in range(2)]
    ybuf = [SB(f"ybuf{i}", [128, TILE], BF16) for i in range(2)]
    y2buf = [SB(f"y2buf{i}", [128, TILE], BF16) for i in range(2)]
    xnb = SB("xnb", [128, 8, TILE], BF16)
    t1b = [SB(f"t1b{i}", [128, TILE], F32) for i in range(2)]
    rrep = SB("rrep", [128, TILE], F32)
    mrep = SB("mrep", [128, TILE], F32)
    rowq = [SB(f"rowq{i}", [1, TILE], F32) for i in range(5)]
    mrrow = SB("mrrow", [1, TILE + 1], F32)
    hcol = [SB(f"hcol{i}", [128, 1], F32) for i in range(2)]
    wstg = SB("wstg", [128, 8, 256], F32)
    ebuf = SB("ebuf", [128, TILE], F32)
    tbb = SB("tbb", [128, TILE], F32)
    sqb = SB("sqb", [128, TILE], BF16)
    fmt = sqb
    mub = SB("mub", [128, 8, 2], F32)
    mu1b = SB("mu1b", [128, 8, 2], F32)
    kc_ = {}
    for nm, sh, dt in (("kmask2", [128, 256], F32),
                       ("kmaskSR", [128, 128], F32),
                       ("ksel0", [128, 128], F32), ("ksel1", [128, 128], F32),
                       ("khalfF", [128, 128], F32), ("khalfB", [128, 128], BF16),
                       ("kidB", [128, 128], BF16), ("kidF", [128, 128], F32),
                       ("kss", [128, 4, 128], BF16), ("ko1b", [128, 1], BF16),
                       ("ko1f", [1, 128], F32), ("keps", [128, 2], F32)):
        kc_[nm] = SB(nm, sh, dt)
    ksel = [kc_["ksel0"], kc_["ksel1"]]
    kgb = SB("kgb", [128, 1], F32)
    ngb = SB("ngb", [128, 1], F32)
    sridx = SB("sridx", [1, 1], I32)

    pb = [PS_(f"pb{i}") for i in range(8)]

    sctx = ExitStack()
    SBs = lambda n, sh, dt: sctx.enter_context(nc.sbuf_tensor(n, sh, dt))
    wp1b = SBs("wp1b", [128, 8, 640], BF16)
    wtmb = SBs("wtmb", [128, 8, 512], BF16)
    qk01 = [SBs(f"qk01s{i}", [128, CPT, 256], BF16) for i in range(2)]
    qk23 = [SBs(f"qk23s{i}", [128, CPT, 256], BF16) for i in range(2)]
    ktmv = [SBs(f"ktmvs{i}", [128, CPT, 512], BF16) for i in range(2)]
    abb = SBs("abb", [128, TILE], F32)
    lntb = SBs("lntb", [128, TILE], F32)
    sscb = SBs("sscb", [128, TILE], F32)
    t1r = SBs("t1r", [128, TILE], F32)
    t2r = SBs("t2r", [128, TILE], F32)
    t3r = sscb
    lnssb = lntb
    _gneg = SBs("gnegs", [128, TILE], F32)
    _colsrc = SBs("colsrcs", [128, TILE], F32)
    gneg = [_gneg, SBs("gnegs1", [128, TILE], F32)]
    rows3 = [SBs(f"rows3_{i}", [128, 3 * TILE], F32) for i in range(2)]
    r3A = lambda i, lc: rows3[i][:, lc * 384:lc * 384 + 128]
    r3M = lambda i, lc: rows3[i][:, lc * 384 + 128:lc * 384 + 256]
    r3C = lambda i, lc: rows3[i][:, lc * 384 + 256:lc * 384 + 384]
    colsrc = [_colsrc, SBs("colsrcs1", [128, TILE], F32)]
    escr = SBs("escr", [128, 128], F32)
    expAq = [SBs(f"expAq{i}", [128, CPT, 128], BF16) for i in range(2)]
    gdif = SBs("gdif", [128, 1], F32)
    ngend = SBs("ngend", [128, 1], F32)
    colsA = SBs("colsA", [128, 128], F32)
    colsBq = [SBs(f"colsBq{i}", [128, CPT, 128], F32) for i in range(2)]
    nmcols = SBs("nmcols", [128, 128], F32)
    t3b = [SBs(f"t3b{i}", [128, 384], F32) for i in range(2)]
    dab3 = [SBs(f"dab3{i}", [128, 384], BF16) for i in range(2)]
    attnq = [[SBs(f"attnq{i}_{s}", [128, CPT, 128], BF16) for s in range(4)]
             for i in range(2)]
    _mnegq = [SBs(f"mnegq_{s}", [128, CPT, 128], BF16) for s in range(4)]
    mnegq = [_mnegq, _mnegq]
    _mrawq = [SBs(f"mrawq_{s}", [128, CPT, 128], BF16) for s in range(4)]
    mrawq = [_mrawq, _mrawq]
    _rhsq = [SBs(f"rhsq_{s}", [128, CPT, 128], BF16) for s in range(4)]
    rhsq = [_rhsq, _rhsq]
    xq = [SBs(f"xq{s}", [128, CPT, 128], BF16) for s in range(4)]
    pq = [[SBs(f"pq{s}_{i}", [128, CPT, 128], BF16) for i in range(2)]
          for s in range(4)]
    prawq = [SBs(f"prawq{s}", [128, CPT, 128], BF16) for s in range(4)]
    _uvw = [SBs(f"uvw_{s}", [128, CPT, 128], BF16) for s in range(4)]
    uvw = [_uvw, _uvw]
    wfm = [SBs(f"wfm{p}", [128, CPT, 128], BF16) for p in range(2)]
    ubuf = [SBs(f"ubuf{s}", [128, 64], BF16) for s in range(4)]
    utb = [SBs(f"utb{s}", [128, 64], BF16) for s in range(4)]
    qgb = [SBs(f"qgb{p}", [128, 128], BF16) for p in range(2)]
    gCcq = [SBs(f"gCcq{i}", [128, CPT], F32) for i in range(2)]
    gCcol = [SBs(f"gCcol{p}", [128, 1], F32) for p in range(2)]
    ps32 = [SBs(f"ps32{p}", [128, 64], F32) for p in range(2)]
    psb = [SBs(f"psb{p}", [128, 64], BF16) for p in range(2)]
    _oseg = SBs("osegs", [128, 2, TILE], BF16)
    oseg = [_oseg, _oseg]

    sems = {}
    for sname in ("dx", "dg", "cc", "t", "v", "s", "g"):
        sems[sname] = ctx.enter_context(nc.semaphore(sname))

    steps = {e: [] for e in ("sync", "tensor", "vector", "scalar", "gpsimd")}
    cnt = {k: 0 for k in sems}
    RES = {}
    HWM = {}

    def _res(obj):
        key = id(obj)
        if key not in RES:
            RES[key] = {"w": None, "r": {}}
        return RES[key]

    deferq = []
    defer_on = [False]

    def emit(q, fn, reads=(), writes=(), sem=None, inc=1, queue=None):
        if defer_on[0]:
            deferq.append((q, fn, reads, writes, sem, inc, queue))
            return
        if sem is None:
            sem = {"sync": "dx", "tensor": "t", "vector": "v",
                   "scalar": "s", "gpsimd": "g"}[q]
        queue = queue or (q if sem not in ("dx", "dg") else "q_" + sem)
        waits = {}

        def addw(ent):
            if ent is None:
                return
            eq, es, ec = ent
            if eq == queue:
                return
            if waits.get(es, 0) < ec:
                waits[es] = ec
        for b in reads:
            addw(_res(b)["w"])
        for b in writes:
            r = _res(b)
            addw(r["w"])
            for eq, (es, ec) in r["r"].items():
                addw((eq, es, ec))
        for s2, v2 in sorted(waits.items()):
            if HWM.get((q, s2), -1) >= v2:
                continue
            HWM[(q, s2)] = v2
            steps[q].append(lambda e, s=s2, v=v2: e.wait_ge(sems[s], v))
        cnt[sem] += inc
        me = (queue, sem, cnt[sem])
        steps[q].append(lambda e, f=fn, s=sem, k=inc: f(e).then_inc(sems[s], k))
        for b in reads:
            _res(b)["r"][queue] = (sem, cnt[sem])
        for b in writes:
            r = _res(b)
            r["w"] = me
            r["r"] = {}
        return cnt[sem]

    def drain(n=1 << 30):
        for _ in range(min(n, len(deferq))):
            q, fn, reads, writes, sem, inc, queue = deferq.pop(0)
            emit(q, fn, reads, writes, sem, inc, queue)

    def fence():
        vals = dict(cnt)
        for q2 in ("sync", "tensor", "vector", "scalar", "gpsimd"):
            for sname, v in vals.items():
                if v > 0 and HWM.get((q2, sname), -1) < v:
                    HWM[(q2, sname)] = v
                    steps[q2].append(
                        lambda e, s=sname, vv=v: e.wait_ge(sems[s], vv))
        RES.clear()

    xtv = xt.rearrange("(c p) t -> p c t", p=128)
    xslv = xsl.rearrange("(c p) t -> p c t", p=128)
    wp1v = wp1.rearrange("(c p) m -> p c m", p=128)
    wtmv = wtm.rearrange("(c p) m -> p c m", p=128)
    wffnv = wffn.rearrange("(c p) m -> p c m", p=128)
    woutv = wout.rearrange("(c p) m -> p c m", p=128)
    aginv = agin.ap().rearrange("(r p) t -> p r t", p=128)
    agoutv = agout.ap().rearrange("r (h p) t -> (r h) p t", p=128)
    dresv = dresid.ap().rearrange("(c p) t -> p c t", p=128)
    outpv = outp.rearrange("(c p) t -> p c t", p=128)

    DMA = "dma_marker"

    emit("sync", lambda e: e.dma_start(out=kc_["kmask2"][:, 0:128],
         in_=c_in["c_maskI"]), writes=[kc_["kmask2"]], inc=16)
    emit("sync", lambda e: e.dma_start(out=kc_["kmask2"][:, 128:256],
         in_=c_in["c_maskS"]), writes=[kc_["kmask2"]], inc=16)
    for nm, dst in (("c_maskSR", kc_["kmaskSR"]),
                    ("c_eps", kc_["keps"]),
                    ("c_sel0", ksel[0]), ("c_sel1", ksel[1]),
                    ("c_halfF", kc_["khalfF"]), ("c_halfB", kc_["khalfB"]),
                    ("c_idB", kc_["kidB"]), ("c_idF", kc_["kidF"]),
                    ("c_ss", kc_["kss"]), ("c_o1b", kc_["ko1b"]),
                    ("c_o1f", kc_["ko1f"])):
        emit("sync", lambda e, d=dst, s=c_in[nm]: e.dma_start(out=d[:], in_=s),
             writes=[dst], inc=16)
    emit("sync", lambda e: e.dma_start(out=mub[:], in_=muin), writes=[mub], inc=16)
    emit("sync", lambda e: e.dma_start(out=sridx[:], in_=ridx), writes=[sridx], inc=16)
    emit("vector", lambda e: e.memset(kgb[:], 0.0), writes=[kgb])
    emit("sync", lambda e: e.dma_start(out=kgb[0:98, :], in_=gbin),
         writes=[kgb], inc=16)
    for lo, hi in ((0, 256), (256, 512), (512, 640)):
        emit("sync", lambda e, lo=lo, hi=hi: e.dma_start(
            out=wstg[:, :, 0:hi - lo], in_=wp1v[:, :, lo:hi]),
            writes=[wstg], inc=16)
        emit("vector", lambda e, lo=lo, hi=hi: e.tensor_copy(
            wp1b[:, :, lo:hi], wstg[:, :, 0:hi - lo]),
            reads=[wstg], writes=[wp1b])
    for lo in (0, 256):
        emit("sync", lambda e, lo=lo: e.dma_start(
            out=wstg[:], in_=wtmv[:, :, lo:lo + 256]), writes=[wstg], inc=16)
        emit("vector", lambda e, lo=lo: e.tensor_copy(
            wtmb[:, :, lo:lo + 256], wstg[:]), reads=[wstg], writes=[wtmb])
    emit("vector", lambda e: e.tensor_scalar(ngb[:], kgb[:], -1.0, None, ALU.mult),
         reads=[kgb], writes=[ngb])
    for p in range(2):
        emit("vector", lambda e, p=p: e.memset(ps32[p][:], 0.0), writes=[ps32[p]])
        emit("vector", lambda e, p=p: e.memset(psb[p][:], 0.0), writes=[psb[p]])
    emit("vector", lambda e: e.tensor_scalar(
        mu1b[:], mub[:], -1.0, 1.0, ALU.mult, ALU.add),
        reads=[mub], writes=[mu1b])

    ridx_reg = nc.gpsimd.alloc_register("ridxreg")
    rank_sv = [None]
    emit("gpsimd", lambda e: e.reg_load(ridx_reg, sridx[0:1, 0:1]),
         reads=[sridx], writes=[], sem="g")

    def XL(t):
        xb = xtile[t % 2]
        emit("sync", lambda e, t=t, xb=xb: e.dma_start(
            out=xb[:], in_=xtv[:, :, t * TILE:(t + 1) * TILE]),
            writes=[xb], inc=16)

    def P1(t):
        pt = t % 2
        xb = xtile[pt]
        for kc in range(8):
            emit("vector", lambda e, kc=kc: e.tensor_copy(
                ybuf[kc % 2][:], xb[:, kc]), reads=[xb], writes=[ybuf[kc % 2]])
            emit("scalar", lambda e, kc=kc: e.activation(
                y2buf[kc % 2][:], xb[:, kc], AF.Square),
                reads=[xb], writes=[y2buf[kc % 2]])
            emit("tensor", lambda e, kc=kc: e.matmul(
                pb[2][0:1, :], kc_["ko1b"][:], ybuf[kc % 2][:],
                start=(kc == 0), stop=(kc == 7)),
                reads=[ybuf[kc % 2], kc_["ko1b"]], writes=[pb[2]])
            emit("tensor", lambda e, kc=kc: e.matmul(
                pb[3][0:1, :], kc_["ko1b"][:], y2buf[kc % 2][:],
                start=(kc == 0), stop=(kc == 7)),
                reads=[y2buf[kc % 2], kc_["ko1b"]], writes=[pb[3]])
        emit("vector", lambda e: e.tensor_scalar(
            rowq[0][:], pb[2][0:1, :], 1.0 / D, None, ALU.mult),
            reads=[pb[2]], writes=[rowq[0]])
        emit("vector", lambda e: e.tensor_tensor(
            rowq[1][:], rowq[0][:], rowq[0][:], ALU.mult),
            reads=[rowq[0]], writes=[rowq[1]])
        emit("vector", lambda e: e.scalar_tensor_tensor(
            rowq[2][:], pb[3][0:1, :], 1.0 / D, rowq[1][:], ALU.mult,
            ALU.subtract), reads=[pb[3], rowq[1]], writes=[rowq[2]])
        emit("scalar", lambda e: e.activation(rowq[3][:], rowq[2][:], AF.Ln,
                                              bias=kc_["keps"][0:1, 0:1]),
             reads=[rowq[2], kc_["keps"]], writes=[rowq[3]])
        emit("scalar", lambda e: e.activation(rowq[4][:], rowq[3][:], AF.Exp,
                                              scale=-0.5),
             reads=[rowq[3]], writes=[rowq[4]])
        emit("vector", lambda e: e.tensor_tensor(
            mrrow[:, 0:TILE], rowq[0][:], rowq[4][:], ALU.mult),
            reads=[rowq[0], rowq[4]], writes=[mrrow])
        emit("tensor", lambda e: e.matmul(pb[2][:, :], kc_["ko1f"][:],
                                          rowq[4][:], start=True, stop=True),
             reads=[rowq[4], kc_["ko1f"]], writes=[pb[2]])
        emit("tensor", lambda e: e.matmul(pb[3][:, :], kc_["ko1f"][:],
                                          mrrow[:, 0:TILE], start=True, stop=True),
             reads=[mrrow, kc_["ko1f"]], writes=[pb[3]])
        emit("scalar", lambda e: e.copy(rrep[:], pb[2][:, :]),
             reads=[pb[2]], writes=[rrep])
        emit("scalar", lambda e: e.copy(mrep[:], pb[3][:, :]),
             reads=[pb[3]], writes=[mrep])
        for kc in range(8):
            emit("gpsimd", lambda e, kc=kc: e.tensor_tensor(
                t1b[kc % 2][:], xb[:, kc], rrep[:], ALU.mult),
                reads=[xb, rrep], writes=[t1b[kc % 2]])
            emit("vector", lambda e, kc=kc: e.scalar_tensor_tensor(
                xnb[:, kc], mrep[:], -1.0, t1b[kc % 2][:], ALU.mult, ALU.add),
                reads=[mrep, t1b[kc % 2]], writes=[xnb])
        for mi in range(5):
            bk = pb[mi % 2]
            for kc in range(8):
                emit("tensor", lambda e, kc=kc, mi=mi, bk=bk: e.matmul(
                    bk[:, :], wp1b[:, kc, mi * 128:(mi + 1) * 128], xnb[:, kc],
                    start=(kc == 0), stop=(kc == 7)),
                    reads=[wp1b, xnb], writes=[bk])
            if mi < 4:
                qkb = [qk01, qk23][mi % 2][pt]
                csl = slice(0, 128) if mi < 2 else slice(128, 256)
                emit("scalar", lambda e, qkb=qkb, csl=csl, bk=bk: e.copy(
                    qkb[:, :, csl], bk[:, :].rearrange("p (c x) -> p c x", x=128)),
                    reads=[bk], writes=[qkb])
            else:
                emit("scalar", lambda e, bk=bk: e.activation(
                    ebuf[:], bk[:, :], AF.Exp, bias=ngb[:], scale=-1.0),
                    reads=[bk, ngb], writes=[ebuf])
                emit("vector", lambda e: e.tensor_scalar(
                    tbb[:], ebuf[:], 1.0, None, ALU.add),
                    reads=[ebuf], writes=[tbb])
                emit("vector", lambda e: e.reciprocal(abb[:], tbb[:]),
                     reads=[tbb], writes=[abb])
                emit("scalar", lambda e: e.activation(lntb[:], tbb[:], AF.Ln),
                     reads=[tbb], writes=[lntb])
                emit("vector", lambda e, pt=pt: e.tensor_tensor_scan(
                    gneg[pt][:], lntb[:], lntb[:], 0.0, ALU.add, ALU.bypass),
                    reads=[lntb], writes=[gneg[pt]])
                emit("vector", lambda e: e.tensor_copy(t1r[0:2, :], lntb[64:66, :]),
                     reads=[lntb], writes=[t1r])
                emit("vector", lambda e: e.tensor_copy(t1r[32:34, :], lntb[96:98, :]),
                     reads=[lntb], writes=[t1r])
        for tg in range(4):
            bk = pb[2 + tg % 2]
            for kc in range(8):
                emit("tensor", lambda e, kc=kc, tg=tg, bk=bk: e.matmul(
                    bk[:, :], xnb[:, kc, tg * 128:(tg + 1) * 128],
                    wtmb[:, kc, :], start=(kc == 0), stop=(kc == 7)),
                    reads=[xnb, wtmb], writes=[bk])
            emit("scalar", lambda e, tg=tg, pt=pt, bk=bk: e.copy(
                ktmv[pt][:, tg, :], bk[:, :]), reads=[bk], writes=[ktmv[pt]])
        for part in range(4):
            qkb = [qk01, qk23][part % 2][pt]
            csl = slice(0, 128) if part < 2 else slice(128, 256)
            emit("scalar", lambda e, qkb=qkb, csl=csl: e.activation(
                sqb[:].rearrange("p (c x) -> p c x", x=128), qkb[:, :, csl],
                AF.Square), reads=[qkb], writes=[sqb])
            emit("tensor", lambda e, part=part: e.matmul(
                pb[0][:, :], kc_["kss"][:, part, :], sqb[:],
                start=(part == 0), stop=(part == 3)),
                reads=[sqb, kc_["kss"]], writes=[pb[0]])
        emit("vector", lambda e: e.tensor_scalar(
            sscb[:], pb[0][:, :], 1e-24, None, ALU.max),
            reads=[pb[0]], writes=[sscb])
        emit("scalar", lambda e: e.activation(lnssb[:], sscb[:], AF.Ln),
             reads=[sscb], writes=[lnssb])
        emit("vector", lambda e: e.tensor_scalar(
            lnssb[:], lnssb[:], 0.5, None, ALU.mult),
            reads=[lnssb], writes=[lnssb])
        emit("vector", lambda e: e.tensor_copy(t2r[0:2, :], lnssb[64:66, :]),
             reads=[lnssb], writes=[t2r])
        emit("vector", lambda e: e.tensor_copy(t2r[32:34, :], lnssb[96:98, :]),
             reads=[lnssb], writes=[t2r])
        emit("gpsimd", lambda e: e.tensor_tensor(
            t3r[:], t2r[:], t1r[:], ALU.add),
            reads=[t2r, t1r], writes=[t3r])
        for lc4 in range(CPT):
            cs4 = slice(lc4 * 128, (lc4 + 1) * 128)
            emit("gpsimd", lambda e, pt=pt, lc4=lc4, cs4=cs4: e.tensor_tensor(
                r3A(pt, lc4), lnssb[:, cs4], gneg[pt][:, cs4], ALU.add),
                reads=[lnssb, gneg[pt]], writes=[rows3[pt]])
            emit("gpsimd", lambda e, pt=pt, lc4=lc4, cs4=cs4: e.tensor_tensor(
                r3M(pt, lc4), t3r[:, cs4], gneg[pt][:, cs4], ALU.add),
                reads=[t3r, gneg[pt]], writes=[rows3[pt]])
            emit("gpsimd", lambda e, pt=pt, lc4=lc4, cs4=cs4: e.tensor_tensor(
                r3C(pt, lc4), t2r[:, cs4], gneg[pt][:, cs4], ALU.subtract),
                reads=[t2r, gneg[pt]], writes=[rows3[pt]])
        emit("gpsimd", lambda e, pt=pt: e.tensor_tensor(
            colsrc[pt][:], gneg[pt][:], t2r[:], ALU.subtract),
            reads=[t2r, gneg[pt]], writes=[colsrc[pt]])
        emit("vector", lambda e, pt=pt: e.tensor_copy(
            colsrc[pt][64:66, :], abb[64:66, :]),
            reads=[abb], writes=[colsrc[pt]])
        emit("vector", lambda e, pt=pt: e.tensor_copy(
            colsrc[pt][96:98, :], abb[96:98, :]),
            reads=[abb], writes=[colsrc[pt]])

    def P23(t, nxt=None):
        if nxt is not None:
            XL(nxt)
        pt = t % 2
        for lc in range(CPT):
            csl = slice(lc * 128, (lc + 1) * 128)
            if lc == 0:
                emit("vector", lambda e, pt=pt, lc=lc: e.tensor_scalar(
                    gdif[:], gneg[pt][:, (lc + 1) * 128 - 1:(lc + 1) * 128],
                    -1.0, None, ALU.mult), reads=[gneg[pt]], writes=[gdif])
            else:
                emit("vector", lambda e, pt=pt, lc=lc: e.tensor_tensor(
                    gdif[:], gneg[pt][:, lc * 128 - 1:lc * 128],
                    gneg[pt][:, (lc + 1) * 128 - 1:(lc + 1) * 128],
                    ALU.subtract), reads=[gneg[pt]], writes=[gdif])
            emit("vector", lambda e, pt=pt, lc=lc: e.tensor_scalar(
                ngend[:], gneg[pt][:, (lc + 1) * 128 - 1:(lc + 1) * 128],
                -1.0, None, ALU.mult), reads=[gneg[pt]], writes=[ngend])
            if lc == 0:
                emit("scalar", lambda e, pt=pt, lc=lc: e.activation(
                    escr[0:64, :], r3M(pt, lc)[0:64, :], AF.Exp,
                    scale=-1.0), reads=[rows3[pt]], writes=[escr])
                emit("scalar", lambda e, pt=pt, lc=lc: e.activation(
                    expAq[pt][0:64, lc, :], r3A(pt, lc)[0:64, :], AF.Exp,
                    scale=-1.0), reads=[rows3[pt]], writes=[expAq[pt]])
            else:
                emit("scalar", lambda e, pt=pt, lc=lc: e.activation(
                    escr[0:64, :], r3M(pt, lc)[0:64, :], AF.Exp, scale=-1.0,
                    bias=gneg[pt][0:64, lc * 128 - 1:lc * 128]),
                    reads=[rows3[pt], gneg[pt]], writes=[escr])
                emit("scalar", lambda e, pt=pt, lc=lc: e.activation(
                    expAq[pt][0:64, lc, :], r3A(pt, lc)[0:64, :], AF.Exp,
                    scale=-1.0,
                    bias=gneg[pt][0:64, lc * 128 - 1:lc * 128]),
                    reads=[rows3[pt], gneg[pt]], writes=[expAq[pt]])
            emit("scalar", lambda e, pt=pt, csl=csl: e.activation(
                escr[64:128, :], colsrc[pt][0:64, csl],
                AF.Exp, bias=ngend[0:64, :]),
                reads=[colsrc[pt], ngend], writes=[escr])
            emit("scalar", lambda e, pt=pt, lc=lc: e.activation(
                gCcq[pt][0:64, lc:lc + 1], gdif[0:64, :], AF.Exp),
                reads=[gdif], writes=[gCcq[pt]])
            emit("tensor", lambda e, pt=pt, csl=csl: e.transpose(
                pb[4][:, 384:512], colsrc[pt][:, csl], kc_["kidF"][:]),
                reads=[colsrc[pt], kc_["kidF"]], writes=[pb[4]])
            emit("vector", lambda e: e.tensor_copy(colsA[:], pb[4][:, 384:512]),
                 reads=[pb[4]], writes=[colsA])
            emit("tensor", lambda e: e.transpose(
                pb[4][:, 384:512], escr[:], kc_["kidF"][:]),
                reads=[escr, kc_["kidF"]], writes=[pb[4]])
            emit("vector", lambda e, lc=lc: e.tensor_copy(
                colsBq[pt][:, lc, :], pb[4][:, 384:512]),
                reads=[pb[4]], writes=[colsBq[pt]])
            emit("tensor", lambda e, pt=pt, lc=lc: e.transpose(
                pb[5][:, 384:512], r3M(pt, lc), kc_["kidF"][:]),
                reads=[rows3[pt], kc_["kidF"]], writes=[pb[5]])
            emit("vector", lambda e: e.tensor_scalar(
                nmcols[:], pb[5][:, 384:512], -1.0, None, ALU.mult),
                reads=[pb[5]], writes=[nmcols])

            for s in range(4):
                p, hf = s // 2, s % 2
                bq = 32 * p
                Bh = 64 * hf
                qkb = [qk01, qk23][p][pt]
                bk = pb[2 + s % 2]
                bk2 = pb[6 + s % 2]
                emit("tensor", lambda e, qkb=qkb, lc=lc, Bh=Bh, bk=bk: e.matmul(
                    bk[:, 0:256], qkb[Bh:Bh + 64, lc, 128:256],
                    qkb[Bh:Bh + 64, lc, :], start=True, stop=True),
                    reads=[qkb], writes=[bk])
                emit("tensor", lambda e, pt=pt, bq=bq, hf=hf, bk2=bk2, lc=lc:
                     e.matmul(bk2[:, 0:384],
                              ksel[hf][bq:bq + 2, :],
                              rows3[pt][bq:bq + 2, lc * 384:(lc + 1) * 384],
                              start=True, stop=True),
                     reads=[ksel[hf], rows3[pt]], writes=[bk2])
                emit("scalar", lambda e, s=s, bq=bq, hf=hf: e.activation(
                    t3b[s % 2][:, 0:256], kc_["kmask2"][:], AF.Identity,
                    bias=colsA[:, bq + hf:bq + hf + 1]),
                    reads=[kc_["kmask2"], colsA], writes=[t3b[s % 2]])
                emit("scalar", lambda e, s=s, bq=bq, hf=hf: e.activation(
                    t3b[s % 2][:, 256:384], kc_["kmaskSR"][:], AF.Identity,
                    bias=nmcols[:, bq + hf:bq + hf + 1]),
                    reads=[kc_["kmaskSR"], nmcols], writes=[t3b[s % 2]])
                emit("vector", lambda e, s=s, bk2=bk2: e.scalar_tensor_tensor(
                    t3b[s % 2][:], bk2[:, 0:384], -1.0, t3b[s % 2][:],
                    ALU.mult, ALU.add), reads=[bk2, t3b[s % 2]],
                    writes=[t3b[s % 2]])
                emit("scalar", lambda e, s=s: e.activation(
                    dab3[s % 2][:], t3b[s % 2][:], AF.Exp),
                    reads=[t3b[s % 2]], writes=[dab3[s % 2]])
                emit("vector", lambda e, s=s, lc=lc, bk=bk: e.scalar_tensor_tensor(
                    attnq[pt][s][:, lc, :], bk[:, 0:128], 1.0,
                    dab3[s % 2][:, 0:128], ALU.mult, ALU.mult),
                    reads=[bk, dab3[s % 2]], writes=[attnq[pt][s]])
                emit("vector", lambda e, s=s, lc=lc, bk=bk: e.scalar_tensor_tensor(
                    mnegq[pt][s][:, lc, :], bk[:, 128:256], -1.0,
                    dab3[s % 2][:, 128:256], ALU.mult, ALU.mult),
                    reads=[bk, dab3[s % 2]], writes=[mnegq[pt][s]])
                emit("vector", lambda e, s=s, lc=lc, bk=bk: e.scalar_tensor_tensor(
                    mrawq[pt][s][:, lc, :], bk[:, 128:256], -1.0,
                    dab3[s % 2][:, 256:384], ALU.mult, ALU.mult),
                    reads=[bk, dab3[s % 2]], writes=[mrawq[pt][s]])
                emit("vector", lambda e, s=s, pt=pt, lc=lc, bq=bq, hf=hf:
                     e.tensor_scalar(
                         rhsq[pt][s][:, lc, 0:64],
                         ktmv[pt][:, lc, 256 + s * 64:256 + s * 64 + 64],
                         colsA[:, 64 + bq + hf:64 + bq + hf + 1], None,
                         ALU.mult),
                     reads=[ktmv[pt], colsA], writes=[rhsq[pt][s]])
                emit("vector", lambda e, s=s, pt=pt, lc=lc, bq=bq, hf=hf:
                     e.tensor_scalar(
                         rhsq[pt][s][:, lc, 64:128],
                         ktmv[pt][:, lc, s * 64:s * 64 + 64],
                         colsBq[pt][:, lc, bq + hf:bq + hf + 1], -1.0,
                         ALU.mult, ALU.mult),
                     reads=[ktmv[pt], colsBq[pt]], writes=[rhsq[pt][s]])

        if nxt is not None:
            defer_on[0] = True
            P1(nxt)
            defer_on[0] = False

        for lev in range(7):
            for s in range(4):
                bk = pb[4 + s]
                Xs = rhsq[pt][s] if lev == 0 else xq[s]
                Ps = mnegq[pt][s] if lev == 0 else pq[s][(lev - 1) % 2]
                for l4 in range(CPT):
                    emit("tensor", lambda e, bk=bk, Ps=Ps, Xs=Xs, l4=l4:
                         e.matmul(bk[:, l4 * 128:(l4 + 1) * 128],
                                  Ps[:, l4, :], Xs[:, l4, :],
                                  start=True, stop=True),
                         reads=[Ps, Xs], writes=[bk])
            for s in range(4):
                bk = pb[4 + s]
                Xs = rhsq[pt][s] if lev == 0 else xq[s]
                dest = uvw[pt][s] if lev == 6 else xq[s]
                emit("vector", lambda e, bk=bk, Xs=Xs, d=dest:
                     e.scalar_tensor_tensor(
                         d[:].rearrange("p c x -> p (c x)"), bk[:, :], 1.0,
                         Xs[:].rearrange("p c x -> p (c x)"), ALU.mult,
                         ALU.add),
                     reads=[bk, Xs], writes=[dest])
            drain(5)
            if lev < 6:
                for s in range(4):
                    bk = pb[4 + s]
                    Ps = mnegq[pt][s] if lev == 0 else pq[s][(lev - 1) % 2]
                    Rs = mrawq[pt][s] if lev == 0 else prawq[s]
                    for l4 in range(CPT):
                        emit("tensor", lambda e, bk=bk, Ps=Ps, Rs=Rs, l4=l4:
                             e.matmul(bk[:, l4 * 128:(l4 + 1) * 128],
                                      Rs[:, l4, :], Ps[:, l4, :],
                                      start=True, stop=True),
                             reads=[Ps, Rs], writes=[bk])
                drain(5)
                for s in range(4):
                    bk = pb[4 + s]
                    emit("scalar", lambda e, bk=bk, s=s, lev=lev: e.copy(
                        pq[s][lev % 2][:].rearrange("p c x -> p (c x)"),
                        bk[:, :]), reads=[bk], writes=[pq[s][lev % 2]])
                for s in range(4):
                    bk = pb[4 + s]
                    Ps = mnegq[pt][s] if lev == 0 else pq[s][(lev - 1) % 2]
                    Rs = mrawq[pt][s] if lev == 0 else prawq[s]
                    for l4 in range(CPT):
                        emit("tensor", lambda e, bk=bk, Ps=Ps, Rs=Rs, l4=l4:
                             e.matmul(bk[:, l4 * 128:(l4 + 1) * 128],
                                      Ps[:, l4, :], Rs[:, l4, :],
                                      start=True, stop=True),
                             reads=[Ps, Rs], writes=[bk])
                for s in range(4):
                    bk = pb[4 + s]
                    emit("vector", lambda e, bk=bk, s=s: e.tensor_copy(
                        prawq[s][:].rearrange("p c x -> p (c x)"),
                        bk[:, :]), reads=[bk], writes=[prawq[s]])
            drain(8)

        for lc in range(CPT):
            for p in range(2):
                bk = pb[4 + 2 * p]
                for hf in range(2):
                    s = 2 * p + hf
                    emit("tensor", lambda e, s=s, hf=hf, lc=lc, bk=bk:
                         e.transpose(bk[64 * hf:64 * hf + 64, 320:384]
                                     .bitcast(BF16), uvw[pt][s][:, lc, 64:128],
                                     kc_["kidB"][:]),
                         reads=[uvw[pt][s], kc_["kidB"]], writes=[bk])
                emit("scalar", lambda e, p=p, lc=lc, bk=bk: e.copy(
                    wfm[p][:, lc, :], bk[:, 320:384].bitcast(BF16)),
                    reads=[bk], writes=[wfm[p]])
            for p in range(2):
                bq = 32 * p
                bk = pb[4 + 2 * p]
                qkb = [qk01, qk23][p][pt]
                emit("tensor", lambda e, bq=bq, bk=bk, lc=lc: e.matmul(
                    bk[:, 448:449], kc_["khalfF"][bq:bq + 2, :],
                    gCcq[pt][bq:bq + 2, lc:lc + 1], start=True, stop=True),
                    reads=[kc_["khalfF"], gCcq[pt]], writes=[bk])
                emit("vector", lambda e, p=p, bk=bk: e.tensor_copy(
                    gCcol[p][:], bk[:, 448:449]), reads=[bk], writes=[gCcol[p]])
                for hf in range(2):
                    s = 2 * p + hf
                    Bh = 64 * hf
                    emit("tensor", lambda e, s=s, lc=lc, bk=bk: e.matmul(
                        bk[:, 0:64], kc_["kidB"][:], uvw[pt][s][:, lc, 0:64],
                        start=True, stop=False),
                        reads=[kc_["kidB"], uvw[pt][s]], writes=[bk])
                    emit("tensor", lambda e, s=s, p=p, lc=lc, Bh=Bh,
                         bk=bk: e.matmul(
                             bk[:, 0:64], wfm[p][Bh:Bh + 64, lc, :],
                             psb[p][Bh:Bh + 64, :], start=False, stop=True),
                         reads=[wfm[p], psb[p]], writes=[bk])
                    emit("scalar", lambda e, s=s, bk=bk: e.copy(
                        ubuf[s][:], bk[:, 0:64]), reads=[bk], writes=[ubuf[s]])
                    emit("vector", lambda e, s=s, bq=bq, hf=hf, lc=lc:
                         e.tensor_scalar(
                             utb[s][:], ubuf[s][:],
                             colsBq[pt][:, lc, 64 + bq + hf:64 + bq + hf + 1],
                             None, ALU.mult),
                         reads=[ubuf[s], colsBq[pt]], writes=[utb[s]])
                emit("tensor", lambda e, bq=bq, bk=bk, lc=lc: e.matmul(
                    bk[:, 64:192], kc_["khalfB"][bq:bq + 2, :],
                    expAq[pt][bq:bq + 2, lc, :], start=True, stop=True),
                    reads=[kc_["khalfB"], expAq[pt]], writes=[bk])
                emit("vector", lambda e, p=p, lc=lc, qkb=qkb, bk=bk:
                     e.tensor_tensor(qgb[p][:], qkb[:, lc, 0:128],
                                     bk[:, 64:192], ALU.mult),
                     reads=[qkb, bk], writes=[qgb[p]])
                for hf in range(2):
                    s = 2 * p + hf
                    Bh = 64 * hf
                    emit("tensor", lambda e, s=s, hf=hf, bk=bk, lc=lc,
                         fs=(hf == 0): e.matmul(
                             bk[64 * hf:64 * hf + 64, 192:320],
                             ubuf[s][:], attnq[pt][s][:, lc, :], start=fs,
                             stop=False),
                         reads=[ubuf[s], attnq[pt][s]], writes=[bk])
                    emit("tensor", lambda e, s=s, p=p, hf=hf, Bh=Bh, bk=bk,
                         lt=(hf == 1): e.matmul(
                             bk[64 * hf:64 * hf + 64, 192:320],
                             psb[p][Bh:Bh + 64, :], qgb[p][Bh:Bh + 64, :],
                             start=False, stop=lt),
                         reads=[psb[p], qgb[p]], writes=[bk])
                emit("vector", lambda e, p=p, pt=pt, lc=lc, bk=bk: e.tensor_copy(
                    oseg[pt][:, p, lc * 128:(lc + 1) * 128], bk[:, 192:320]),
                    reads=[bk], writes=[oseg[pt]])
                for hf in range(2):
                    s = 2 * p + hf
                    emit("tensor", lambda e, s=s, hf=hf, pt=pt, lc=lc, bk=bk:
                         e.matmul(bk[64 * hf:64 * hf + 64, 320:384],
                                  ktmv[pt][:, lc, s * 64:s * 64 + 64],
                                  utb[s][:], start=(hf == 0), stop=(hf == 1)),
                         reads=[ktmv[pt], utb[s]], writes=[bk])
                emit("vector", lambda e, p=p, bk=bk: e.scalar_tensor_tensor(
                    ps32[p][:], ps32[p][:], gCcol[p][:], bk[:, 320:384],
                    ALU.mult, ALU.add),
                    reads=[ps32[p], gCcol[p], bk], writes=[ps32[p]])
                emit("scalar", lambda e, p=p: e.copy(psb[p][:], ps32[p][:]),
                     reads=[ps32[p]], writes=[psb[p]])
            drain(8)
        drain()
        emit("gpsimd", lambda e, t=t, pt=pt: e.dma_start(
            out=aginv[:, :, t * TILE:(t + 1) * TILE], in_=oseg[pt][:]),
            reads=[oseg[pt]], writes=[agin], sem="dg", inc=16, queue="gpsq")

    def FFN(fp):
        base = fp * 512
        emit("sync", lambda e, base=base: e.dma_start(
            out=fxb[:], in_=xslv[:, :, base:base + 513]), writes=[fxb], inc=16)
        for kc in range(8):
            emit("vector", lambda e, kc=kc: e.tensor_copy(
                ybuf[kc % 2][:], fxb[:, kc, 0:512]),
                reads=[fxb], writes=[ybuf[kc % 2]])
            emit("scalar", lambda e, kc=kc: e.activation(
                y2buf[kc % 2][:], fxb[:, kc, 0:512], AF.Square),
                reads=[fxb], writes=[y2buf[kc % 2]])
            emit("tensor", lambda e, kc=kc: e.matmul(
                pb[2][0:1, :], kc_["ko1b"][:], ybuf[kc % 2][:],
                start=(kc == 0), stop=(kc == 7)),
                reads=[ybuf[kc % 2], kc_["ko1b"]], writes=[pb[2]])
            emit("tensor", lambda e, kc=kc: e.matmul(
                pb[3][0:1, :], kc_["ko1b"][:], y2buf[kc % 2][:],
                start=(kc == 0), stop=(kc == 7)),
                reads=[y2buf[kc % 2], kc_["ko1b"]], writes=[pb[3]])
        for kc in range(8):
            emit("vector", lambda e, kc=kc: e.tensor_copy(
                ybuf[kc % 2][:, 0:1], fxb[:, kc, 512:513]),
                reads=[fxb], writes=[ybuf[kc % 2]])
            emit("scalar", lambda e, kc=kc: e.activation(
                y2buf[kc % 2][:, 0:1], fxb[:, kc, 512:513], AF.Square),
                reads=[fxb], writes=[y2buf[kc % 2]])
            emit("tensor", lambda e, kc=kc: e.matmul(
                pb[2][32:33, 0:1], kc_["ko1b"][:], ybuf[kc % 2][:, 0:1],
                start=(kc == 0), stop=(kc == 7)),
                reads=[ybuf[kc % 2], kc_["ko1b"]], writes=[pb[2]])
            emit("tensor", lambda e, kc=kc: e.matmul(
                pb[3][32:33, 0:1], kc_["ko1b"][:], y2buf[kc % 2][:, 0:1],
                start=(kc == 0), stop=(kc == 7)),
                reads=[y2buf[kc % 2], kc_["ko1b"]], writes=[pb[3]])
        emit("vector", lambda e: e.tensor_scalar(
            rowq[0][:, 0:512], pb[2][0:1, :], 1.0 / D, None, ALU.mult),
            reads=[pb[2]], writes=[rowq[0]])
        emit("vector", lambda e: e.tensor_scalar(
            mrrow[:, 0:1], pb[2][32:33, 0:1], 1.0 / D, None, ALU.mult),
            reads=[pb[2]], writes=[mrrow])
        emit("vector", lambda e: e.tensor_tensor(
            rowq[1][:, 0:512], rowq[0][:, 0:512], rowq[0][:, 0:512], ALU.mult),
            reads=[rowq[0]], writes=[rowq[1]])
        emit("vector", lambda e: e.scalar_tensor_tensor(
            rowq[2][:, 0:512], pb[3][0:1, :], 1.0 / D, rowq[1][:, 0:512],
            ALU.mult, ALU.subtract), reads=[pb[3], rowq[1]], writes=[rowq[2]])
        emit("scalar", lambda e: e.activation(
            rowq[3][:, 0:512], rowq[2][:, 0:512], AF.Ln,
            bias=kc_["keps"][0:1, 0:1]),
            reads=[rowq[2], kc_["keps"]], writes=[rowq[3]])
        emit("scalar", lambda e: e.activation(
            rowq[4][:, 0:512], rowq[3][:, 0:512], AF.Exp, scale=-0.5),
            reads=[rowq[3]], writes=[rowq[4]])
        emit("vector", lambda e: e.tensor_tensor(
            mrrow[:, 1:513], rowq[0][:, 0:512], rowq[4][:, 0:512], ALU.mult),
            reads=[rowq[0], rowq[4]], writes=[mrrow])
        emit("vector", lambda e: e.tensor_tensor(
            hcol[0][0:1, :], mrrow[0:1, 0:1], mrrow[0:1, 0:1], ALU.mult),
            reads=[mrrow], writes=[hcol[0]])
        emit("vector", lambda e: e.scalar_tensor_tensor(
            hcol[0][0:1, :], pb[3][32:33, 0:1], 1.0 / D, hcol[0][0:1, :],
            ALU.mult, ALU.subtract), reads=[pb[3], hcol[0]], writes=[hcol[0]])
        emit("scalar", lambda e: e.activation(
            hcol[0][0:1, :], hcol[0][0:1, :], AF.Ln,
            bias=kc_["keps"][0:1, 0:1]),
            reads=[hcol[0], kc_["keps"]], writes=[hcol[0]])
        emit("scalar", lambda e: e.activation(
            hcol[0][0:1, :], hcol[0][0:1, :], AF.Exp, scale=-0.5),
            reads=[hcol[0]], writes=[hcol[0]])
        emit("vector", lambda e: e.tensor_tensor(
            hcol[1][0:1, :], mrrow[0:1, 0:1], hcol[0][0:1, :], ALU.mult),
            reads=[mrrow, hcol[0]], writes=[hcol[1]])
        emit("tensor", lambda e: e.matmul(
            pb[2][:, :], kc_["ko1f"][:], rowq[4][:, 0:512], start=True,
            stop=True), reads=[kc_["ko1f"], rowq[4]], writes=[pb[2]])
        emit("tensor", lambda e: e.matmul(
            pb[3][:, :], kc_["ko1f"][:], mrrow[:, 1:513], start=True,
            stop=True), reads=[kc_["ko1f"], mrrow], writes=[pb[3]])
        emit("scalar", lambda e: e.copy(rrep[:], pb[2][:, :]),
             reads=[pb[2]], writes=[rrep])
        emit("scalar", lambda e: e.copy(mrep[:], pb[3][:, :]),
             reads=[pb[3]], writes=[mrep])
        emit("tensor", lambda e: e.matmul(
            pb[2][:, 0:1], kc_["ko1f"][:], hcol[0][0:1, :], start=True,
            stop=True), reads=[kc_["ko1f"], hcol[0]], writes=[pb[2]])
        emit("tensor", lambda e: e.matmul(
            pb[3][:, 0:1], kc_["ko1f"][:], hcol[1][0:1, :], start=True,
            stop=True), reads=[kc_["ko1f"], hcol[1]], writes=[pb[3]])
        emit("scalar", lambda e: e.copy(hcol[0][:], pb[2][:, 0:1]),
             reads=[pb[2]], writes=[hcol[0]])
        emit("scalar", lambda e: e.copy(hcol[1][:], pb[3][:, 0:1]),
             reads=[pb[3]], writes=[hcol[1]])
        for kc in range(8):
            emit("gpsimd", lambda e, kc=kc: e.tensor_tensor(
                t1b[kc % 2][:], fxb[:, kc, 0:512], rrep[:], ALU.mult),
                reads=[fxb, rrep], writes=[t1b[kc % 2]])
            emit("vector", lambda e, kc=kc: e.scalar_tensor_tensor(
                fxn[:, kc, 0:512], mrep[:], -1.0, t1b[kc % 2][:], ALU.mult,
                ALU.add), reads=[mrep, t1b[kc % 2]], writes=[fxn])
        for kc in range(8):
            emit("gpsimd", lambda e, kc=kc: e.tensor_tensor(
                t1b[kc % 2][:, 0:1], fxb[:, kc, 512:513], hcol[0][:], ALU.mult),
                reads=[fxb, hcol[0]], writes=[t1b[kc % 2]])
            emit("vector", lambda e, kc=kc: e.scalar_tensor_tensor(
                fxn[:, kc, 512:513], hcol[1][:], -1.0, t1b[kc % 2][:, 0:1],
                ALU.mult, ALU.add), reads=[hcol[1], t1b[kc % 2]], writes=[fxn])
        for stage in range(3):
            for q4 in range(4):
                emit("sync", lambda e, stage=stage, q4=q4: e.dma_start(
                    out=wstg[:], in_=wffnv[:, :, stage * 1024 + q4 * 256:
                                           stage * 1024 + (q4 + 1) * 256]),
                    writes=[wstg], inc=16)
                emit("vector", lambda e, q4=q4: e.tensor_copy(
                    wbf[:, :, q4 * 256:(q4 + 1) * 256], wstg[:]),
                    reads=[wstg], writes=[wbf])
            if stage < 2:
                for kc in range(8):
                    emit("vector", lambda e, kc=kc, stage=stage: e.tensor_scalar(
                        fmt[:], fxn[:, kc, 1:513], mub[:, kc, stage:stage + 1],
                        None, ALU.mult), reads=[fxn, mub], writes=[fmt])
                    emit("vector", lambda e, kc=kc, stage=stage:
                         e.scalar_tensor_tensor(
                             fmx[:, kc], fxn[:, kc, 0:512],
                             mu1b[:, kc, stage:stage + 1], fmt[:],
                             ALU.mult, ALU.add),
                         reads=[fxn, mu1b, fmt], writes=[fmx])
                rhs = fmx
            else:
                rhs = kab
            for mi in range(8):
                bk = pb[mi % 2]
                for kc in range(8):
                    emit("tensor", lambda e, kc=kc, mi=mi, bk=bk, rhs=rhs:
                         e.matmul(bk[:, :], wbf[:, kc, mi * 128:(mi + 1) * 128],
                                  rhs[:, kc], start=(kc == 0), stop=(kc == 7)),
                         reads=[wbf, rhs], writes=[bk])
                if stage == 0:
                    emit("scalar", lambda e, bk=bk: e.activation(
                        ebuf[:], bk[:, :], AF.Exp, scale=-1.0),
                        reads=[bk], writes=[ebuf])
                    emit("vector", lambda e: e.tensor_scalar(
                        tbb[:], ebuf[:], 1.0, None, ALU.add),
                        reads=[ebuf], writes=[tbb])
                    emit("vector", lambda e: e.reciprocal(
                        ftmp[:], tbb[:]), reads=[tbb], writes=[ftmp])
                    emit("vector", lambda e, mi=mi: e.tensor_copy(
                        rfb[:, mi], ftmp[:]), reads=[ftmp], writes=[rfb])
                elif stage == 1:
                    emit("vector", lambda e, bk=bk: e.tensor_scalar(
                        ftmp[:], bk[:, :], 0.0, None, ALU.max),
                        reads=[bk], writes=[ftmp])
                    emit("scalar", lambda e, mi=mi: e.activation(
                        kab[:, mi], ftmp[:], AF.Square),
                        reads=[ftmp], writes=[kab])
                else:
                    emit("vector", lambda e, mi=mi, bk=bk: e.scalar_tensor_tensor(
                        ftmp[:], bk[:, :], 1.0, rfb[:, mi], ALU.mult, ALU.mult),
                        reads=[bk, rfb], writes=[ftmp])
                    emit("vector", lambda e, mi=mi: e.tensor_tensor(
                        fresp[mi % 2][:], ftmp[:], fxb[:, mi, 1:513], ALU.add),
                        reads=[ftmp, fxb], writes=[fresp[mi % 2]])
                    emit("gpsimd", lambda e, base=base, mi=mi: e.dma_start(
                        out=dresv[:, mi, base:base + 512], in_=fresp[mi % 2][:]),
                        reads=[fresp[mi % 2]], writes=[dresid], sem="dg",
                        inc=16, queue="gpsq")

    XL(0)
    P1(0)
    for t in range(NT):
        P23(t, nxt=t + 1 if t + 1 < NT else None)

    emit("gpsimd", lambda e: e.collective_compute(
        "AllGather", ALU.bypass, replica_groups=[[0, 1, 2, 3], [4, 5, 6, 7]],
        ins=[agin.ap().opt()], outs=[agout.ap().opt()]),
        reads=[agin], writes=[agout], sem="cc", inc=1, queue="gpsq")

    fence()
    sctx.close()
    wbf = SB("wbf", [128, 8, 1024], BF16)
    wotb = SB("wotb", [128, 8, 1024], BF16)
    fxb = SB("fxb", [128, 8, 513], F32)
    fxn = SB("fxn", [128, 8, 513], BF16)
    rfb = SB("rfb", [128, 8, 512], BF16)
    kab = SB("kab", [128, 8, 512], BF16)
    ftmp = SB("ftmp", [128, 512], F32)
    fmx = xnb
    fresp = t1b
    otile = [xtile[0][:, :, 0:256].bitcast(BF16),
             xtile[0][:, :, 256:512].bitcast(BF16)]
    _resb = xtile[1][:]
    resb = [_resb, _resb]
    _outsb = SB("outsb", [128, 8, 512], F32)[:]
    outsb = [_outsb, _outsb]


    for q4 in range(4):
        emit("sync", lambda e, q4=q4: e.dma_start(
            out=wstg[:], in_=woutv[:, :, q4 * 256:(q4 + 1) * 256]),
            writes=[wstg], inc=16)
        emit("vector", lambda e, q4=q4: e.tensor_copy(
            wotb[:, :, q4 * 256:(q4 + 1) * 256], wstg[:]),
            reads=[wstg], writes=[wotb])

    def OLOAD(ot):
        def oload(e, ot=ot):
            if rank_sv[0] is None:
                rank_sv[0] = e.snap(ridx_reg)
            return e.dma_start(
                out=otile[ot % 2][:],
                in_=agoutv.transpose([1, 0, 2])[:, :, bass.ds(rank_sv[0] * SLICE + ot * 512, 512)])
        emit("gpsimd", oload, reads=[agout], writes=[otile[ot % 2]],
             sem="dg", inc=16, queue="gpsq")
        emit("sync", lambda e, ot=ot: e.dma_start(
            out=resb[ot % 2][:], in_=dresv[:, :, ot * 512:(ot + 1) * 512]),
            reads=[dresid], writes=[resb[ot % 2]], inc=16)

    def OCOMP(ot):
        for mi in range(8):
            bk = pb[2 + mi % 2]
            for kc in range(8):
                emit("tensor", lambda e, kc=kc, mi=mi, ot=ot, bk=bk: e.matmul(
                    bk[:, :], wotb[:, kc, mi * 128:(mi + 1) * 128],
                    otile[ot % 2][:, kc], start=(kc == 0), stop=(kc == 7)),
                    reads=[wotb, otile[ot % 2]], writes=[bk])
            emit("vector", lambda e, mi=mi, ot=ot, bk=bk: e.scalar_tensor_tensor(
                outsb[ot % 2][:, mi], bk[:, :], 1.0, resb[ot % 2][:, mi],
                ALU.mult, ALU.add), reads=[bk, resb[ot % 2]],
                writes=[outsb[ot % 2]])
        emit("gpsimd", lambda e, ot=ot: e.dma_start(
            out=outpv[:, :, ot * 512:(ot + 1) * 512], in_=outsb[ot % 2][:]),
            reads=[outsb[ot % 2]], writes=[outp], sem="dg", inc=16,
            queue="gpsq")


    FFN(0)
    OLOAD(0)
    for fp in range(1, 4):
        FFN(fp)
        OCOMP(fp - 1)
        OLOAD(fp)
    OCOMP(3)

    fin_dg = cnt["dg"]
    fin_dx = cnt["dx"]
    steps["gpsimd"].append(lambda e: e.wait_ge(sems["dg"], fin_dg))
    steps["gpsimd"].append(lambda e: e.wait_ge(sems["dx"], fin_dx))

    with ctx:
        with nc.Block() as block:
            @block.sync
            def _(e):
                for fn in steps["sync"]:
                    fn(e)

            @block.tensor
            def _(e):
                for fn in steps["tensor"]:
                    fn(e)

            @block.vector
            def _(e):
                for fn in steps["vector"]:
                    fn(e)

            @block.scalar
            def _(e):
                for fn in steps["scalar"]:
                    fn(e)

            @block.gpsimd
            def _(e):
                for fn in steps["gpsimd"]:
                    fn(e)
    return nc


dbg_taps_emit = None


def _host_inputs(inputs):
    x = np.asarray(inputs["L0_attn_x"], np.float32)
    Wqkv = np.asarray(inputs["L0_attn_Wqkv"], np.float32)
    Wg = np.asarray(inputs["L0_attn_Wg"], np.float32)
    bg = np.asarray(inputs["L0_attn_bg"], np.float32)
    Wb = np.asarray(inputs["L0_attn_Wb"], np.float32)
    bb = np.asarray(inputs["L0_attn_bb"], np.float32)
    Wout = np.asarray(inputs["L0_attn_Wout"], np.float32)
    W_r = np.asarray(inputs["L0_ffn_W_r"], np.float32)
    W_k = np.asarray(inputs["L0_ffn_W_k"], np.float32)
    W_v = np.asarray(inputs["L0_ffn_W_v"], np.float32)
    mu_r = np.asarray(inputs["L0_ffn_mu_r"], np.float32)
    mu_k = np.asarray(inputs["L0_ffn_mu_k"], np.float32)

    jj, ii = np.meshgrid(np.arange(128), np.arange(128), indexing="ij")
    maskS = np.where(jj < ii, 0.0, NEG).astype(np.float32)
    maskI = np.where(jj <= ii, 0.0, NEG).astype(np.float32)
    sel0 = np.zeros((128, 128), np.float32)
    sel0[np.arange(128) % 32 == 0, :] = 1.0
    sel1 = np.zeros((128, 128), np.float32)
    sel1[np.arange(128) % 32 == 1, :] = 1.0
    halfF = np.zeros((128, 128), np.float32)
    halfF[np.arange(128) % 32 == 0, 0:64] = 1.0
    halfF[np.arange(128) % 32 == 1, 64:128] = 1.0
    css = np.zeros((128, 4, 128), np.float32)
    for part in range(4):
        c0 = [0, 32, 64, 96][part]
        css[0:64, part, c0] = 1.0
        css[64:128, part, c0 + 1] = 1.0
    ident = np.eye(128, dtype=np.float32)
    mu = np.stack([mu_r.reshape(8, 128).T, mu_k.reshape(8, 128).T], axis=-1)
    wffn = np.hstack([W_r.T, W_k.T, W_v.T]).astype(np.float32)
    wout_t = np.ascontiguousarray(Wout.T)

    ins = []
    for c in range(NCORE):
        b, s = c // 4, c % 4
        xslice = np.empty((D, SLICE + 1), np.float32)
        xslice[:, 1:] = x[b, s * SLICE:(s + 1) * SLICE].T
        xslice[:, 0] = x[b, s * SLICE - 1] if s > 0 else 0.0
        h0 = 4 * s
        wp1c = np.zeros((D, 640), np.float32)
        wp1c[:, 0:128] = Wqkv[h0 * 64:(h0 + 2) * 64].T
        wp1c[:, 128:256] = Wqkv[(h0 + 2) * 64:(h0 + 4) * 64].T
        wp1c[:, 256:384] = Wqkv[D + h0 * 64:D + (h0 + 2) * 64].T
        wp1c[:, 384:512] = Wqkv[D + (h0 + 2) * 64:D + (h0 + 4) * 64].T
        for j in range(2):
            wp1c[:, 512 + j] = Wg[h0 + j]
            wp1c[:, 512 + 32 + j] = Wg[h0 + 2 + j]
            wp1c[:, 512 + 64 + j] = Wb[h0 + j]
            wp1c[:, 512 + 96 + j] = Wb[h0 + 2 + j]
        wtmc = np.hstack([Wqkv[D + h0 * 64:D + h0 * 64 + 256].T,
                          Wqkv[2 * D + h0 * 64:2 * D + h0 * 64 + 256].T])
        gb = np.zeros((98, 1), np.float32)
        for j in range(2):
            gb[j, 0] = bg[h0 + j]
            gb[32 + j, 0] = bg[h0 + 2 + j]
            gb[64 + j, 0] = bb[h0 + j]
            gb[96 + j, 0] = bb[h0 + 2 + j]
        ins.append(dict(
            xt=np.ascontiguousarray(x[b].T), xsl=xslice, wp1=wp1c,
            wtm=np.ascontiguousarray(wtmc), wffn=wffn, wout=wout_t,
            mu=np.ascontiguousarray(mu), gb=gb,
            ridx=np.array([[s]], np.int32),
            c_maskS=maskS, c_maskI=maskI, c_sel0=sel0, c_sel1=sel1,
            c_maskSR=np.ascontiguousarray(maskS.T),
            c_halfF=halfF, c_halfB=halfF.astype(BFNP),
            c_idB=ident.astype(BFNP), c_idF=ident,
            c_ss=css.astype(BFNP),
            c_o1b=np.ones((128, 1), BFNP),
            c_o1f=np.ones((1, 128), np.float32),
            c_eps=np.stack([np.full(128, 1e-5, np.float32),
                            np.zeros(128, np.float32)], axis=1),
        ))
    return ins


def kernel(**inputs):
    ins = _host_inputs(inputs)
    nc = _build()
    res = run_bass_kernel_spmd(nc, ins, core_ids=list(range(NCORE)))
    out = np.empty((B, T, D), np.float32)
    for c in range(NCORE):
        b, s = c // 4, c % 4
        out[b, s * SLICE:(s + 1) * SLICE, :] = np.asarray(
            res.results[c]["outp"], np.float32).T
    return out


# version 19 (final)
# speedup vs baseline: 1.1365x; 1.0290x over previous
import numpy as np
import ml_dtypes
from contextlib import ExitStack

import concourse.bass as bass
import concourse.mybir as mybir
from concourse.bass_utils import run_bass_kernel_spmd

F32 = mybir.dt.float32
BF16 = mybir.dt.bfloat16
I32 = mybir.dt.int32
AF = mybir.ActivationFunctionType
ALU = mybir.AluOpType
BFNP = ml_dtypes.bfloat16

B, T, D, H, DH = 2, 8192, 1024, 16, 64
NCORE = 8
TILE = 512
NT = T // TILE
CH = 128
CPT = TILE // CH
SLICE = T // 4
NEG = -1.0e6


def _build(dbg_taps=None):
    nc = bass.Bass("TRN2", target_bir_lowering=False, debug=False)

    xt = nc.dram_tensor("xt", [D, T], F32, kind="ExternalInput").ap()
    xsl = nc.dram_tensor("xsl", [D, SLICE + 1], F32, kind="ExternalInput").ap()
    wp1 = nc.dram_tensor("wp1", [D, 640], F32, kind="ExternalInput").ap()
    wtm = nc.dram_tensor("wtm", [D, 512], F32, kind="ExternalInput").ap()
    wffn = nc.dram_tensor("wffn", [D, 3072], F32, kind="ExternalInput").ap()
    wout = nc.dram_tensor("wout", [D, D], F32, kind="ExternalInput").ap()
    muin = nc.dram_tensor("mu", [128, 8, 2], F32, kind="ExternalInput").ap()
    gbin = nc.dram_tensor("gb", [98, 1], F32, kind="ExternalInput").ap()
    ridx = nc.dram_tensor("ridx", [1, 1], I32, kind="ExternalInput").ap()
    c_in = {}
    for nm, sh, dt in (("c_maskS", [128, 128], F32), ("c_maskI", [128, 128], F32),
                       ("c_maskSR", [128, 128], F32),
                       ("c_sel0", [128, 128], F32), ("c_sel1", [128, 128], F32),
                       ("c_halfF", [128, 128], F32), ("c_halfB", [128, 128], BF16),
                       ("c_idB", [128, 128], BF16), ("c_idF", [128, 128], F32),
                       ("c_ss", [128, 4, 128], BF16), ("c_o1b", [128, 1], BF16),
                       ("c_o1f", [1, 128], F32), ("c_eps", [128, 2], F32)):
        c_in[nm] = nc.dram_tensor(nm, sh, dt, kind="ExternalInput").ap()

    outp = nc.dram_tensor("outp", [D, SLICE], F32, kind="ExternalOutput").ap()
    agin = nc.dram_tensor("agin", [256, T], BF16, kind="Internal")
    agout = nc.dram_tensor("agout", [4, 256, T], BF16, kind="Internal")
    dresid = nc.dram_tensor("dresid", [D, SLICE], F32, kind="Internal")
    dbg_out = {}
    if dbg_taps:
        for nm, sh, dt in dbg_taps:
            dbg_out[nm] = nc.dram_tensor(nm, sh, dt, kind="ExternalOutput").ap()

    ctx = ExitStack()
    SB = lambda n, sh, dt: ctx.enter_context(nc.sbuf_tensor(n, sh, dt))
    PS_ = lambda n: ctx.enter_context(nc.psum_tensor(n, [128, 512], F32))

    xtile = [SB(f"xtile{i}", [128, 8, TILE], F32) for i in range(2)]
    ybuf = [SB(f"ybuf{i}", [128, TILE], BF16) for i in range(2)]
    y2buf = [SB(f"y2buf{i}", [128, TILE], BF16) for i in range(2)]
    xnb = SB("xnb", [128, 8, TILE], BF16)
    t1b = [SB(f"t1b{i}", [128, TILE], F32) for i in range(2)]
    rrep = SB("rrep", [128, TILE], F32)
    mrep = SB("mrep", [128, TILE], F32)
    rowq = [SB(f"rowq{i}", [1, TILE], F32) for i in range(5)]
    mrrow = SB("mrrow", [1, TILE + 1], F32)
    hcol = [SB(f"hcol{i}", [128, 1], F32) for i in range(2)]
    wstg = SB("wstg", [128, 8, 256], F32)
    ebuf = SB("ebuf", [128, TILE], F32)
    tbb = SB("tbb", [128, TILE], F32)
    sqb = SB("sqb", [128, TILE], BF16)
    fmt = sqb
    mub = SB("mub", [128, 8, 2], F32)
    mu1b = SB("mu1b", [128, 8, 2], F32)
    kc_ = {}
    for nm, sh, dt in (("kmask2", [128, 256], F32),
                       ("kmaskSR", [128, 128], F32),
                       ("ksel0", [128, 128], F32), ("ksel1", [128, 128], F32),
                       ("khalfF", [128, 128], F32), ("khalfB", [128, 128], BF16),
                       ("kidB", [128, 128], BF16), ("kidF", [128, 128], F32),
                       ("kss", [128, 4, 128], BF16), ("ko1b", [128, 1], BF16),
                       ("ko1f", [1, 128], F32), ("keps", [128, 2], F32)):
        kc_[nm] = SB(nm, sh, dt)
    ksel = [kc_["ksel0"], kc_["ksel1"]]
    kgb = SB("kgb", [128, 1], F32)
    ngb = SB("ngb", [128, 1], F32)
    sridx = SB("sridx", [1, 1], I32)

    pb = [PS_(f"pb{i}") for i in range(8)]

    sctx = ExitStack()
    SBs = lambda n, sh, dt: sctx.enter_context(nc.sbuf_tensor(n, sh, dt))
    wp1b = SBs("wp1b", [128, 8, 640], BF16)
    wtmb = SBs("wtmb", [128, 8, 512], BF16)
    qk01 = [SBs(f"qk01s{i}", [128, CPT, 256], BF16) for i in range(2)]
    qk23 = [SBs(f"qk23s{i}", [128, CPT, 256], BF16) for i in range(2)]
    ktmv = [SBs(f"ktmvs{i}", [128, CPT, 512], BF16) for i in range(2)]
    abb = SBs("abb", [128, TILE], F32)
    lntb = SBs("lntb", [128, TILE], F32)
    sscb = SBs("sscb", [128, TILE], F32)
    t1r = SBs("t1r", [128, TILE], F32)
    t2r = SBs("t2r", [128, TILE], F32)
    t3r = sscb
    lnssb = lntb
    _gneg = SBs("gnegs", [128, TILE], F32)
    _colsrc = SBs("colsrcs", [128, TILE], F32)
    gneg = [_gneg, SBs("gnegs1", [128, TILE], F32)]
    rows3 = [SBs(f"rows3_{i}", [128, 3 * TILE], F32) for i in range(2)]
    r3A = lambda i, lc: rows3[i][:, lc * 384:lc * 384 + 128]
    r3M = lambda i, lc: rows3[i][:, lc * 384 + 128:lc * 384 + 256]
    r3C = lambda i, lc: rows3[i][:, lc * 384 + 256:lc * 384 + 384]
    colsrc = [_colsrc, SBs("colsrcs1", [128, TILE], F32)]
    escr = SBs("escr", [128, 128], F32)
    expAq = [SBs(f"expAq{i}", [128, CPT, 128], BF16) for i in range(2)]
    gdif = SBs("gdif", [128, 1], F32)
    ngend = SBs("ngend", [128, 1], F32)
    colsA = SBs("colsA", [128, 128], F32)
    colsBq = [SBs(f"colsBq{i}", [128, CPT, 128], F32) for i in range(2)]
    nmcols = SBs("nmcols", [128, 128], F32)
    t3b = [SBs(f"t3b{i}", [128, 384], F32) for i in range(2)]
    dab3 = [SBs(f"dab3{i}", [128, 384], BF16) for i in range(2)]
    attnq = [[SBs(f"attnq{i}_{s}", [128, CPT, 128], BF16) for s in range(4)]
             for i in range(2)]
    _mnegq = [SBs(f"mnegq_{s}", [128, CPT, 128], BF16) for s in range(4)]
    mnegq = [_mnegq, _mnegq]
    _mrawq = [SBs(f"mrawq_{s}", [128, CPT, 128], BF16) for s in range(4)]
    mrawq = [_mrawq, _mrawq]
    _rhsq = [SBs(f"rhsq_{s}", [128, CPT, 128], BF16) for s in range(4)]
    rhsq = [_rhsq, _rhsq]
    xq = [SBs(f"xq{s}", [128, CPT, 128], BF16) for s in range(4)]
    pq = [[SBs(f"pq{s}_{i}", [128, CPT, 128], BF16) for i in range(2)]
          for s in range(4)]
    prawq = [SBs(f"prawq{s}", [128, CPT, 128], BF16) for s in range(4)]
    _uvw = [SBs(f"uvw_{s}", [128, CPT, 128], BF16) for s in range(4)]
    uvw = [_uvw, _uvw]
    wfm = [SBs(f"wfm{p}", [128, CPT, 128], BF16) for p in range(2)]
    ubuf = [SBs(f"ubuf{s}", [128, 64], BF16) for s in range(4)]
    utb = [SBs(f"utb{s}", [128, 64], BF16) for s in range(4)]
    qgb = [SBs(f"qgb{p}", [128, 128], BF16) for p in range(2)]
    gCcq = [SBs(f"gCcq{i}", [128, CPT], F32) for i in range(2)]
    gCcol = [SBs(f"gCcol{p}", [128, 1], F32) for p in range(2)]
    ps32 = [SBs(f"ps32{p}", [128, 64], F32) for p in range(2)]
    psb = [SBs(f"psb{p}", [128, 64], BF16) for p in range(2)]
    _oseg = SBs("osegs", [128, 2, TILE], BF16)
    oseg = [_oseg, _oseg]

    sems = {}
    for sname in ("dx", "dg", "cc", "t", "v", "s", "g"):
        sems[sname] = ctx.enter_context(nc.semaphore(sname))

    steps = {e: [] for e in ("sync", "tensor", "vector", "scalar", "gpsimd")}
    cnt = {k: 0 for k in sems}
    RES = {}
    HWM = {}

    def _res(obj):
        key = id(obj)
        if key not in RES:
            RES[key] = {"w": None, "r": {}}
        return RES[key]

    deferq = []
    defer_on = [False]

    def emit(q, fn, reads=(), writes=(), sem=None, inc=1, queue=None):
        if defer_on[0]:
            deferq.append((q, fn, reads, writes, sem, inc, queue))
            return
        if sem is None:
            sem = {"sync": "dx", "tensor": "t", "vector": "v",
                   "scalar": "s", "gpsimd": "g"}[q]
        queue = queue or (q if sem not in ("dx", "dg") else "q_" + sem)
        waits = {}

        def addw(ent):
            if ent is None:
                return
            eq, es, ec = ent
            if eq == queue:
                return
            if waits.get(es, 0) < ec:
                waits[es] = ec
        for b in reads:
            addw(_res(b)["w"])
        for b in writes:
            r = _res(b)
            addw(r["w"])
            for eq, (es, ec) in r["r"].items():
                addw((eq, es, ec))
        for s2, v2 in sorted(waits.items()):
            if HWM.get((q, s2), -1) >= v2:
                continue
            HWM[(q, s2)] = v2
            steps[q].append(lambda e, s=s2, v=v2: e.wait_ge(sems[s], v))
        cnt[sem] += inc
        me = (queue, sem, cnt[sem])
        steps[q].append(lambda e, f=fn, s=sem, k=inc: f(e).then_inc(sems[s], k))
        for b in reads:
            _res(b)["r"][queue] = (sem, cnt[sem])
        for b in writes:
            r = _res(b)
            r["w"] = me
            r["r"] = {}
        return cnt[sem]

    def drain(n=1 << 30):
        for _ in range(min(n, len(deferq))):
            q, fn, reads, writes, sem, inc, queue = deferq.pop(0)
            emit(q, fn, reads, writes, sem, inc, queue)

    def fence():
        vals = dict(cnt)
        for q2 in ("sync", "tensor", "vector", "scalar", "gpsimd"):
            for sname, v in vals.items():
                if v > 0 and HWM.get((q2, sname), -1) < v:
                    HWM[(q2, sname)] = v
                    steps[q2].append(
                        lambda e, s=sname, vv=v: e.wait_ge(sems[s], vv))
        RES.clear()

    xtv = xt.rearrange("(c p) t -> p c t", p=128)
    xslv = xsl.rearrange("(c p) t -> p c t", p=128)
    wp1v = wp1.rearrange("(c p) m -> p c m", p=128)
    wtmv = wtm.rearrange("(c p) m -> p c m", p=128)
    wffnv = wffn.rearrange("(c p) m -> p c m", p=128)
    woutv = wout.rearrange("(c p) m -> p c m", p=128)
    aginv = agin.ap().rearrange("(r p) t -> p r t", p=128)
    agoutv = agout.ap().rearrange("r (h p) t -> (r h) p t", p=128)
    dresv = dresid.ap().rearrange("(c p) t -> p c t", p=128)
    outpv = outp.rearrange("(c p) t -> p c t", p=128)

    DMA = "dma_marker"

    emit("sync", lambda e: e.dma_start(out=kc_["kmask2"][:, 0:128],
         in_=c_in["c_maskI"]), writes=[kc_["kmask2"]], inc=16)
    emit("sync", lambda e: e.dma_start(out=kc_["kmask2"][:, 128:256],
         in_=c_in["c_maskS"]), writes=[kc_["kmask2"]], inc=16)
    for nm, dst in (("c_maskSR", kc_["kmaskSR"]),
                    ("c_eps", kc_["keps"]),
                    ("c_sel0", ksel[0]), ("c_sel1", ksel[1]),
                    ("c_halfF", kc_["khalfF"]), ("c_halfB", kc_["khalfB"]),
                    ("c_idB", kc_["kidB"]), ("c_idF", kc_["kidF"]),
                    ("c_ss", kc_["kss"]), ("c_o1b", kc_["ko1b"]),
                    ("c_o1f", kc_["ko1f"])):
        emit("sync", lambda e, d=dst, s=c_in[nm]: e.dma_start(out=d[:], in_=s),
             writes=[dst], inc=16)
    emit("sync", lambda e: e.dma_start(out=mub[:], in_=muin), writes=[mub], inc=16)
    emit("sync", lambda e: e.dma_start(out=sridx[:], in_=ridx), writes=[sridx], inc=16)
    emit("vector", lambda e: e.memset(kgb[:], 0.0), writes=[kgb])
    emit("sync", lambda e: e.dma_start(out=kgb[0:98, :], in_=gbin),
         writes=[kgb], inc=16)
    for lo, hi in ((0, 256), (256, 512), (512, 640)):
        emit("sync", lambda e, lo=lo, hi=hi: e.dma_start(
            out=wstg[:, :, 0:hi - lo], in_=wp1v[:, :, lo:hi]),
            writes=[wstg], inc=16)
        emit("vector", lambda e, lo=lo, hi=hi: e.tensor_copy(
            wp1b[:, :, lo:hi], wstg[:, :, 0:hi - lo]),
            reads=[wstg], writes=[wp1b])
    for lo in (0, 256):
        emit("sync", lambda e, lo=lo: e.dma_start(
            out=wstg[:], in_=wtmv[:, :, lo:lo + 256]), writes=[wstg], inc=16)
        emit("vector", lambda e, lo=lo: e.tensor_copy(
            wtmb[:, :, lo:lo + 256], wstg[:]), reads=[wstg], writes=[wtmb])
    emit("vector", lambda e: e.tensor_scalar(ngb[:], kgb[:], -1.0, None, ALU.mult),
         reads=[kgb], writes=[ngb])
    for p in range(2):
        emit("vector", lambda e, p=p: e.memset(ps32[p][:], 0.0), writes=[ps32[p]])
        emit("vector", lambda e, p=p: e.memset(psb[p][:], 0.0), writes=[psb[p]])
    emit("vector", lambda e: e.tensor_scalar(
        mu1b[:], mub[:], -1.0, 1.0, ALU.mult, ALU.add),
        reads=[mub], writes=[mu1b])

    ridx_reg = nc.gpsimd.alloc_register("ridxreg")
    rank_sv = [None]
    emit("gpsimd", lambda e: e.reg_load(ridx_reg, sridx[0:1, 0:1]),
         reads=[sridx], writes=[], sem="g")

    def XL(t):
        xb = xtile[t % 2]
        emit("sync", lambda e, t=t, xb=xb: e.dma_start(
            out=xb[:], in_=xtv[:, :, t * TILE:(t + 1) * TILE]),
            writes=[xb], inc=16)

    def P1(t):
        pt = t % 2
        xb = xtile[pt]
        for kc in range(8):
            emit("vector", lambda e, kc=kc: e.tensor_copy(
                ybuf[kc % 2][:], xb[:, kc]), reads=[xb], writes=[ybuf[kc % 2]])
            emit("scalar", lambda e, kc=kc: e.activation(
                y2buf[kc % 2][:], xb[:, kc], AF.Square),
                reads=[xb], writes=[y2buf[kc % 2]])
            emit("tensor", lambda e, kc=kc: e.matmul(
                pb[2][0:1, :], kc_["ko1b"][:], ybuf[kc % 2][:],
                start=(kc == 0), stop=(kc == 7)),
                reads=[ybuf[kc % 2], kc_["ko1b"]], writes=[pb[2]])
            emit("tensor", lambda e, kc=kc: e.matmul(
                pb[3][0:1, :], kc_["ko1b"][:], y2buf[kc % 2][:],
                start=(kc == 0), stop=(kc == 7)),
                reads=[y2buf[kc % 2], kc_["ko1b"]], writes=[pb[3]])
        emit("vector", lambda e: e.tensor_scalar(
            rowq[0][:], pb[2][0:1, :], 1.0 / D, None, ALU.mult),
            reads=[pb[2]], writes=[rowq[0]])
        emit("vector", lambda e: e.tensor_tensor(
            rowq[1][:], rowq[0][:], rowq[0][:], ALU.mult),
            reads=[rowq[0]], writes=[rowq[1]])
        emit("vector", lambda e: e.scalar_tensor_tensor(
            rowq[2][:], pb[3][0:1, :], 1.0 / D, rowq[1][:], ALU.mult,
            ALU.subtract), reads=[pb[3], rowq[1]], writes=[rowq[2]])
        emit("scalar", lambda e: e.activation(rowq[3][:], rowq[2][:], AF.Ln,
                                              bias=kc_["keps"][0:1, 0:1]),
             reads=[rowq[2], kc_["keps"]], writes=[rowq[3]])
        emit("scalar", lambda e: e.activation(rowq[4][:], rowq[3][:], AF.Exp,
                                              scale=-0.5),
             reads=[rowq[3]], writes=[rowq[4]])
        emit("vector", lambda e: e.tensor_tensor(
            mrrow[:, 0:TILE], rowq[0][:], rowq[4][:], ALU.mult),
            reads=[rowq[0], rowq[4]], writes=[mrrow])
        emit("tensor", lambda e: e.matmul(pb[2][:, :], kc_["ko1f"][:],
                                          rowq[4][:], start=True, stop=True),
             reads=[rowq[4], kc_["ko1f"]], writes=[pb[2]])
        emit("tensor", lambda e: e.matmul(pb[3][:, :], kc_["ko1f"][:],
                                          mrrow[:, 0:TILE], start=True, stop=True),
             reads=[mrrow, kc_["ko1f"]], writes=[pb[3]])
        emit("scalar", lambda e: e.copy(rrep[:], pb[2][:, :]),
             reads=[pb[2]], writes=[rrep])
        emit("scalar", lambda e: e.copy(mrep[:], pb[3][:, :]),
             reads=[pb[3]], writes=[mrep])
        for kc in range(8):
            emit("gpsimd", lambda e, kc=kc: e.tensor_tensor(
                t1b[kc % 2][:], xb[:, kc], rrep[:], ALU.mult),
                reads=[xb, rrep], writes=[t1b[kc % 2]])
            emit("vector", lambda e, kc=kc: e.scalar_tensor_tensor(
                xnb[:, kc], mrep[:], -1.0, t1b[kc % 2][:], ALU.mult, ALU.add),
                reads=[mrep, t1b[kc % 2]], writes=[xnb])
        for mi in range(5):
            bk = pb[mi % 2]
            for kc in range(8):
                emit("tensor", lambda e, kc=kc, mi=mi, bk=bk: e.matmul(
                    bk[:, :], wp1b[:, kc, mi * 128:(mi + 1) * 128], xnb[:, kc],
                    start=(kc == 0), stop=(kc == 7)),
                    reads=[wp1b, xnb], writes=[bk])
            if mi < 4:
                qkb = [qk01, qk23][mi % 2][pt]
                csl = slice(0, 128) if mi < 2 else slice(128, 256)
                emit("scalar", lambda e, qkb=qkb, csl=csl, bk=bk: e.copy(
                    qkb[:, :, csl], bk[:, :].rearrange("p (c x) -> p c x", x=128)),
                    reads=[bk], writes=[qkb])
            else:
                emit("scalar", lambda e, bk=bk: e.activation(
                    ebuf[:], bk[:, :], AF.Exp, bias=ngb[:], scale=-1.0),
                    reads=[bk, ngb], writes=[ebuf])
                emit("vector", lambda e: e.tensor_scalar(
                    tbb[:], ebuf[:], 1.0, None, ALU.add),
                    reads=[ebuf], writes=[tbb])
                emit("vector", lambda e: e.reciprocal(abb[:], tbb[:]),
                     reads=[tbb], writes=[abb])
                emit("scalar", lambda e: e.activation(lntb[:], tbb[:], AF.Ln),
                     reads=[tbb], writes=[lntb])
                emit("vector", lambda e, pt=pt: e.tensor_tensor_scan(
                    gneg[pt][:], lntb[:], lntb[:], 0.0, ALU.add, ALU.bypass),
                    reads=[lntb], writes=[gneg[pt]])
                emit("vector", lambda e: e.tensor_copy(t1r[0:2, :], lntb[64:66, :]),
                     reads=[lntb], writes=[t1r])
                emit("vector", lambda e: e.tensor_copy(t1r[32:34, :], lntb[96:98, :]),
                     reads=[lntb], writes=[t1r])
        for tg in range(4):
            bk = pb[2 + tg % 2]
            for kc in range(8):
                emit("tensor", lambda e, kc=kc, tg=tg, bk=bk: e.matmul(
                    bk[:, :], xnb[:, kc, tg * 128:(tg + 1) * 128],
                    wtmb[:, kc, :], start=(kc == 0), stop=(kc == 7)),
                    reads=[xnb, wtmb], writes=[bk])
            emit("scalar", lambda e, tg=tg, pt=pt, bk=bk: e.copy(
                ktmv[pt][:, tg, :], bk[:, :]), reads=[bk], writes=[ktmv[pt]])
        for part in range(4):
            qkb = [qk01, qk23][part % 2][pt]
            csl = slice(0, 128) if part < 2 else slice(128, 256)
            emit("scalar", lambda e, qkb=qkb, csl=csl: e.activation(
                sqb[:].rearrange("p (c x) -> p c x", x=128), qkb[:, :, csl],
                AF.Square), reads=[qkb], writes=[sqb])
            emit("tensor", lambda e, part=part: e.matmul(
                pb[0][:, :], kc_["kss"][:, part, :], sqb[:],
                start=(part == 0), stop=(part == 3)),
                reads=[sqb, kc_["kss"]], writes=[pb[0]])
        emit("vector", lambda e: e.tensor_scalar(
            sscb[:], pb[0][:, :], 1e-24, None, ALU.max),
            reads=[pb[0]], writes=[sscb])
        emit("scalar", lambda e: e.activation(lnssb[:], sscb[:], AF.Ln),
             reads=[sscb], writes=[lnssb])
        emit("vector", lambda e: e.tensor_scalar(
            lnssb[:], lnssb[:], 0.5, None, ALU.mult),
            reads=[lnssb], writes=[lnssb])
        emit("vector", lambda e: e.tensor_copy(t2r[0:2, :], lnssb[64:66, :]),
             reads=[lnssb], writes=[t2r])
        emit("vector", lambda e: e.tensor_copy(t2r[32:34, :], lnssb[96:98, :]),
             reads=[lnssb], writes=[t2r])
        emit("gpsimd", lambda e: e.tensor_tensor(
            t3r[:], t2r[:], t1r[:], ALU.add),
            reads=[t2r, t1r], writes=[t3r])
        for lc4 in range(CPT):
            cs4 = slice(lc4 * 128, (lc4 + 1) * 128)
            emit("gpsimd", lambda e, pt=pt, lc4=lc4, cs4=cs4: e.tensor_tensor(
                r3A(pt, lc4), lnssb[:, cs4], gneg[pt][:, cs4], ALU.add),
                reads=[lnssb, gneg[pt]], writes=[rows3[pt]])
            emit("gpsimd", lambda e, pt=pt, lc4=lc4, cs4=cs4: e.tensor_tensor(
                r3M(pt, lc4), t3r[:, cs4], gneg[pt][:, cs4], ALU.add),
                reads=[t3r, gneg[pt]], writes=[rows3[pt]])
            emit("gpsimd", lambda e, pt=pt, lc4=lc4, cs4=cs4: e.tensor_tensor(
                r3C(pt, lc4), t2r[:, cs4], gneg[pt][:, cs4], ALU.subtract),
                reads=[t2r, gneg[pt]], writes=[rows3[pt]])
        emit("gpsimd", lambda e, pt=pt: e.tensor_tensor(
            colsrc[pt][:], gneg[pt][:], t2r[:], ALU.subtract),
            reads=[t2r, gneg[pt]], writes=[colsrc[pt]])
        emit("vector", lambda e, pt=pt: e.tensor_copy(
            colsrc[pt][64:66, :], abb[64:66, :]),
            reads=[abb], writes=[colsrc[pt]])
        emit("vector", lambda e, pt=pt: e.tensor_copy(
            colsrc[pt][96:98, :], abb[96:98, :]),
            reads=[abb], writes=[colsrc[pt]])

    def P23(t, nxt=None):
        if nxt is not None:
            XL(nxt)
        pt = t % 2
        for lc in range(CPT):
            csl = slice(lc * 128, (lc + 1) * 128)
            if lc == 0:
                emit("vector", lambda e, pt=pt, lc=lc: e.tensor_scalar(
                    gdif[:], gneg[pt][:, (lc + 1) * 128 - 1:(lc + 1) * 128],
                    -1.0, None, ALU.mult), reads=[gneg[pt]], writes=[gdif])
            else:
                emit("vector", lambda e, pt=pt, lc=lc: e.tensor_tensor(
                    gdif[:], gneg[pt][:, lc * 128 - 1:lc * 128],
                    gneg[pt][:, (lc + 1) * 128 - 1:(lc + 1) * 128],
                    ALU.subtract), reads=[gneg[pt]], writes=[gdif])
            emit("vector", lambda e, pt=pt, lc=lc: e.tensor_scalar(
                ngend[:], gneg[pt][:, (lc + 1) * 128 - 1:(lc + 1) * 128],
                -1.0, None, ALU.mult), reads=[gneg[pt]], writes=[ngend])
            if lc == 0:
                emit("scalar", lambda e, pt=pt, lc=lc: e.activation(
                    escr[0:64, :], r3M(pt, lc)[0:64, :], AF.Exp,
                    scale=-1.0), reads=[rows3[pt]], writes=[escr])
                emit("scalar", lambda e, pt=pt, lc=lc: e.activation(
                    expAq[pt][0:64, lc, :], r3A(pt, lc)[0:64, :], AF.Exp,
                    scale=-1.0), reads=[rows3[pt]], writes=[expAq[pt]])
            else:
                emit("scalar", lambda e, pt=pt, lc=lc: e.activation(
                    escr[0:64, :], r3M(pt, lc)[0:64, :], AF.Exp, scale=-1.0,
                    bias=gneg[pt][0:64, lc * 128 - 1:lc * 128]),
                    reads=[rows3[pt], gneg[pt]], writes=[escr])
                emit("scalar", lambda e, pt=pt, lc=lc: e.activation(
                    expAq[pt][0:64, lc, :], r3A(pt, lc)[0:64, :], AF.Exp,
                    scale=-1.0,
                    bias=gneg[pt][0:64, lc * 128 - 1:lc * 128]),
                    reads=[rows3[pt], gneg[pt]], writes=[expAq[pt]])
            emit("scalar", lambda e, pt=pt, csl=csl: e.activation(
                escr[64:128, :], colsrc[pt][0:64, csl],
                AF.Exp, bias=ngend[0:64, :]),
                reads=[colsrc[pt], ngend], writes=[escr])
            emit("scalar", lambda e, pt=pt, lc=lc: e.activation(
                gCcq[pt][0:64, lc:lc + 1], gdif[0:64, :], AF.Exp),
                reads=[gdif], writes=[gCcq[pt]])
            emit("tensor", lambda e, pt=pt, csl=csl: e.transpose(
                pb[4][:, 384:512], colsrc[pt][:, csl], kc_["kidF"][:]),
                reads=[colsrc[pt], kc_["kidF"]], writes=[pb[4]])
            emit("vector", lambda e: e.tensor_copy(colsA[:], pb[4][:, 384:512]),
                 reads=[pb[4]], writes=[colsA])
            emit("tensor", lambda e: e.transpose(
                pb[4][:, 384:512], escr[:], kc_["kidF"][:]),
                reads=[escr, kc_["kidF"]], writes=[pb[4]])
            emit("vector", lambda e, lc=lc: e.tensor_copy(
                colsBq[pt][:, lc, :], pb[4][:, 384:512]),
                reads=[pb[4]], writes=[colsBq[pt]])
            emit("tensor", lambda e, pt=pt, lc=lc: e.transpose(
                pb[5][:, 384:512], r3M(pt, lc), kc_["kidF"][:]),
                reads=[rows3[pt], kc_["kidF"]], writes=[pb[5]])
            emit("vector", lambda e: e.tensor_scalar(
                nmcols[:], pb[5][:, 384:512], -1.0, None, ALU.mult),
                reads=[pb[5]], writes=[nmcols])

            for s in range(4):
                p, hf = s // 2, s % 2
                bq = 32 * p
                Bh = 64 * hf
                qkb = [qk01, qk23][p][pt]
                bk = pb[2 + s % 2]
                bk2 = pb[6 + s % 2]
                emit("tensor", lambda e, qkb=qkb, lc=lc, Bh=Bh, bk=bk: e.matmul(
                    bk[:, 0:256], qkb[Bh:Bh + 64, lc, 128:256],
                    qkb[Bh:Bh + 64, lc, :], start=True, stop=True),
                    reads=[qkb], writes=[bk])
                emit("tensor", lambda e, pt=pt, bq=bq, hf=hf, bk2=bk2, lc=lc:
                     e.matmul(bk2[:, 0:384],
                              ksel[hf][bq:bq + 2, :],
                              rows3[pt][bq:bq + 2, lc * 384:(lc + 1) * 384],
                              start=True, stop=True),
                     reads=[ksel[hf], rows3[pt]], writes=[bk2])
                emit("scalar", lambda e, s=s, bq=bq, hf=hf: e.activation(
                    t3b[s % 2][:, 0:256], kc_["kmask2"][:], AF.Identity,
                    bias=colsA[:, bq + hf:bq + hf + 1]),
                    reads=[kc_["kmask2"], colsA], writes=[t3b[s % 2]])
                emit("scalar", lambda e, s=s, bq=bq, hf=hf: e.activation(
                    t3b[s % 2][:, 256:384], kc_["kmaskSR"][:], AF.Identity,
                    bias=nmcols[:, bq + hf:bq + hf + 1]),
                    reads=[kc_["kmaskSR"], nmcols], writes=[t3b[s % 2]])
                emit("vector", lambda e, s=s, bk2=bk2: e.scalar_tensor_tensor(
                    t3b[s % 2][:], bk2[:, 0:384], -1.0, t3b[s % 2][:],
                    ALU.mult, ALU.add), reads=[bk2, t3b[s % 2]],
                    writes=[t3b[s % 2]])
                emit("scalar", lambda e, s=s: e.activation(
                    dab3[s % 2][:], t3b[s % 2][:], AF.Exp),
                    reads=[t3b[s % 2]], writes=[dab3[s % 2]])
                emit("vector", lambda e, s=s, lc=lc, bk=bk: e.scalar_tensor_tensor(
                    attnq[pt][s][:, lc, :], bk[:, 0:128], 1.0,
                    dab3[s % 2][:, 0:128], ALU.mult, ALU.mult),
                    reads=[bk, dab3[s % 2]], writes=[attnq[pt][s]])
                emit("vector", lambda e, s=s, lc=lc, bk=bk: e.scalar_tensor_tensor(
                    mnegq[pt][s][:, lc, :], bk[:, 128:256], -1.0,
                    dab3[s % 2][:, 128:256], ALU.mult, ALU.mult),
                    reads=[bk, dab3[s % 2]], writes=[mnegq[pt][s]])
                emit("vector", lambda e, s=s, lc=lc, bk=bk: e.scalar_tensor_tensor(
                    mrawq[pt][s][:, lc, :], bk[:, 128:256], -1.0,
                    dab3[s % 2][:, 256:384], ALU.mult, ALU.mult),
                    reads=[bk, dab3[s % 2]], writes=[mrawq[pt][s]])
                emit("vector", lambda e, s=s, pt=pt, lc=lc, bq=bq, hf=hf:
                     e.tensor_scalar(
                         rhsq[pt][s][:, lc, 0:64],
                         ktmv[pt][:, lc, 256 + s * 64:256 + s * 64 + 64],
                         colsA[:, 64 + bq + hf:64 + bq + hf + 1], None,
                         ALU.mult),
                     reads=[ktmv[pt], colsA], writes=[rhsq[pt][s]])
                emit("vector", lambda e, s=s, pt=pt, lc=lc, bq=bq, hf=hf:
                     e.tensor_scalar(
                         rhsq[pt][s][:, lc, 64:128],
                         ktmv[pt][:, lc, s * 64:s * 64 + 64],
                         colsBq[pt][:, lc, bq + hf:bq + hf + 1], -1.0,
                         ALU.mult, ALU.mult),
                     reads=[ktmv[pt], colsBq[pt]], writes=[rhsq[pt][s]])

        if nxt is not None:
            defer_on[0] = True
            P1(nxt)
            defer_on[0] = False

        for lev in range(7):
            for s in range(4):
                bk = pb[4 + s]
                Xs = rhsq[pt][s] if lev == 0 else xq[s]
                Ps = mnegq[pt][s] if lev == 0 else pq[s][(lev - 1) % 2]
                for l4 in range(CPT):
                    emit("tensor", lambda e, bk=bk, Ps=Ps, Xs=Xs, l4=l4:
                         e.matmul(bk[:, l4 * 128:(l4 + 1) * 128],
                                  Ps[:, l4, :], Xs[:, l4, :],
                                  start=True, stop=True),
                         reads=[Ps, Xs], writes=[bk])
            for s in range(4):
                bk = pb[4 + s]
                Xs = rhsq[pt][s] if lev == 0 else xq[s]
                dest = uvw[pt][s] if lev == 6 else xq[s]
                emit("vector", lambda e, bk=bk, Xs=Xs, d=dest:
                     e.scalar_tensor_tensor(
                         d[:].rearrange("p c x -> p (c x)"), bk[:, :], 1.0,
                         Xs[:].rearrange("p c x -> p (c x)"), ALU.mult,
                         ALU.add),
                     reads=[bk, Xs], writes=[dest])
            drain(7)
            if lev < 6:
                for s in range(4):
                    bk = pb[4 + s]
                    Ps = mnegq[pt][s] if lev == 0 else pq[s][(lev - 1) % 2]
                    Rs = mrawq[pt][s] if lev == 0 else prawq[s]
                    for l4 in range(CPT):
                        emit("tensor", lambda e, bk=bk, Ps=Ps, Rs=Rs, l4=l4:
                             e.matmul(bk[:, l4 * 128:(l4 + 1) * 128],
                                      Rs[:, l4, :], Ps[:, l4, :],
                                      start=True, stop=True),
                             reads=[Ps, Rs], writes=[bk])
                drain(7)
                for s in range(4):
                    bk = pb[4 + s]
                    emit("scalar", lambda e, bk=bk, s=s, lev=lev: e.copy(
                        pq[s][lev % 2][:].rearrange("p c x -> p (c x)"),
                        bk[:, :]), reads=[bk], writes=[pq[s][lev % 2]])
                for s in range(4):
                    bk = pb[4 + s]
                    Ps = mnegq[pt][s] if lev == 0 else pq[s][(lev - 1) % 2]
                    Rs = mrawq[pt][s] if lev == 0 else prawq[s]
                    for l4 in range(CPT):
                        emit("tensor", lambda e, bk=bk, Ps=Ps, Rs=Rs, l4=l4:
                             e.matmul(bk[:, l4 * 128:(l4 + 1) * 128],
                                      Ps[:, l4, :], Rs[:, l4, :],
                                      start=True, stop=True),
                             reads=[Ps, Rs], writes=[bk])
                for s in range(4):
                    bk = pb[4 + s]
                    emit("vector", lambda e, bk=bk, s=s: e.tensor_copy(
                        prawq[s][:].rearrange("p c x -> p (c x)"),
                        bk[:, :]), reads=[bk], writes=[prawq[s]])
            drain(8)

        for lc in range(CPT):
            for p in range(2):
                bk = pb[4 + 2 * p]
                for hf in range(2):
                    s = 2 * p + hf
                    emit("tensor", lambda e, s=s, hf=hf, lc=lc, bk=bk:
                         e.transpose(bk[64 * hf:64 * hf + 64, 320:384]
                                     .bitcast(BF16), uvw[pt][s][:, lc, 64:128],
                                     kc_["kidB"][:]),
                         reads=[uvw[pt][s], kc_["kidB"]], writes=[bk])
                emit("scalar", lambda e, p=p, lc=lc, bk=bk: e.copy(
                    wfm[p][:, lc, :], bk[:, 320:384].bitcast(BF16)),
                    reads=[bk], writes=[wfm[p]])
            for p in range(2):
                bq = 32 * p
                bk = pb[4 + 2 * p]
                qkb = [qk01, qk23][p][pt]
                emit("tensor", lambda e, bq=bq, bk=bk, lc=lc: e.matmul(
                    bk[:, 448:449], kc_["khalfF"][bq:bq + 2, :],
                    gCcq[pt][bq:bq + 2, lc:lc + 1], start=True, stop=True),
                    reads=[kc_["khalfF"], gCcq[pt]], writes=[bk])
                emit("vector", lambda e, p=p, bk=bk: e.tensor_copy(
                    gCcol[p][:], bk[:, 448:449]), reads=[bk], writes=[gCcol[p]])
                for hf in range(2):
                    s = 2 * p + hf
                    Bh = 64 * hf
                    emit("tensor", lambda e, s=s, lc=lc, bk=bk: e.matmul(
                        bk[:, 0:64], kc_["kidB"][:], uvw[pt][s][:, lc, 0:64],
                        start=True, stop=False),
                        reads=[kc_["kidB"], uvw[pt][s]], writes=[bk])
                    emit("tensor", lambda e, s=s, p=p, lc=lc, Bh=Bh,
                         bk=bk: e.matmul(
                             bk[:, 0:64], wfm[p][Bh:Bh + 64, lc, :],
                             psb[p][Bh:Bh + 64, :], start=False, stop=True),
                         reads=[wfm[p], psb[p]], writes=[bk])
                    emit("scalar", lambda e, s=s, bk=bk: e.copy(
                        ubuf[s][:], bk[:, 0:64]), reads=[bk], writes=[ubuf[s]])
                    emit("vector", lambda e, s=s, bq=bq, hf=hf, lc=lc:
                         e.tensor_scalar(
                             utb[s][:], ubuf[s][:],
                             colsBq[pt][:, lc, 64 + bq + hf:64 + bq + hf + 1],
                             None, ALU.mult),
                         reads=[ubuf[s], colsBq[pt]], writes=[utb[s]])
                emit("tensor", lambda e, bq=bq, bk=bk, lc=lc: e.matmul(
                    bk[:, 64:192], kc_["khalfB"][bq:bq + 2, :],
                    expAq[pt][bq:bq + 2, lc, :], start=True, stop=True),
                    reads=[kc_["khalfB"], expAq[pt]], writes=[bk])
                emit("vector", lambda e, p=p, lc=lc, qkb=qkb, bk=bk:
                     e.tensor_tensor(qgb[p][:], qkb[:, lc, 0:128],
                                     bk[:, 64:192], ALU.mult),
                     reads=[qkb, bk], writes=[qgb[p]])
                for hf in range(2):
                    s = 2 * p + hf
                    Bh = 64 * hf
                    emit("tensor", lambda e, s=s, hf=hf, bk=bk, lc=lc,
                         fs=(hf == 0): e.matmul(
                             bk[64 * hf:64 * hf + 64, 192:320],
                             ubuf[s][:], attnq[pt][s][:, lc, :], start=fs,
                             stop=False),
                         reads=[ubuf[s], attnq[pt][s]], writes=[bk])
                    emit("tensor", lambda e, s=s, p=p, hf=hf, Bh=Bh, bk=bk,
                         lt=(hf == 1): e.matmul(
                             bk[64 * hf:64 * hf + 64, 192:320],
                             psb[p][Bh:Bh + 64, :], qgb[p][Bh:Bh + 64, :],
                             start=False, stop=lt),
                         reads=[psb[p], qgb[p]], writes=[bk])
                emit("vector", lambda e, p=p, pt=pt, lc=lc, bk=bk: e.tensor_copy(
                    oseg[pt][:, p, lc * 128:(lc + 1) * 128], bk[:, 192:320]),
                    reads=[bk], writes=[oseg[pt]])
                for hf in range(2):
                    s = 2 * p + hf
                    emit("tensor", lambda e, s=s, hf=hf, pt=pt, lc=lc, bk=bk:
                         e.matmul(bk[64 * hf:64 * hf + 64, 320:384],
                                  ktmv[pt][:, lc, s * 64:s * 64 + 64],
                                  utb[s][:], start=(hf == 0), stop=(hf == 1)),
                         reads=[ktmv[pt], utb[s]], writes=[bk])
                emit("vector", lambda e, p=p, bk=bk: e.scalar_tensor_tensor(
                    ps32[p][:], ps32[p][:], gCcol[p][:], bk[:, 320:384],
                    ALU.mult, ALU.add),
                    reads=[ps32[p], gCcol[p], bk], writes=[ps32[p]])
                emit("scalar", lambda e, p=p: e.copy(psb[p][:], ps32[p][:]),
                     reads=[ps32[p]], writes=[psb[p]])
            drain(4)
        drain()
        emit("gpsimd", lambda e, t=t, pt=pt: e.dma_start(
            out=aginv[:, :, t * TILE:(t + 1) * TILE], in_=oseg[pt][:]),
            reads=[oseg[pt]], writes=[agin], sem="dg", inc=16, queue="gpsq")

    def FFN(fp):
        base = fp * 512
        emit("sync", lambda e, base=base: e.dma_start(
            out=fxb[:], in_=xslv[:, :, base:base + 513]), writes=[fxb], inc=16)
        for kc in range(8):
            emit("vector", lambda e, kc=kc: e.tensor_copy(
                ybuf[kc % 2][:], fxb[:, kc, 0:512]),
                reads=[fxb], writes=[ybuf[kc % 2]])
            emit("scalar", lambda e, kc=kc: e.activation(
                y2buf[kc % 2][:], fxb[:, kc, 0:512], AF.Square),
                reads=[fxb], writes=[y2buf[kc % 2]])
            emit("tensor", lambda e, kc=kc: e.matmul(
                pb[2][0:1, :], kc_["ko1b"][:], ybuf[kc % 2][:],
                start=(kc == 0), stop=(kc == 7)),
                reads=[ybuf[kc % 2], kc_["ko1b"]], writes=[pb[2]])
            emit("tensor", lambda e, kc=kc: e.matmul(
                pb[3][0:1, :], kc_["ko1b"][:], y2buf[kc % 2][:],
                start=(kc == 0), stop=(kc == 7)),
                reads=[y2buf[kc % 2], kc_["ko1b"]], writes=[pb[3]])
        for kc in range(8):
            emit("vector", lambda e, kc=kc: e.tensor_copy(
                ybuf[kc % 2][:, 0:1], fxb[:, kc, 512:513]),
                reads=[fxb], writes=[ybuf[kc % 2]])
            emit("scalar", lambda e, kc=kc: e.activation(
                y2buf[kc % 2][:, 0:1], fxb[:, kc, 512:513], AF.Square),
                reads=[fxb], writes=[y2buf[kc % 2]])
            emit("tensor", lambda e, kc=kc: e.matmul(
                pb[2][32:33, 0:1], kc_["ko1b"][:], ybuf[kc % 2][:, 0:1],
                start=(kc == 0), stop=(kc == 7)),
                reads=[ybuf[kc % 2], kc_["ko1b"]], writes=[pb[2]])
            emit("tensor", lambda e, kc=kc: e.matmul(
                pb[3][32:33, 0:1], kc_["ko1b"][:], y2buf[kc % 2][:, 0:1],
                start=(kc == 0), stop=(kc == 7)),
                reads=[y2buf[kc % 2], kc_["ko1b"]], writes=[pb[3]])
        emit("vector", lambda e: e.tensor_scalar(
            rowq[0][:, 0:512], pb[2][0:1, :], 1.0 / D, None, ALU.mult),
            reads=[pb[2]], writes=[rowq[0]])
        emit("vector", lambda e: e.tensor_scalar(
            mrrow[:, 0:1], pb[2][32:33, 0:1], 1.0 / D, None, ALU.mult),
            reads=[pb[2]], writes=[mrrow])
        emit("vector", lambda e: e.tensor_tensor(
            rowq[1][:, 0:512], rowq[0][:, 0:512], rowq[0][:, 0:512], ALU.mult),
            reads=[rowq[0]], writes=[rowq[1]])
        emit("vector", lambda e: e.scalar_tensor_tensor(
            rowq[2][:, 0:512], pb[3][0:1, :], 1.0 / D, rowq[1][:, 0:512],
            ALU.mult, ALU.subtract), reads=[pb[3], rowq[1]], writes=[rowq[2]])
        emit("scalar", lambda e: e.activation(
            rowq[3][:, 0:512], rowq[2][:, 0:512], AF.Ln,
            bias=kc_["keps"][0:1, 0:1]),
            reads=[rowq[2], kc_["keps"]], writes=[rowq[3]])
        emit("scalar", lambda e: e.activation(
            rowq[4][:, 0:512], rowq[3][:, 0:512], AF.Exp, scale=-0.5),
            reads=[rowq[3]], writes=[rowq[4]])
        emit("vector", lambda e: e.tensor_tensor(
            mrrow[:, 1:513], rowq[0][:, 0:512], rowq[4][:, 0:512], ALU.mult),
            reads=[rowq[0], rowq[4]], writes=[mrrow])
        emit("vector", lambda e: e.tensor_tensor(
            hcol[0][0:1, :], mrrow[0:1, 0:1], mrrow[0:1, 0:1], ALU.mult),
            reads=[mrrow], writes=[hcol[0]])
        emit("vector", lambda e: e.scalar_tensor_tensor(
            hcol[0][0:1, :], pb[3][32:33, 0:1], 1.0 / D, hcol[0][0:1, :],
            ALU.mult, ALU.subtract), reads=[pb[3], hcol[0]], writes=[hcol[0]])
        emit("scalar", lambda e: e.activation(
            hcol[0][0:1, :], hcol[0][0:1, :], AF.Ln,
            bias=kc_["keps"][0:1, 0:1]),
            reads=[hcol[0], kc_["keps"]], writes=[hcol[0]])
        emit("scalar", lambda e: e.activation(
            hcol[0][0:1, :], hcol[0][0:1, :], AF.Exp, scale=-0.5),
            reads=[hcol[0]], writes=[hcol[0]])
        emit("vector", lambda e: e.tensor_tensor(
            hcol[1][0:1, :], mrrow[0:1, 0:1], hcol[0][0:1, :], ALU.mult),
            reads=[mrrow, hcol[0]], writes=[hcol[1]])
        emit("tensor", lambda e: e.matmul(
            pb[2][:, :], kc_["ko1f"][:], rowq[4][:, 0:512], start=True,
            stop=True), reads=[kc_["ko1f"], rowq[4]], writes=[pb[2]])
        emit("tensor", lambda e: e.matmul(
            pb[3][:, :], kc_["ko1f"][:], mrrow[:, 1:513], start=True,
            stop=True), reads=[kc_["ko1f"], mrrow], writes=[pb[3]])
        emit("scalar", lambda e: e.copy(rrep[:], pb[2][:, :]),
             reads=[pb[2]], writes=[rrep])
        emit("scalar", lambda e: e.copy(mrep[:], pb[3][:, :]),
             reads=[pb[3]], writes=[mrep])
        emit("tensor", lambda e: e.matmul(
            pb[2][:, 0:1], kc_["ko1f"][:], hcol[0][0:1, :], start=True,
            stop=True), reads=[kc_["ko1f"], hcol[0]], writes=[pb[2]])
        emit("tensor", lambda e: e.matmul(
            pb[3][:, 0:1], kc_["ko1f"][:], hcol[1][0:1, :], start=True,
            stop=True), reads=[kc_["ko1f"], hcol[1]], writes=[pb[3]])
        emit("scalar", lambda e: e.copy(hcol[0][:], pb[2][:, 0:1]),
             reads=[pb[2]], writes=[hcol[0]])
        emit("scalar", lambda e: e.copy(hcol[1][:], pb[3][:, 0:1]),
             reads=[pb[3]], writes=[hcol[1]])
        for kc in range(8):
            emit("gpsimd", lambda e, kc=kc: e.tensor_tensor(
                t1b[kc % 2][:], fxb[:, kc, 0:512], rrep[:], ALU.mult),
                reads=[fxb, rrep], writes=[t1b[kc % 2]])
            emit("vector", lambda e, kc=kc: e.scalar_tensor_tensor(
                fxn[:, kc, 0:512], mrep[:], -1.0, t1b[kc % 2][:], ALU.mult,
                ALU.add), reads=[mrep, t1b[kc % 2]], writes=[fxn])
        for kc in range(8):
            emit("gpsimd", lambda e, kc=kc: e.tensor_tensor(
                t1b[kc % 2][:, 0:1], fxb[:, kc, 512:513], hcol[0][:], ALU.mult),
                reads=[fxb, hcol[0]], writes=[t1b[kc % 2]])
            emit("vector", lambda e, kc=kc: e.scalar_tensor_tensor(
                fxn[:, kc, 512:513], hcol[1][:], -1.0, t1b[kc % 2][:, 0:1],
                ALU.mult, ALU.add), reads=[hcol[1], t1b[kc % 2]], writes=[fxn])
        for stage in range(3):
            for q4 in range(4):
                emit("sync", lambda e, stage=stage, q4=q4: e.dma_start(
                    out=wstg[:], in_=wffnv[:, :, stage * 1024 + q4 * 256:
                                           stage * 1024 + (q4 + 1) * 256]),
                    writes=[wstg], inc=16)
                emit("vector", lambda e, q4=q4: e.tensor_copy(
                    wbf[:, :, q4 * 256:(q4 + 1) * 256], wstg[:]),
                    reads=[wstg], writes=[wbf])
            if stage < 2:
                for kc in range(8):
                    emit("vector", lambda e, kc=kc, stage=stage: e.tensor_scalar(
                        fmt[:], fxn[:, kc, 1:513], mub[:, kc, stage:stage + 1],
                        None, ALU.mult), reads=[fxn, mub], writes=[fmt])
                    emit("vector", lambda e, kc=kc, stage=stage:
                         e.scalar_tensor_tensor(
                             fmx[:, kc], fxn[:, kc, 0:512],
                             mu1b[:, kc, stage:stage + 1], fmt[:],
                             ALU.mult, ALU.add),
                         reads=[fxn, mu1b, fmt], writes=[fmx])
                rhs = fmx
            else:
                rhs = kab
            for mi in range(8):
                bk = pb[mi % 2]
                for kc in range(8):
                    emit("tensor", lambda e, kc=kc, mi=mi, bk=bk, rhs=rhs:
                         e.matmul(bk[:, :], wbf[:, kc, mi * 128:(mi + 1) * 128],
                                  rhs[:, kc], start=(kc == 0), stop=(kc == 7)),
                         reads=[wbf, rhs], writes=[bk])
                if stage == 0:
                    emit("scalar", lambda e, bk=bk: e.activation(
                        ebuf[:], bk[:, :], AF.Exp, scale=-1.0),
                        reads=[bk], writes=[ebuf])
                    emit("vector", lambda e: e.tensor_scalar(
                        tbb[:], ebuf[:], 1.0, None, ALU.add),
                        reads=[ebuf], writes=[tbb])
                    emit("vector", lambda e: e.reciprocal(
                        ftmp[:], tbb[:]), reads=[tbb], writes=[ftmp])
                    emit("vector", lambda e, mi=mi: e.tensor_copy(
                        rfb[:, mi], ftmp[:]), reads=[ftmp], writes=[rfb])
                elif stage == 1:
                    emit("vector", lambda e, bk=bk: e.tensor_scalar(
                        ftmp[:], bk[:, :], 0.0, None, ALU.max),
                        reads=[bk], writes=[ftmp])
                    emit("scalar", lambda e, mi=mi: e.activation(
                        kab[:, mi], ftmp[:], AF.Square),
                        reads=[ftmp], writes=[kab])
                else:
                    emit("vector", lambda e, mi=mi, bk=bk: e.scalar_tensor_tensor(
                        ftmp[:], bk[:, :], 1.0, rfb[:, mi], ALU.mult, ALU.mult),
                        reads=[bk, rfb], writes=[ftmp])
                    emit("vector", lambda e, mi=mi: e.tensor_tensor(
                        fresp[mi % 2][:], ftmp[:], fxb[:, mi, 1:513], ALU.add),
                        reads=[ftmp, fxb], writes=[fresp[mi % 2]])
                    emit("gpsimd", lambda e, base=base, mi=mi: e.dma_start(
                        out=dresv[:, mi, base:base + 512], in_=fresp[mi % 2][:]),
                        reads=[fresp[mi % 2]], writes=[dresid], sem="dg",
                        inc=16, queue="gpsq")

    XL(0)
    P1(0)
    for t in range(NT):
        P23(t, nxt=t + 1 if t + 1 < NT else None)

    emit("gpsimd", lambda e: e.collective_compute(
        "AllGather", ALU.bypass, replica_groups=[[0, 1, 2, 3], [4, 5, 6, 7]],
        ins=[agin.ap().opt()], outs=[agout.ap().opt()]),
        reads=[agin], writes=[agout], sem="cc", inc=1, queue="gpsq")

    fence()
    sctx.close()
    wbf = SB("wbf", [128, 8, 1024], BF16)
    wotb = SB("wotb", [128, 8, 1024], BF16)
    fxb = SB("fxb", [128, 8, 513], F32)
    fxn = SB("fxn", [128, 8, 513], BF16)
    rfb = SB("rfb", [128, 8, 512], BF16)
    kab = SB("kab", [128, 8, 512], BF16)
    ftmp = SB("ftmp", [128, 512], F32)
    fmx = xnb
    fresp = t1b
    otile = [xtile[0][:, :, 0:256].bitcast(BF16),
             xtile[0][:, :, 256:512].bitcast(BF16)]
    _resb = xtile[1][:]
    resb = [_resb, _resb]
    _outsb = SB("outsb", [128, 8, 512], F32)[:]
    outsb = [_outsb, _outsb]


    for q4 in range(4):
        emit("sync", lambda e, q4=q4: e.dma_start(
            out=wstg[:], in_=woutv[:, :, q4 * 256:(q4 + 1) * 256]),
            writes=[wstg], inc=16)
        emit("vector", lambda e, q4=q4: e.tensor_copy(
            wotb[:, :, q4 * 256:(q4 + 1) * 256], wstg[:]),
            reads=[wstg], writes=[wotb])

    def OUT(ot):
        def oload(e, ot=ot):
            if rank_sv[0] is None:
                rank_sv[0] = e.snap(ridx_reg)
            return e.dma_start(
                out=otile[ot % 2][:],
                in_=agoutv.transpose([1, 0, 2])[:, :, bass.ds(rank_sv[0] * SLICE + ot * 512, 512)])
        emit("gpsimd", oload, reads=[agout], writes=[otile[ot % 2]],
             sem="dg", inc=16, queue="gpsq")
        emit("sync", lambda e, ot=ot: e.dma_start(
            out=resb[ot % 2][:], in_=dresv[:, :, ot * 512:(ot + 1) * 512]),
            reads=[dresid], writes=[resb[ot % 2]], inc=16)
        for mi in range(8):
            bk = pb[2 + mi % 2]
            for kc in range(8):
                emit("tensor", lambda e, kc=kc, mi=mi, ot=ot, bk=bk: e.matmul(
                    bk[:, :], wotb[:, kc, mi * 128:(mi + 1) * 128],
                    otile[ot % 2][:, kc], start=(kc == 0), stop=(kc == 7)),
                    reads=[wotb, otile[ot % 2]], writes=[bk])
            emit("vector", lambda e, mi=mi, ot=ot, bk=bk: e.scalar_tensor_tensor(
                outsb[ot % 2][:, mi], bk[:, :], 1.0, resb[ot % 2][:, mi],
                ALU.mult, ALU.add), reads=[bk, resb[ot % 2]],
                writes=[outsb[ot % 2]])
        emit("gpsimd", lambda e, ot=ot: e.dma_start(
            out=outpv[:, :, ot * 512:(ot + 1) * 512], in_=outsb[ot % 2][:]),
            reads=[outsb[ot % 2]], writes=[outp], sem="dg", inc=16,
            queue="gpsq")


    for fp in range(4):
        FFN(fp)
    for ot in range(4):
        OUT(ot)

    fin_dg = cnt["dg"]
    fin_dx = cnt["dx"]
    steps["gpsimd"].append(lambda e: e.wait_ge(sems["dg"], fin_dg))
    steps["gpsimd"].append(lambda e: e.wait_ge(sems["dx"], fin_dx))

    with ctx:
        with nc.Block() as block:
            @block.sync
            def _(e):
                for fn in steps["sync"]:
                    fn(e)

            @block.tensor
            def _(e):
                for fn in steps["tensor"]:
                    fn(e)

            @block.vector
            def _(e):
                for fn in steps["vector"]:
                    fn(e)

            @block.scalar
            def _(e):
                for fn in steps["scalar"]:
                    fn(e)

            @block.gpsimd
            def _(e):
                for fn in steps["gpsimd"]:
                    fn(e)
    return nc


dbg_taps_emit = None


def _host_inputs(inputs):
    x = np.asarray(inputs["L0_attn_x"], np.float32)
    Wqkv = np.asarray(inputs["L0_attn_Wqkv"], np.float32)
    Wg = np.asarray(inputs["L0_attn_Wg"], np.float32)
    bg = np.asarray(inputs["L0_attn_bg"], np.float32)
    Wb = np.asarray(inputs["L0_attn_Wb"], np.float32)
    bb = np.asarray(inputs["L0_attn_bb"], np.float32)
    Wout = np.asarray(inputs["L0_attn_Wout"], np.float32)
    W_r = np.asarray(inputs["L0_ffn_W_r"], np.float32)
    W_k = np.asarray(inputs["L0_ffn_W_k"], np.float32)
    W_v = np.asarray(inputs["L0_ffn_W_v"], np.float32)
    mu_r = np.asarray(inputs["L0_ffn_mu_r"], np.float32)
    mu_k = np.asarray(inputs["L0_ffn_mu_k"], np.float32)

    jj, ii = np.meshgrid(np.arange(128), np.arange(128), indexing="ij")
    maskS = np.where(jj < ii, 0.0, NEG).astype(np.float32)
    maskI = np.where(jj <= ii, 0.0, NEG).astype(np.float32)
    sel0 = np.zeros((128, 128), np.float32)
    sel0[np.arange(128) % 32 == 0, :] = 1.0
    sel1 = np.zeros((128, 128), np.float32)
    sel1[np.arange(128) % 32 == 1, :] = 1.0
    halfF = np.zeros((128, 128), np.float32)
    halfF[np.arange(128) % 32 == 0, 0:64] = 1.0
    halfF[np.arange(128) % 32 == 1, 64:128] = 1.0
    css = np.zeros((128, 4, 128), np.float32)
    for part in range(4):
        c0 = [0, 32, 64, 96][part]
        css[0:64, part, c0] = 1.0
        css[64:128, part, c0 + 1] = 1.0
    ident = np.eye(128, dtype=np.float32)
    mu = np.stack([mu_r.reshape(8, 128).T, mu_k.reshape(8, 128).T], axis=-1)
    wffn = np.hstack([W_r.T, W_k.T, W_v.T]).astype(np.float32)
    wout_t = np.ascontiguousarray(Wout.T)

    ins = []
    for c in range(NCORE):
        b, s = c // 4, c % 4
        xslice = np.empty((D, SLICE + 1), np.float32)
        xslice[:, 1:] = x[b, s * SLICE:(s + 1) * SLICE].T
        xslice[:, 0] = x[b, s * SLICE - 1] if s > 0 else 0.0
        h0 = 4 * s
        wp1c = np.zeros((D, 640), np.float32)
        wp1c[:, 0:128] = Wqkv[h0 * 64:(h0 + 2) * 64].T
        wp1c[:, 128:256] = Wqkv[(h0 + 2) * 64:(h0 + 4) * 64].T
        wp1c[:, 256:384] = Wqkv[D + h0 * 64:D + (h0 + 2) * 64].T
        wp1c[:, 384:512] = Wqkv[D + (h0 + 2) * 64:D + (h0 + 4) * 64].T
        for j in range(2):
            wp1c[:, 512 + j] = Wg[h0 + j]
            wp1c[:, 512 + 32 + j] = Wg[h0 + 2 + j]
            wp1c[:, 512 + 64 + j] = Wb[h0 + j]
            wp1c[:, 512 + 96 + j] = Wb[h0 + 2 + j]
        wtmc = np.hstack([Wqkv[D + h0 * 64:D + h0 * 64 + 256].T,
                          Wqkv[2 * D + h0 * 64:2 * D + h0 * 64 + 256].T])
        gb = np.zeros((98, 1), np.float32)
        for j in range(2):
            gb[j, 0] = bg[h0 + j]
            gb[32 + j, 0] = bg[h0 + 2 + j]
            gb[64 + j, 0] = bb[h0 + j]
            gb[96 + j, 0] = bb[h0 + 2 + j]
        ins.append(dict(
            xt=np.ascontiguousarray(x[b].T), xsl=xslice, wp1=wp1c,
            wtm=np.ascontiguousarray(wtmc), wffn=wffn, wout=wout_t,
            mu=np.ascontiguousarray(mu), gb=gb,
            ridx=np.array([[s]], np.int32),
            c_maskS=maskS, c_maskI=maskI, c_sel0=sel0, c_sel1=sel1,
            c_maskSR=np.ascontiguousarray(maskS.T),
            c_halfF=halfF, c_halfB=halfF.astype(BFNP),
            c_idB=ident.astype(BFNP), c_idF=ident,
            c_ss=css.astype(BFNP),
            c_o1b=np.ones((128, 1), BFNP),
            c_o1f=np.ones((1, 128), np.float32),
            c_eps=np.stack([np.full(128, 1e-5, np.float32),
                            np.zeros(128, np.float32)], axis=1),
        ))
    return ins


def kernel(**inputs):
    ins = _host_inputs(inputs)
    nc = _build()
    res = run_bass_kernel_spmd(nc, ins, core_ids=list(range(NCORE)))
    out = np.empty((B, T, D), np.float32)
    for c in range(NCORE):
        b, s = c // 4, c % 4
        out[b, s * SLICE:(s + 1) * SLICE, :] = np.asarray(
            res.results[c]["outp"], np.float32).T
    return out


# version 21
# speedup vs baseline: 1.1481x; 1.0103x over previous
import numpy as np
import ml_dtypes
from contextlib import ExitStack

import concourse.bass as bass
import concourse.mybir as mybir
from concourse.bass_utils import run_bass_kernel_spmd

F32 = mybir.dt.float32
BF16 = mybir.dt.bfloat16
I32 = mybir.dt.int32
AF = mybir.ActivationFunctionType
ALU = mybir.AluOpType
BFNP = ml_dtypes.bfloat16

B, T, D, H, DH = 2, 8192, 1024, 16, 64
NCORE = 8
TILE = 512
NT = T // TILE
CH = 128
CPT = TILE // CH
SLICE = T // 4
NEG = -1.0e6


def _build(dbg_taps=None):
    nc = bass.Bass("TRN2", target_bir_lowering=False, debug=False)

    xt = nc.dram_tensor("xt", [D, T], F32, kind="ExternalInput").ap()
    xsl = nc.dram_tensor("xsl", [D, SLICE + 1], F32, kind="ExternalInput").ap()
    wp1 = nc.dram_tensor("wp1", [D, 640], F32, kind="ExternalInput").ap()
    wtm = nc.dram_tensor("wtm", [D, 512], F32, kind="ExternalInput").ap()
    wffn = nc.dram_tensor("wffn", [D, 3072], F32, kind="ExternalInput").ap()
    wout = nc.dram_tensor("wout", [D, D], F32, kind="ExternalInput").ap()
    muin = nc.dram_tensor("mu", [128, 8, 2], F32, kind="ExternalInput").ap()
    gbin = nc.dram_tensor("gb", [98, 1], F32, kind="ExternalInput").ap()
    ridx = nc.dram_tensor("ridx", [1, 1], I32, kind="ExternalInput").ap()
    c_in = {}
    for nm, sh, dt in (("c_maskS", [128, 128], F32), ("c_maskI", [128, 128], F32),
                       ("c_maskSR", [128, 128], F32),
                       ("c_sel0", [128, 128], F32), ("c_sel1", [128, 128], F32),
                       ("c_halfF", [128, 128], F32), ("c_halfB", [128, 128], BF16),
                       ("c_idB", [128, 128], BF16), ("c_idF", [128, 128], F32),
                       ("c_ss", [128, 4, 128], BF16), ("c_o1b", [128, 1], BF16),
                       ("c_o1f", [1, 128], F32), ("c_eps", [128, 2], F32)):
        c_in[nm] = nc.dram_tensor(nm, sh, dt, kind="ExternalInput").ap()

    outp = nc.dram_tensor("outp", [D, SLICE], F32, kind="ExternalOutput").ap()
    agin = nc.dram_tensor("agin", [256, T], BF16, kind="Internal")
    agout = nc.dram_tensor("agout", [4, 256, T], BF16, kind="Internal")
    dresid = nc.dram_tensor("dresid", [D, SLICE], F32, kind="Internal")
    dbg_out = {}
    if dbg_taps:
        for nm, sh, dt in dbg_taps:
            dbg_out[nm] = nc.dram_tensor(nm, sh, dt, kind="ExternalOutput").ap()

    ctx = ExitStack()
    SB = lambda n, sh, dt: ctx.enter_context(nc.sbuf_tensor(n, sh, dt))
    PS_ = lambda n: ctx.enter_context(nc.psum_tensor(n, [128, 512], F32))

    xtile = [SB(f"xtile{i}", [128, 8, TILE], F32) for i in range(2)]
    ybuf = [SB(f"ybuf{i}", [128, TILE], BF16) for i in range(2)]
    y2buf = [SB(f"y2buf{i}", [128, TILE], BF16) for i in range(2)]
    xnb = SB("xnb", [128, 8, TILE], BF16)
    t1b = [SB(f"t1b{i}", [128, TILE], F32) for i in range(2)]
    rrep = SB("rrep", [128, TILE], F32)
    mrep = SB("mrep", [128, TILE], F32)
    rowq = [SB(f"rowq{i}", [1, TILE], F32) for i in range(5)]
    mrrow = SB("mrrow", [1, TILE + 1], F32)
    hcol = [SB(f"hcol{i}", [128, 1], F32) for i in range(2)]
    wstg = SB("wstg", [128, 8, 256], F32)
    ebuf = SB("ebuf", [128, TILE], F32)
    tbb = SB("tbb", [128, TILE], F32)
    sqb = SB("sqb", [128, TILE], BF16)
    fmt = sqb
    mub = SB("mub", [128, 8, 2], F32)
    mu1b = SB("mu1b", [128, 8, 2], F32)
    kc_ = {}
    for nm, sh, dt in (("kmask2", [128, 256], F32),
                       ("kmaskSR", [128, 128], F32),
                       ("ksel0", [128, 128], F32), ("ksel1", [128, 128], F32),
                       ("khalfF", [128, 128], F32), ("khalfB", [128, 128], BF16),
                       ("kidB", [128, 128], BF16), ("kidF", [128, 128], F32),
                       ("kss", [128, 4, 128], BF16), ("ko1b", [128, 1], BF16),
                       ("ko1f", [1, 128], F32), ("keps", [128, 2], F32)):
        kc_[nm] = SB(nm, sh, dt)
    ksel = [kc_["ksel0"], kc_["ksel1"]]
    kgb = SB("kgb", [128, 1], F32)
    ngb = SB("ngb", [128, 1], F32)
    sridx = SB("sridx", [1, 1], I32)

    pb = [PS_(f"pb{i}") for i in range(8)]

    sctx = ExitStack()
    SBs = lambda n, sh, dt: sctx.enter_context(nc.sbuf_tensor(n, sh, dt))
    wp1b = SBs("wp1b", [128, 8, 640], BF16)
    wtmb = SBs("wtmb", [128, 8, 512], BF16)
    qk01 = [SBs(f"qk01s{i}", [128, CPT, 256], BF16) for i in range(2)]
    qk23 = [SBs(f"qk23s{i}", [128, CPT, 256], BF16) for i in range(2)]
    ktmv = [SBs(f"ktmvs{i}", [128, CPT, 512], BF16) for i in range(2)]
    abb = SBs("abb", [128, TILE], F32)
    lntb = SBs("lntb", [128, TILE], F32)
    sscb = SBs("sscb", [128, TILE], F32)
    t1r = SBs("t1r", [128, TILE], F32)
    t2r = SBs("t2r", [128, TILE], F32)
    t3r = sscb
    lnssb = lntb
    _gneg = SBs("gnegs", [128, TILE], F32)
    _colsrc = SBs("colsrcs", [128, TILE], F32)
    gneg = [_gneg, SBs("gnegs1", [128, TILE], F32)]
    rows3 = [SBs(f"rows3_{i}", [128, 3 * TILE], F32) for i in range(2)]
    r3A = lambda i, lc: rows3[i][:, lc * 384:lc * 384 + 128]
    r3M = lambda i, lc: rows3[i][:, lc * 384 + 128:lc * 384 + 256]
    r3C = lambda i, lc: rows3[i][:, lc * 384 + 256:lc * 384 + 384]
    colsrc = [_colsrc, SBs("colsrcs1", [128, TILE], F32)]
    escr = SBs("escr", [128, 128], F32)
    expAq = [SBs(f"expAq{i}", [128, CPT, 128], BF16) for i in range(2)]
    gdif = SBs("gdif", [128, 1], F32)
    ngend = SBs("ngend", [128, 1], F32)
    colsA = SBs("colsA", [128, 128], F32)
    colsBq = [SBs(f"colsBq{i}", [128, CPT, 128], F32) for i in range(2)]
    nmcols = SBs("nmcols", [128, 128], F32)
    t3b = [SBs(f"t3b{i}", [128, 384], F32) for i in range(2)]
    dab3 = [SBs(f"dab3{i}", [128, 384], BF16) for i in range(2)]
    attnq = [[SBs(f"attnq{i}_{s}", [128, CPT, 128], BF16) for s in range(4)]
             for i in range(2)]
    _mnegq = [SBs(f"mnegq_{s}", [128, CPT, 128], BF16) for s in range(4)]
    mnegq = [_mnegq, _mnegq]
    _mrawq = [SBs(f"mrawq_{s}", [128, CPT, 128], BF16) for s in range(4)]
    mrawq = [_mrawq, _mrawq]
    _rhsq = [SBs(f"rhsq_{s}", [128, CPT, 128], BF16) for s in range(4)]
    rhsq = [_rhsq, _rhsq]
    xq = [SBs(f"xq{s}", [128, CPT, 128], BF16) for s in range(4)]
    pq = [[SBs(f"pq{s}_{i}", [128, CPT, 128], BF16) for i in range(2)]
          for s in range(4)]
    prawq = [SBs(f"prawq{s}", [128, CPT, 128], BF16) for s in range(4)]
    _uvw = [SBs(f"uvw_{s}", [128, CPT, 128], BF16) for s in range(4)]
    uvw = [_uvw, _uvw]
    wfm = [SBs(f"wfm{p}", [128, CPT, 128], BF16) for p in range(2)]
    ubuf = [SBs(f"ubuf{s}", [128, 64], BF16) for s in range(4)]
    utb = [SBs(f"utb{s}", [128, 64], BF16) for s in range(4)]
    qgb = [SBs(f"qgb{p}", [128, 128], BF16) for p in range(2)]
    gCcq = [SBs(f"gCcq{i}", [128, CPT], F32) for i in range(2)]
    gCcol = [SBs(f"gCcol{p}", [128, 1], F32) for p in range(2)]
    ps32 = [SBs(f"ps32{p}", [128, 64], F32) for p in range(2)]
    psb = [SBs(f"psb{p}", [128, 64], BF16) for p in range(2)]
    _oseg = SBs("osegs", [128, 2, TILE], BF16)
    oseg = [_oseg, _oseg]

    sems = {}
    for sname in ("dx", "dg", "cc", "t", "v", "s", "g"):
        sems[sname] = ctx.enter_context(nc.semaphore(sname))

    steps = {e: [] for e in ("sync", "tensor", "vector", "scalar", "gpsimd")}
    cnt = {k: 0 for k in sems}
    RES = {}
    HWM = {}

    def _res(obj):
        key = id(obj)
        if key not in RES:
            RES[key] = {"w": None, "r": {}}
        return RES[key]

    deferq = []
    defer_on = [False]

    def emit(q, fn, reads=(), writes=(), sem=None, inc=1, queue=None):
        if defer_on[0]:
            deferq.append((q, fn, reads, writes, sem, inc, queue))
            return
        if sem is None:
            sem = {"sync": "dx", "tensor": "t", "vector": "v",
                   "scalar": "s", "gpsimd": "g"}[q]
        queue = queue or (q if sem not in ("dx", "dg") else "q_" + sem)
        waits = {}

        def addw(ent):
            if ent is None:
                return
            eq, es, ec = ent
            if eq == queue:
                return
            if waits.get(es, 0) < ec:
                waits[es] = ec
        for b in reads:
            addw(_res(b)["w"])
        for b in writes:
            r = _res(b)
            addw(r["w"])
            for eq, (es, ec) in r["r"].items():
                addw((eq, es, ec))
        for s2, v2 in sorted(waits.items()):
            if HWM.get((q, s2), -1) >= v2:
                continue
            HWM[(q, s2)] = v2
            steps[q].append(lambda e, s=s2, v=v2: e.wait_ge(sems[s], v))
        cnt[sem] += inc
        me = (queue, sem, cnt[sem])
        steps[q].append(lambda e, f=fn, s=sem, k=inc: f(e).then_inc(sems[s], k))
        for b in reads:
            _res(b)["r"][queue] = (sem, cnt[sem])
        for b in writes:
            r = _res(b)
            r["w"] = me
            r["r"] = {}
        return cnt[sem]

    def drain(n=1 << 30):
        for _ in range(min(n, len(deferq))):
            q, fn, reads, writes, sem, inc, queue = deferq.pop(0)
            emit(q, fn, reads, writes, sem, inc, queue)

    def fence():
        vals = dict(cnt)
        for q2 in ("sync", "tensor", "vector", "scalar", "gpsimd"):
            for sname, v in vals.items():
                if v > 0 and HWM.get((q2, sname), -1) < v:
                    HWM[(q2, sname)] = v
                    steps[q2].append(
                        lambda e, s=sname, vv=v: e.wait_ge(sems[s], vv))
        RES.clear()

    xtv = xt.rearrange("(c p) t -> p c t", p=128)
    xslv = xsl.rearrange("(c p) t -> p c t", p=128)
    wp1v = wp1.rearrange("(c p) m -> p c m", p=128)
    wtmv = wtm.rearrange("(c p) m -> p c m", p=128)
    wffnv = wffn.rearrange("(c p) m -> p c m", p=128)
    woutv = wout.rearrange("(c p) m -> p c m", p=128)
    aginv = agin.ap().rearrange("(r p) t -> p r t", p=128)
    agoutv = agout.ap().rearrange("r (h p) t -> (r h) p t", p=128)
    dresv = dresid.ap().rearrange("(c p) t -> p c t", p=128)
    outpv = outp.rearrange("(c p) t -> p c t", p=128)

    DMA = "dma_marker"

    emit("sync", lambda e: e.dma_start(out=kc_["kmask2"][:, 0:128],
         in_=c_in["c_maskI"]), writes=[kc_["kmask2"]], inc=16)
    emit("sync", lambda e: e.dma_start(out=kc_["kmask2"][:, 128:256],
         in_=c_in["c_maskS"]), writes=[kc_["kmask2"]], inc=16)
    for nm, dst in (("c_maskSR", kc_["kmaskSR"]),
                    ("c_eps", kc_["keps"]),
                    ("c_sel0", ksel[0]), ("c_sel1", ksel[1]),
                    ("c_halfF", kc_["khalfF"]), ("c_halfB", kc_["khalfB"]),
                    ("c_idB", kc_["kidB"]), ("c_idF", kc_["kidF"]),
                    ("c_ss", kc_["kss"]), ("c_o1b", kc_["ko1b"]),
                    ("c_o1f", kc_["ko1f"])):
        emit("sync", lambda e, d=dst, s=c_in[nm]: e.dma_start(out=d[:], in_=s),
             writes=[dst], inc=16)
    emit("sync", lambda e: e.dma_start(out=mub[:], in_=muin), writes=[mub], inc=16)
    emit("sync", lambda e: e.dma_start(out=sridx[:], in_=ridx), writes=[sridx], inc=16)
    emit("vector", lambda e: e.memset(kgb[:], 0.0), writes=[kgb])
    emit("sync", lambda e: e.dma_start(out=kgb[0:98, :], in_=gbin),
         writes=[kgb], inc=16)
    for lo, hi in ((0, 256), (256, 512), (512, 640)):
        emit("sync", lambda e, lo=lo, hi=hi: e.dma_start(
            out=wstg[:, :, 0:hi - lo], in_=wp1v[:, :, lo:hi]),
            writes=[wstg], inc=16)
        emit("vector", lambda e, lo=lo, hi=hi: e.tensor_copy(
            wp1b[:, :, lo:hi], wstg[:, :, 0:hi - lo]),
            reads=[wstg], writes=[wp1b])
    for lo in (0, 256):
        emit("sync", lambda e, lo=lo: e.dma_start(
            out=wstg[:], in_=wtmv[:, :, lo:lo + 256]), writes=[wstg], inc=16)
        emit("vector", lambda e, lo=lo: e.tensor_copy(
            wtmb[:, :, lo:lo + 256], wstg[:]), reads=[wstg], writes=[wtmb])
    emit("vector", lambda e: e.tensor_scalar(ngb[:], kgb[:], -1.0, None, ALU.mult),
         reads=[kgb], writes=[ngb])
    for p in range(2):
        emit("vector", lambda e, p=p: e.memset(ps32[p][:], 0.0), writes=[ps32[p]])
        emit("vector", lambda e, p=p: e.memset(psb[p][:], 0.0), writes=[psb[p]])
    emit("vector", lambda e: e.tensor_scalar(
        mu1b[:], mub[:], -1.0, 1.0, ALU.mult, ALU.add),
        reads=[mub], writes=[mu1b])

    ridx_reg = nc.gpsimd.alloc_register("ridxreg")
    rank_sv = [None]
    emit("gpsimd", lambda e: e.reg_load(ridx_reg, sridx[0:1, 0:1]),
         reads=[sridx], writes=[], sem="g")

    def XL(t):
        xb = xtile[t % 2]
        emit("sync", lambda e, t=t, xb=xb: e.dma_start(
            out=xb[:], in_=xtv[:, :, t * TILE:(t + 1) * TILE]),
            writes=[xb], inc=16)

    def P1(t):
        pt = t % 2
        xb = xtile[pt]
        for kc in range(8):
            emit("vector", lambda e, kc=kc: e.tensor_copy(
                ybuf[kc % 2][:], xb[:, kc]), reads=[xb], writes=[ybuf[kc % 2]])
            emit("scalar", lambda e, kc=kc: e.activation(
                y2buf[kc % 2][:], xb[:, kc], AF.Square),
                reads=[xb], writes=[y2buf[kc % 2]])
            emit("tensor", lambda e, kc=kc: e.matmul(
                pb[2][0:1, :], kc_["ko1b"][:], ybuf[kc % 2][:],
                start=(kc == 0), stop=(kc == 7)),
                reads=[ybuf[kc % 2], kc_["ko1b"]], writes=[pb[2]])
            emit("tensor", lambda e, kc=kc: e.matmul(
                pb[3][0:1, :], kc_["ko1b"][:], y2buf[kc % 2][:],
                start=(kc == 0), stop=(kc == 7)),
                reads=[y2buf[kc % 2], kc_["ko1b"]], writes=[pb[3]])
        emit("vector", lambda e: e.tensor_scalar(
            rowq[0][:], pb[2][0:1, :], 1.0 / D, None, ALU.mult),
            reads=[pb[2]], writes=[rowq[0]])
        emit("vector", lambda e: e.tensor_tensor(
            rowq[1][:], rowq[0][:], rowq[0][:], ALU.mult),
            reads=[rowq[0]], writes=[rowq[1]])
        emit("vector", lambda e: e.scalar_tensor_tensor(
            rowq[2][:], pb[3][0:1, :], 1.0 / D, rowq[1][:], ALU.mult,
            ALU.subtract), reads=[pb[3], rowq[1]], writes=[rowq[2]])
        emit("scalar", lambda e: e.activation(rowq[3][:], rowq[2][:], AF.Ln,
                                              bias=kc_["keps"][0:1, 0:1]),
             reads=[rowq[2], kc_["keps"]], writes=[rowq[3]])
        emit("scalar", lambda e: e.activation(rowq[4][:], rowq[3][:], AF.Exp,
                                              scale=-0.5),
             reads=[rowq[3]], writes=[rowq[4]])
        emit("vector", lambda e: e.tensor_tensor(
            mrrow[:, 0:TILE], rowq[0][:], rowq[4][:], ALU.mult),
            reads=[rowq[0], rowq[4]], writes=[mrrow])
        emit("tensor", lambda e: e.matmul(pb[2][:, :], kc_["ko1f"][:],
                                          rowq[4][:], start=True, stop=True),
             reads=[rowq[4], kc_["ko1f"]], writes=[pb[2]])
        emit("tensor", lambda e: e.matmul(pb[3][:, :], kc_["ko1f"][:],
                                          mrrow[:, 0:TILE], start=True, stop=True),
             reads=[mrrow, kc_["ko1f"]], writes=[pb[3]])
        emit("scalar", lambda e: e.copy(rrep[:], pb[2][:, :]),
             reads=[pb[2]], writes=[rrep])
        emit("scalar", lambda e: e.copy(mrep[:], pb[3][:, :]),
             reads=[pb[3]], writes=[mrep])
        for kc in range(8):
            emit("gpsimd", lambda e, kc=kc: e.tensor_tensor(
                t1b[kc % 2][:], xb[:, kc], rrep[:], ALU.mult),
                reads=[xb, rrep], writes=[t1b[kc % 2]])
            emit("vector", lambda e, kc=kc: e.scalar_tensor_tensor(
                xnb[:, kc], mrep[:], -1.0, t1b[kc % 2][:], ALU.mult, ALU.add),
                reads=[mrep, t1b[kc % 2]], writes=[xnb])
        for mi in range(5):
            bk = pb[mi % 2]
            for kc in range(8):
                emit("tensor", lambda e, kc=kc, mi=mi, bk=bk: e.matmul(
                    bk[:, :], wp1b[:, kc, mi * 128:(mi + 1) * 128], xnb[:, kc],
                    start=(kc == 0), stop=(kc == 7)),
                    reads=[wp1b, xnb], writes=[bk])
            if mi < 4:
                qkb = [qk01, qk23][mi % 2][pt]
                csl = slice(0, 128) if mi < 2 else slice(128, 256)
                emit("scalar", lambda e, qkb=qkb, csl=csl, bk=bk: e.copy(
                    qkb[:, :, csl], bk[:, :].rearrange("p (c x) -> p c x", x=128)),
                    reads=[bk], writes=[qkb])
            else:
                emit("scalar", lambda e, bk=bk: e.activation(
                    ebuf[:], bk[:, :], AF.Exp, bias=ngb[:], scale=-1.0),
                    reads=[bk, ngb], writes=[ebuf])
                emit("vector", lambda e: e.tensor_scalar(
                    tbb[:], ebuf[:], 1.0, None, ALU.add),
                    reads=[ebuf], writes=[tbb])
                emit("vector", lambda e: e.reciprocal(abb[:], tbb[:]),
                     reads=[tbb], writes=[abb])
                emit("scalar", lambda e: e.activation(lntb[:], tbb[:], AF.Ln),
                     reads=[tbb], writes=[lntb])
                emit("vector", lambda e, pt=pt: e.tensor_tensor_scan(
                    gneg[pt][:], lntb[:], lntb[:], 0.0, ALU.add, ALU.bypass),
                    reads=[lntb], writes=[gneg[pt]])
                emit("vector", lambda e: e.tensor_copy(t1r[0:2, :], lntb[64:66, :]),
                     reads=[lntb], writes=[t1r])
                emit("vector", lambda e: e.tensor_copy(t1r[32:34, :], lntb[96:98, :]),
                     reads=[lntb], writes=[t1r])
        for tg in range(4):
            bk = pb[2 + tg % 2]
            for kc in range(8):
                emit("tensor", lambda e, kc=kc, tg=tg, bk=bk: e.matmul(
                    bk[:, :], xnb[:, kc, tg * 128:(tg + 1) * 128],
                    wtmb[:, kc, :], start=(kc == 0), stop=(kc == 7)),
                    reads=[xnb, wtmb], writes=[bk])
            emit("scalar", lambda e, tg=tg, pt=pt, bk=bk: e.copy(
                ktmv[pt][:, tg, :], bk[:, :]), reads=[bk], writes=[ktmv[pt]])
        for part in range(4):
            qkb = [qk01, qk23][part % 2][pt]
            csl = slice(0, 128) if part < 2 else slice(128, 256)
            emit("scalar", lambda e, qkb=qkb, csl=csl: e.activation(
                sqb[:].rearrange("p (c x) -> p c x", x=128), qkb[:, :, csl],
                AF.Square), reads=[qkb], writes=[sqb])
            emit("tensor", lambda e, part=part: e.matmul(
                pb[0][:, :], kc_["kss"][:, part, :], sqb[:],
                start=(part == 0), stop=(part == 3)),
                reads=[sqb, kc_["kss"]], writes=[pb[0]])
        emit("vector", lambda e: e.tensor_scalar(
            sscb[:], pb[0][:, :], 1e-24, None, ALU.max),
            reads=[pb[0]], writes=[sscb])
        emit("scalar", lambda e: e.activation(lnssb[:], sscb[:], AF.Ln),
             reads=[sscb], writes=[lnssb])
        emit("vector", lambda e: e.tensor_scalar(
            lnssb[:], lnssb[:], 0.5, None, ALU.mult),
            reads=[lnssb], writes=[lnssb])
        emit("vector", lambda e: e.tensor_copy(t2r[0:2, :], lnssb[64:66, :]),
             reads=[lnssb], writes=[t2r])
        emit("vector", lambda e: e.tensor_copy(t2r[32:34, :], lnssb[96:98, :]),
             reads=[lnssb], writes=[t2r])
        emit("gpsimd", lambda e: e.tensor_tensor(
            t3r[:], t2r[:], t1r[:], ALU.add),
            reads=[t2r, t1r], writes=[t3r])
        for lc4 in range(CPT):
            cs4 = slice(lc4 * 128, (lc4 + 1) * 128)
            emit("gpsimd", lambda e, pt=pt, lc4=lc4, cs4=cs4: e.tensor_tensor(
                r3A(pt, lc4), lnssb[:, cs4], gneg[pt][:, cs4], ALU.add),
                reads=[lnssb, gneg[pt]], writes=[rows3[pt]])
            emit("gpsimd", lambda e, pt=pt, lc4=lc4, cs4=cs4: e.tensor_tensor(
                r3M(pt, lc4), t3r[:, cs4], gneg[pt][:, cs4], ALU.add),
                reads=[t3r, gneg[pt]], writes=[rows3[pt]])
            emit("gpsimd", lambda e, pt=pt, lc4=lc4, cs4=cs4: e.tensor_tensor(
                r3C(pt, lc4), t2r[:, cs4], gneg[pt][:, cs4], ALU.subtract),
                reads=[t2r, gneg[pt]], writes=[rows3[pt]])
        emit("gpsimd", lambda e, pt=pt: e.tensor_tensor(
            colsrc[pt][:], gneg[pt][:], t2r[:], ALU.subtract),
            reads=[t2r, gneg[pt]], writes=[colsrc[pt]])
        emit("vector", lambda e, pt=pt: e.tensor_copy(
            colsrc[pt][64:66, :], abb[64:66, :]),
            reads=[abb], writes=[colsrc[pt]])
        emit("vector", lambda e, pt=pt: e.tensor_copy(
            colsrc[pt][96:98, :], abb[96:98, :]),
            reads=[abb], writes=[colsrc[pt]])

    def P23(t, nxt=None):
        if nxt is not None:
            XL(nxt)
        pt = t % 2
        for lc in range(CPT):
            csl = slice(lc * 128, (lc + 1) * 128)
            if lc == 0:
                emit("vector", lambda e, pt=pt, lc=lc: e.tensor_scalar(
                    gdif[:], gneg[pt][:, (lc + 1) * 128 - 1:(lc + 1) * 128],
                    -1.0, None, ALU.mult), reads=[gneg[pt]], writes=[gdif])
            else:
                emit("vector", lambda e, pt=pt, lc=lc: e.tensor_tensor(
                    gdif[:], gneg[pt][:, lc * 128 - 1:lc * 128],
                    gneg[pt][:, (lc + 1) * 128 - 1:(lc + 1) * 128],
                    ALU.subtract), reads=[gneg[pt]], writes=[gdif])
            emit("vector", lambda e, pt=pt, lc=lc: e.tensor_scalar(
                ngend[:], gneg[pt][:, (lc + 1) * 128 - 1:(lc + 1) * 128],
                -1.0, None, ALU.mult), reads=[gneg[pt]], writes=[ngend])
            if lc == 0:
                emit("scalar", lambda e, pt=pt, lc=lc: e.activation(
                    escr[0:64, :], r3M(pt, lc)[0:64, :], AF.Exp,
                    scale=-1.0), reads=[rows3[pt]], writes=[escr])
                emit("scalar", lambda e, pt=pt, lc=lc: e.activation(
                    expAq[pt][0:64, lc, :], r3A(pt, lc)[0:64, :], AF.Exp,
                    scale=-1.0), reads=[rows3[pt]], writes=[expAq[pt]])
            else:
                emit("scalar", lambda e, pt=pt, lc=lc: e.activation(
                    escr[0:64, :], r3M(pt, lc)[0:64, :], AF.Exp, scale=-1.0,
                    bias=gneg[pt][0:64, lc * 128 - 1:lc * 128]),
                    reads=[rows3[pt], gneg[pt]], writes=[escr])
                emit("scalar", lambda e, pt=pt, lc=lc: e.activation(
                    expAq[pt][0:64, lc, :], r3A(pt, lc)[0:64, :], AF.Exp,
                    scale=-1.0,
                    bias=gneg[pt][0:64, lc * 128 - 1:lc * 128]),
                    reads=[rows3[pt], gneg[pt]], writes=[expAq[pt]])
            emit("scalar", lambda e, pt=pt, csl=csl: e.activation(
                escr[64:128, :], colsrc[pt][0:64, csl],
                AF.Exp, bias=ngend[0:64, :]),
                reads=[colsrc[pt], ngend], writes=[escr])
            emit("scalar", lambda e, pt=pt, lc=lc: e.activation(
                gCcq[pt][0:64, lc:lc + 1], gdif[0:64, :], AF.Exp),
                reads=[gdif], writes=[gCcq[pt]])
            emit("tensor", lambda e, pt=pt, csl=csl: e.transpose(
                pb[4][:, 384:512], colsrc[pt][:, csl], kc_["kidF"][:]),
                reads=[colsrc[pt], kc_["kidF"]], writes=[pb[4]])
            emit("vector", lambda e: e.tensor_copy(colsA[:], pb[4][:, 384:512]),
                 reads=[pb[4]], writes=[colsA])
            emit("tensor", lambda e: e.transpose(
                pb[4][:, 384:512], escr[:], kc_["kidF"][:]),
                reads=[escr, kc_["kidF"]], writes=[pb[4]])
            emit("vector", lambda e, lc=lc: e.tensor_copy(
                colsBq[pt][:, lc, :], pb[4][:, 384:512]),
                reads=[pb[4]], writes=[colsBq[pt]])
            emit("tensor", lambda e, pt=pt, lc=lc: e.transpose(
                pb[5][:, 384:512], r3M(pt, lc), kc_["kidF"][:]),
                reads=[rows3[pt], kc_["kidF"]], writes=[pb[5]])
            emit("vector", lambda e: e.tensor_scalar(
                nmcols[:], pb[5][:, 384:512], -1.0, None, ALU.mult),
                reads=[pb[5]], writes=[nmcols])

            for s in range(4):
                p, hf = s // 2, s % 2
                bq = 32 * p
                Bh = 64 * hf
                qkb = [qk01, qk23][p][pt]
                bk = pb[2 + s % 2]
                bk2 = pb[6 + s % 2]
                emit("tensor", lambda e, qkb=qkb, lc=lc, Bh=Bh, bk=bk: e.matmul(
                    bk[:, 0:256], qkb[Bh:Bh + 64, lc, 128:256],
                    qkb[Bh:Bh + 64, lc, :], start=True, stop=True),
                    reads=[qkb], writes=[bk])
                emit("tensor", lambda e, pt=pt, bq=bq, hf=hf, bk2=bk2, lc=lc:
                     e.matmul(bk2[:, 0:384],
                              ksel[hf][bq:bq + 2, :],
                              rows3[pt][bq:bq + 2, lc * 384:(lc + 1) * 384],
                              start=True, stop=True),
                     reads=[ksel[hf], rows3[pt]], writes=[bk2])
                emit("scalar", lambda e, s=s, bq=bq, hf=hf: e.activation(
                    t3b[s % 2][:, 0:256], kc_["kmask2"][:], AF.Identity,
                    bias=colsA[:, bq + hf:bq + hf + 1]),
                    reads=[kc_["kmask2"], colsA], writes=[t3b[s % 2]])
                emit("scalar", lambda e, s=s, bq=bq, hf=hf: e.activation(
                    t3b[s % 2][:, 256:384], kc_["kmaskSR"][:], AF.Identity,
                    bias=nmcols[:, bq + hf:bq + hf + 1]),
                    reads=[kc_["kmaskSR"], nmcols], writes=[t3b[s % 2]])
                emit("vector", lambda e, s=s, bk2=bk2: e.scalar_tensor_tensor(
                    t3b[s % 2][:], bk2[:, 0:384], -1.0, t3b[s % 2][:],
                    ALU.mult, ALU.add), reads=[bk2, t3b[s % 2]],
                    writes=[t3b[s % 2]])
                emit("scalar", lambda e, s=s: e.activation(
                    dab3[s % 2][:], t3b[s % 2][:], AF.Exp),
                    reads=[t3b[s % 2]], writes=[dab3[s % 2]])
                emit("vector", lambda e, s=s, lc=lc, bk=bk: e.scalar_tensor_tensor(
                    attnq[pt][s][:, lc, :], bk[:, 0:128], 1.0,
                    dab3[s % 2][:, 0:128], ALU.mult, ALU.mult),
                    reads=[bk, dab3[s % 2]], writes=[attnq[pt][s]])
                emit("vector", lambda e, s=s, lc=lc, bk=bk: e.scalar_tensor_tensor(
                    mnegq[pt][s][:, lc, :], bk[:, 128:256], -1.0,
                    dab3[s % 2][:, 128:256], ALU.mult, ALU.mult),
                    reads=[bk, dab3[s % 2]], writes=[mnegq[pt][s]])
                emit("vector", lambda e, s=s, lc=lc, bk=bk: e.scalar_tensor_tensor(
                    mrawq[pt][s][:, lc, :], bk[:, 128:256], -1.0,
                    dab3[s % 2][:, 256:384], ALU.mult, ALU.mult),
                    reads=[bk, dab3[s % 2]], writes=[mrawq[pt][s]])
                emit("vector", lambda e, s=s, pt=pt, lc=lc, bq=bq, hf=hf:
                     e.tensor_scalar(
                         rhsq[pt][s][:, lc, 0:64],
                         ktmv[pt][:, lc, 256 + s * 64:256 + s * 64 + 64],
                         colsA[:, 64 + bq + hf:64 + bq + hf + 1], None,
                         ALU.mult),
                     reads=[ktmv[pt], colsA], writes=[rhsq[pt][s]])
                emit("vector", lambda e, s=s, pt=pt, lc=lc, bq=bq, hf=hf:
                     e.tensor_scalar(
                         rhsq[pt][s][:, lc, 64:128],
                         ktmv[pt][:, lc, s * 64:s * 64 + 64],
                         colsBq[pt][:, lc, bq + hf:bq + hf + 1], -1.0,
                         ALU.mult, ALU.mult),
                     reads=[ktmv[pt], colsBq[pt]], writes=[rhsq[pt][s]])

        if nxt is not None:
            defer_on[0] = True
            P1(nxt)
            defer_on[0] = False

        for lev in range(7):
            for s in range(4):
                bk = pb[4 + s]
                Xs = rhsq[pt][s] if lev == 0 else xq[s]
                Ps = mnegq[pt][s] if lev == 0 else pq[s][(lev - 1) % 2]
                for l4 in range(CPT):
                    emit("tensor", lambda e, bk=bk, Ps=Ps, Xs=Xs, l4=l4:
                         e.matmul(bk[:, l4 * 128:(l4 + 1) * 128],
                                  Ps[:, l4, :], Xs[:, l4, :],
                                  start=True, stop=True),
                         reads=[Ps, Xs], writes=[bk])
            for s in range(4):
                bk = pb[4 + s]
                Xs = rhsq[pt][s] if lev == 0 else xq[s]
                dest = uvw[pt][s] if lev == 6 else xq[s]
                emit("vector", lambda e, bk=bk, Xs=Xs, d=dest:
                     e.scalar_tensor_tensor(
                         d[:].rearrange("p c x -> p (c x)"), bk[:, :], 1.0,
                         Xs[:].rearrange("p c x -> p (c x)"), ALU.mult,
                         ALU.add),
                     reads=[bk, Xs], writes=[dest])
            drain(8)
            if lev < 6:
                for s in range(4):
                    bk = pb[4 + s]
                    Ps = mnegq[pt][s] if lev == 0 else pq[s][(lev - 1) % 2]
                    Rs = mrawq[pt][s] if lev == 0 else prawq[s]
                    for l4 in range(CPT):
                        emit("tensor", lambda e, bk=bk, Ps=Ps, Rs=Rs, l4=l4:
                             e.matmul(bk[:, l4 * 128:(l4 + 1) * 128],
                                      Rs[:, l4, :], Ps[:, l4, :],
                                      start=True, stop=True),
                             reads=[Ps, Rs], writes=[bk])
                drain(6)
                for s in range(4):
                    bk = pb[4 + s]
                    emit("scalar", lambda e, bk=bk, s=s, lev=lev: e.copy(
                        pq[s][lev % 2][:].rearrange("p c x -> p (c x)"),
                        bk[:, :]), reads=[bk], writes=[pq[s][lev % 2]])
                for s in range(4):
                    bk = pb[4 + s]
                    Ps = mnegq[pt][s] if lev == 0 else pq[s][(lev - 1) % 2]
                    Rs = mrawq[pt][s] if lev == 0 else prawq[s]
                    for l4 in range(CPT):
                        emit("tensor", lambda e, bk=bk, Ps=Ps, Rs=Rs, l4=l4:
                             e.matmul(bk[:, l4 * 128:(l4 + 1) * 128],
                                      Ps[:, l4, :], Rs[:, l4, :],
                                      start=True, stop=True),
                             reads=[Ps, Rs], writes=[bk])
                for s in range(4):
                    bk = pb[4 + s]
                    emit("vector", lambda e, bk=bk, s=s: e.tensor_copy(
                        prawq[s][:].rearrange("p c x -> p (c x)"),
                        bk[:, :]), reads=[bk], writes=[prawq[s]])
            drain(8)

        for lc in range(CPT):
            for p in range(2):
                bk = pb[4 + 2 * p]
                for hf in range(2):
                    s = 2 * p + hf
                    emit("tensor", lambda e, s=s, hf=hf, lc=lc, bk=bk:
                         e.transpose(bk[64 * hf:64 * hf + 64, 320:384]
                                     .bitcast(BF16), uvw[pt][s][:, lc, 64:128],
                                     kc_["kidB"][:]),
                         reads=[uvw[pt][s], kc_["kidB"]], writes=[bk])
                emit("scalar", lambda e, p=p, lc=lc, bk=bk: e.copy(
                    wfm[p][:, lc, :], bk[:, 320:384].bitcast(BF16)),
                    reads=[bk], writes=[wfm[p]])
            for p in range(2):
                bq = 32 * p
                bk = pb[4 + 2 * p]
                qkb = [qk01, qk23][p][pt]
                emit("tensor", lambda e, bq=bq, bk=bk, lc=lc: e.matmul(
                    bk[:, 448:449], kc_["khalfF"][bq:bq + 2, :],
                    gCcq[pt][bq:bq + 2, lc:lc + 1], start=True, stop=True),
                    reads=[kc_["khalfF"], gCcq[pt]], writes=[bk])
                emit("vector", lambda e, p=p, bk=bk: e.tensor_copy(
                    gCcol[p][:], bk[:, 448:449]), reads=[bk], writes=[gCcol[p]])
                for hf in range(2):
                    s = 2 * p + hf
                    Bh = 64 * hf
                    emit("tensor", lambda e, s=s, lc=lc, bk=bk: e.matmul(
                        bk[:, 0:64], kc_["kidB"][:], uvw[pt][s][:, lc, 0:64],
                        start=True, stop=False),
                        reads=[kc_["kidB"], uvw[pt][s]], writes=[bk])
                    emit("tensor", lambda e, s=s, p=p, lc=lc, Bh=Bh,
                         bk=bk: e.matmul(
                             bk[:, 0:64], wfm[p][Bh:Bh + 64, lc, :],
                             psb[p][Bh:Bh + 64, :], start=False, stop=True),
                         reads=[wfm[p], psb[p]], writes=[bk])
                    emit("scalar", lambda e, s=s, bk=bk: e.copy(
                        ubuf[s][:], bk[:, 0:64]), reads=[bk], writes=[ubuf[s]])
                    emit("vector", lambda e, s=s, bq=bq, hf=hf, lc=lc:
                         e.tensor_scalar(
                             utb[s][:], ubuf[s][:],
                             colsBq[pt][:, lc, 64 + bq + hf:64 + bq + hf + 1],
                             None, ALU.mult),
                         reads=[ubuf[s], colsBq[pt]], writes=[utb[s]])
                emit("tensor", lambda e, bq=bq, bk=bk, lc=lc: e.matmul(
                    bk[:, 64:192], kc_["khalfB"][bq:bq + 2, :],
                    expAq[pt][bq:bq + 2, lc, :], start=True, stop=True),
                    reads=[kc_["khalfB"], expAq[pt]], writes=[bk])
                emit("vector", lambda e, p=p, lc=lc, qkb=qkb, bk=bk:
                     e.tensor_tensor(qgb[p][:], qkb[:, lc, 0:128],
                                     bk[:, 64:192], ALU.mult),
                     reads=[qkb, bk], writes=[qgb[p]])
                for hf in range(2):
                    s = 2 * p + hf
                    Bh = 64 * hf
                    emit("tensor", lambda e, s=s, hf=hf, bk=bk, lc=lc,
                         fs=(hf == 0): e.matmul(
                             bk[64 * hf:64 * hf + 64, 192:320],
                             ubuf[s][:], attnq[pt][s][:, lc, :], start=fs,
                             stop=False),
                         reads=[ubuf[s], attnq[pt][s]], writes=[bk])
                    emit("tensor", lambda e, s=s, p=p, hf=hf, Bh=Bh, bk=bk,
                         lt=(hf == 1): e.matmul(
                             bk[64 * hf:64 * hf + 64, 192:320],
                             psb[p][Bh:Bh + 64, :], qgb[p][Bh:Bh + 64, :],
                             start=False, stop=lt),
                         reads=[psb[p], qgb[p]], writes=[bk])
                emit("vector", lambda e, p=p, pt=pt, lc=lc, bk=bk: e.tensor_copy(
                    oseg[pt][:, p, lc * 128:(lc + 1) * 128], bk[:, 192:320]),
                    reads=[bk], writes=[oseg[pt]])
                for hf in range(2):
                    s = 2 * p + hf
                    emit("tensor", lambda e, s=s, hf=hf, pt=pt, lc=lc, bk=bk:
                         e.matmul(bk[64 * hf:64 * hf + 64, 320:384],
                                  ktmv[pt][:, lc, s * 64:s * 64 + 64],
                                  utb[s][:], start=(hf == 0), stop=(hf == 1)),
                         reads=[ktmv[pt], utb[s]], writes=[bk])
                emit("vector", lambda e, p=p, bk=bk: e.scalar_tensor_tensor(
                    ps32[p][:], ps32[p][:], gCcol[p][:], bk[:, 320:384],
                    ALU.mult, ALU.add),
                    reads=[ps32[p], gCcol[p], bk], writes=[ps32[p]])
                emit("scalar", lambda e, p=p: e.copy(psb[p][:], ps32[p][:]),
                     reads=[ps32[p]], writes=[psb[p]])
            drain(4)
        drain()
        emit("gpsimd", lambda e, t=t, pt=pt: e.dma_start(
            out=aginv[:, :, t * TILE:(t + 1) * TILE], in_=oseg[pt][:]),
            reads=[oseg[pt]], writes=[agin], sem="dg", inc=16, queue="gpsq")

    def FFN(fp):
        base = fp * 512
        emit("sync", lambda e, base=base: e.dma_start(
            out=fxb[:], in_=xslv[:, :, base:base + 513]), writes=[fxb], inc=16)
        for kc in range(8):
            emit("vector", lambda e, kc=kc: e.tensor_copy(
                ybuf[kc % 2][:], fxb[:, kc, 0:512]),
                reads=[fxb], writes=[ybuf[kc % 2]])
            emit("scalar", lambda e, kc=kc: e.activation(
                y2buf[kc % 2][:], fxb[:, kc, 0:512], AF.Square),
                reads=[fxb], writes=[y2buf[kc % 2]])
            emit("tensor", lambda e, kc=kc: e.matmul(
                pb[2][0:1, :], kc_["ko1b"][:], ybuf[kc % 2][:],
                start=(kc == 0), stop=(kc == 7)),
                reads=[ybuf[kc % 2], kc_["ko1b"]], writes=[pb[2]])
            emit("tensor", lambda e, kc=kc: e.matmul(
                pb[3][0:1, :], kc_["ko1b"][:], y2buf[kc % 2][:],
                start=(kc == 0), stop=(kc == 7)),
                reads=[y2buf[kc % 2], kc_["ko1b"]], writes=[pb[3]])
        for kc in range(8):
            emit("vector", lambda e, kc=kc: e.tensor_copy(
                ybuf[kc % 2][:, 0:1], fxb[:, kc, 512:513]),
                reads=[fxb], writes=[ybuf[kc % 2]])
            emit("scalar", lambda e, kc=kc: e.activation(
                y2buf[kc % 2][:, 0:1], fxb[:, kc, 512:513], AF.Square),
                reads=[fxb], writes=[y2buf[kc % 2]])
            emit("tensor", lambda e, kc=kc: e.matmul(
                pb[2][32:33, 0:1], kc_["ko1b"][:], ybuf[kc % 2][:, 0:1],
                start=(kc == 0), stop=(kc == 7)),
                reads=[ybuf[kc % 2], kc_["ko1b"]], writes=[pb[2]])
            emit("tensor", lambda e, kc=kc: e.matmul(
                pb[3][32:33, 0:1], kc_["ko1b"][:], y2buf[kc % 2][:, 0:1],
                start=(kc == 0), stop=(kc == 7)),
                reads=[y2buf[kc % 2], kc_["ko1b"]], writes=[pb[3]])
        emit("vector", lambda e: e.tensor_scalar(
            rowq[0][:, 0:512], pb[2][0:1, :], 1.0 / D, None, ALU.mult),
            reads=[pb[2]], writes=[rowq[0]])
        emit("vector", lambda e: e.tensor_scalar(
            mrrow[:, 0:1], pb[2][32:33, 0:1], 1.0 / D, None, ALU.mult),
            reads=[pb[2]], writes=[mrrow])
        emit("vector", lambda e: e.tensor_tensor(
            rowq[1][:, 0:512], rowq[0][:, 0:512], rowq[0][:, 0:512], ALU.mult),
            reads=[rowq[0]], writes=[rowq[1]])
        emit("vector", lambda e: e.scalar_tensor_tensor(
            rowq[2][:, 0:512], pb[3][0:1, :], 1.0 / D, rowq[1][:, 0:512],
            ALU.mult, ALU.subtract), reads=[pb[3], rowq[1]], writes=[rowq[2]])
        emit("scalar", lambda e: e.activation(
            rowq[3][:, 0:512], rowq[2][:, 0:512], AF.Ln,
            bias=kc_["keps"][0:1, 0:1]),
            reads=[rowq[2], kc_["keps"]], writes=[rowq[3]])
        emit("scalar", lambda e: e.activation(
            rowq[4][:, 0:512], rowq[3][:, 0:512], AF.Exp, scale=-0.5),
            reads=[rowq[3]], writes=[rowq[4]])
        emit("vector", lambda e: e.tensor_tensor(
            mrrow[:, 1:513], rowq[0][:, 0:512], rowq[4][:, 0:512], ALU.mult),
            reads=[rowq[0], rowq[4]], writes=[mrrow])
        emit("vector", lambda e: e.tensor_tensor(
            hcol[0][0:1, :], mrrow[0:1, 0:1], mrrow[0:1, 0:1], ALU.mult),
            reads=[mrrow], writes=[hcol[0]])
        emit("vector", lambda e: e.scalar_tensor_tensor(
            hcol[0][0:1, :], pb[3][32:33, 0:1], 1.0 / D, hcol[0][0:1, :],
            ALU.mult, ALU.subtract), reads=[pb[3], hcol[0]], writes=[hcol[0]])
        emit("scalar", lambda e: e.activation(
            hcol[0][0:1, :], hcol[0][0:1, :], AF.Ln,
            bias=kc_["keps"][0:1, 0:1]),
            reads=[hcol[0], kc_["keps"]], writes=[hcol[0]])
        emit("scalar", lambda e: e.activation(
            hcol[0][0:1, :], hcol[0][0:1, :], AF.Exp, scale=-0.5),
            reads=[hcol[0]], writes=[hcol[0]])
        emit("vector", lambda e: e.tensor_tensor(
            hcol[1][0:1, :], mrrow[0:1, 0:1], hcol[0][0:1, :], ALU.mult),
            reads=[mrrow, hcol[0]], writes=[hcol[1]])
        emit("tensor", lambda e: e.matmul(
            pb[2][:, :], kc_["ko1f"][:], rowq[4][:, 0:512], start=True,
            stop=True), reads=[kc_["ko1f"], rowq[4]], writes=[pb[2]])
        emit("tensor", lambda e: e.matmul(
            pb[3][:, :], kc_["ko1f"][:], mrrow[:, 1:513], start=True,
            stop=True), reads=[kc_["ko1f"], mrrow], writes=[pb[3]])
        emit("scalar", lambda e: e.copy(rrep[:], pb[2][:, :]),
             reads=[pb[2]], writes=[rrep])
        emit("scalar", lambda e: e.copy(mrep[:], pb[3][:, :]),
             reads=[pb[3]], writes=[mrep])
        emit("tensor", lambda e: e.matmul(
            pb[2][:, 0:1], kc_["ko1f"][:], hcol[0][0:1, :], start=True,
            stop=True), reads=[kc_["ko1f"], hcol[0]], writes=[pb[2]])
        emit("tensor", lambda e: e.matmul(
            pb[3][:, 0:1], kc_["ko1f"][:], hcol[1][0:1, :], start=True,
            stop=True), reads=[kc_["ko1f"], hcol[1]], writes=[pb[3]])
        emit("scalar", lambda e: e.copy(hcol[0][:], pb[2][:, 0:1]),
             reads=[pb[2]], writes=[hcol[0]])
        emit("scalar", lambda e: e.copy(hcol[1][:], pb[3][:, 0:1]),
             reads=[pb[3]], writes=[hcol[1]])
        for kc in range(8):
            emit("gpsimd", lambda e, kc=kc: e.tensor_tensor(
                t1b[kc % 2][:], fxb[:, kc, 0:512], rrep[:], ALU.mult),
                reads=[fxb, rrep], writes=[t1b[kc % 2]])
            emit("vector", lambda e, kc=kc: e.scalar_tensor_tensor(
                fxn[:, kc, 0:512], mrep[:], -1.0, t1b[kc % 2][:], ALU.mult,
                ALU.add), reads=[mrep, t1b[kc % 2]], writes=[fxn])
        for kc in range(8):
            emit("gpsimd", lambda e, kc=kc: e.tensor_tensor(
                t1b[kc % 2][:, 0:1], fxb[:, kc, 512:513], hcol[0][:], ALU.mult),
                reads=[fxb, hcol[0]], writes=[t1b[kc % 2]])
            emit("vector", lambda e, kc=kc: e.scalar_tensor_tensor(
                fxn[:, kc, 512:513], hcol[1][:], -1.0, t1b[kc % 2][:, 0:1],
                ALU.mult, ALU.add), reads=[hcol[1], t1b[kc % 2]], writes=[fxn])
        for stage in range(3):
            for q4 in range(4):
                emit("sync", lambda e, stage=stage, q4=q4: e.dma_start(
                    out=wstg[:], in_=wffnv[:, :, stage * 1024 + q4 * 256:
                                           stage * 1024 + (q4 + 1) * 256]),
                    writes=[wstg], inc=16)
                emit("vector", lambda e, q4=q4: e.tensor_copy(
                    wbf[:, :, q4 * 256:(q4 + 1) * 256], wstg[:]),
                    reads=[wstg], writes=[wbf])
            if stage < 2:
                for kc in range(8):
                    emit("vector", lambda e, kc=kc, stage=stage: e.tensor_scalar(
                        fmt[:], fxn[:, kc, 1:513], mub[:, kc, stage:stage + 1],
                        None, ALU.mult), reads=[fxn, mub], writes=[fmt])
                    emit("vector", lambda e, kc=kc, stage=stage:
                         e.scalar_tensor_tensor(
                             fmx[:, kc], fxn[:, kc, 0:512],
                             mu1b[:, kc, stage:stage + 1], fmt[:],
                             ALU.mult, ALU.add),
                         reads=[fxn, mu1b, fmt], writes=[fmx])
                rhs = fmx
            else:
                rhs = kab
            for mi in range(8):
                bk = pb[mi % 2]
                for kc in range(8):
                    emit("tensor", lambda e, kc=kc, mi=mi, bk=bk, rhs=rhs:
                         e.matmul(bk[:, :], wbf[:, kc, mi * 128:(mi + 1) * 128],
                                  rhs[:, kc], start=(kc == 0), stop=(kc == 7)),
                         reads=[wbf, rhs], writes=[bk])
                if stage == 0:
                    emit("scalar", lambda e, bk=bk: e.activation(
                        ebuf[:], bk[:, :], AF.Exp, scale=-1.0),
                        reads=[bk], writes=[ebuf])
                    emit("vector", lambda e: e.tensor_scalar(
                        tbb[:], ebuf[:], 1.0, None, ALU.add),
                        reads=[ebuf], writes=[tbb])
                    emit("vector", lambda e: e.reciprocal(
                        ftmp[:], tbb[:]), reads=[tbb], writes=[ftmp])
                    emit("vector", lambda e, mi=mi: e.tensor_copy(
                        rfb[:, mi], ftmp[:]), reads=[ftmp], writes=[rfb])
                elif stage == 1:
                    emit("vector", lambda e, bk=bk: e.tensor_scalar(
                        ftmp[:], bk[:, :], 0.0, None, ALU.max),
                        reads=[bk], writes=[ftmp])
                    emit("scalar", lambda e, mi=mi: e.activation(
                        kab[:, mi], ftmp[:], AF.Square),
                        reads=[ftmp], writes=[kab])
                else:
                    emit("vector", lambda e, mi=mi, bk=bk: e.scalar_tensor_tensor(
                        ftmp[:], bk[:, :], 1.0, rfb[:, mi], ALU.mult, ALU.mult),
                        reads=[bk, rfb], writes=[ftmp])
                    emit("vector", lambda e, mi=mi: e.tensor_tensor(
                        fresp[mi % 2][:], ftmp[:], fxb[:, mi, 1:513], ALU.add),
                        reads=[ftmp, fxb], writes=[fresp[mi % 2]])
                    emit("gpsimd", lambda e, base=base, mi=mi: e.dma_start(
                        out=dresv[:, mi, base:base + 512], in_=fresp[mi % 2][:]),
                        reads=[fresp[mi % 2]], writes=[dresid], sem="dg",
                        inc=16, queue="gpsq")

    XL(0)
    P1(0)
    for t in range(NT):
        P23(t, nxt=t + 1 if t + 1 < NT else None)

    emit("gpsimd", lambda e: e.collective_compute(
        "AllGather", ALU.bypass, replica_groups=[[0, 1, 2, 3], [4, 5, 6, 7]],
        ins=[agin.ap().opt()], outs=[agout.ap().opt()]),
        reads=[agin], writes=[agout], sem="cc", inc=1, queue="gpsq")

    fence()
    sctx.close()
    wbf = SB("wbf", [128, 8, 1024], BF16)
    wotb = SB("wotb", [128, 8, 1024], BF16)
    fxb = SB("fxb", [128, 8, 513], F32)
    fxn = SB("fxn", [128, 8, 513], BF16)
    rfb = SB("rfb", [128, 8, 512], BF16)
    kab = SB("kab", [128, 8, 512], BF16)
    ftmp = SB("ftmp", [128, 512], F32)
    fmx = xnb
    fresp = t1b
    otile = [xtile[0][:, :, 0:256].bitcast(BF16),
             xtile[0][:, :, 256:512].bitcast(BF16)]
    _resb = xtile[1][:]
    resb = [_resb, _resb]
    _outsb = SB("outsb", [128, 8, 512], F32)[:]
    outsb = [_outsb, _outsb]


    for q4 in range(4):
        emit("sync", lambda e, q4=q4: e.dma_start(
            out=wstg[:], in_=woutv[:, :, q4 * 256:(q4 + 1) * 256]),
            writes=[wstg], inc=16)
        emit("vector", lambda e, q4=q4: e.tensor_copy(
            wotb[:, :, q4 * 256:(q4 + 1) * 256], wstg[:]),
            reads=[wstg], writes=[wotb])

    def OUT(ot):
        def oload(e, ot=ot):
            if rank_sv[0] is None:
                rank_sv[0] = e.snap(ridx_reg)
            return e.dma_start(
                out=otile[ot % 2][:],
                in_=agoutv.transpose([1, 0, 2])[:, :, bass.ds(rank_sv[0] * SLICE + ot * 512, 512)])
        emit("gpsimd", oload, reads=[agout], writes=[otile[ot % 2]],
             sem="dg", inc=16, queue="gpsq")
        emit("sync", lambda e, ot=ot: e.dma_start(
            out=resb[ot % 2][:], in_=dresv[:, :, ot * 512:(ot + 1) * 512]),
            reads=[dresid], writes=[resb[ot % 2]], inc=16)
        for mi in range(8):
            bk = pb[2 + mi % 2]
            for kc in range(8):
                emit("tensor", lambda e, kc=kc, mi=mi, ot=ot, bk=bk: e.matmul(
                    bk[:, :], wotb[:, kc, mi * 128:(mi + 1) * 128],
                    otile[ot % 2][:, kc], start=(kc == 0), stop=(kc == 7)),
                    reads=[wotb, otile[ot % 2]], writes=[bk])
            emit("vector", lambda e, mi=mi, ot=ot, bk=bk: e.scalar_tensor_tensor(
                outsb[ot % 2][:, mi], bk[:, :], 1.0, resb[ot % 2][:, mi],
                ALU.mult, ALU.add), reads=[bk, resb[ot % 2]],
                writes=[outsb[ot % 2]])
        emit("gpsimd", lambda e, ot=ot: e.dma_start(
            out=outpv[:, :, ot * 512:(ot + 1) * 512], in_=outsb[ot % 2][:]),
            reads=[outsb[ot % 2]], writes=[outp], sem="dg", inc=16,
            queue="gpsq")


    for fp in range(4):
        FFN(fp)
    for ot in range(4):
        OUT(ot)

    fin_dg = cnt["dg"]
    fin_dx = cnt["dx"]
    steps["gpsimd"].append(lambda e: e.wait_ge(sems["dg"], fin_dg))
    steps["gpsimd"].append(lambda e: e.wait_ge(sems["dx"], fin_dx))

    with ctx:
        with nc.Block() as block:
            @block.sync
            def _(e):
                for fn in steps["sync"]:
                    fn(e)

            @block.tensor
            def _(e):
                for fn in steps["tensor"]:
                    fn(e)

            @block.vector
            def _(e):
                for fn in steps["vector"]:
                    fn(e)

            @block.scalar
            def _(e):
                for fn in steps["scalar"]:
                    fn(e)

            @block.gpsimd
            def _(e):
                for fn in steps["gpsimd"]:
                    fn(e)
    return nc


dbg_taps_emit = None


def _host_inputs(inputs):
    x = np.asarray(inputs["L0_attn_x"], np.float32)
    Wqkv = np.asarray(inputs["L0_attn_Wqkv"], np.float32)
    Wg = np.asarray(inputs["L0_attn_Wg"], np.float32)
    bg = np.asarray(inputs["L0_attn_bg"], np.float32)
    Wb = np.asarray(inputs["L0_attn_Wb"], np.float32)
    bb = np.asarray(inputs["L0_attn_bb"], np.float32)
    Wout = np.asarray(inputs["L0_attn_Wout"], np.float32)
    W_r = np.asarray(inputs["L0_ffn_W_r"], np.float32)
    W_k = np.asarray(inputs["L0_ffn_W_k"], np.float32)
    W_v = np.asarray(inputs["L0_ffn_W_v"], np.float32)
    mu_r = np.asarray(inputs["L0_ffn_mu_r"], np.float32)
    mu_k = np.asarray(inputs["L0_ffn_mu_k"], np.float32)

    jj, ii = np.meshgrid(np.arange(128), np.arange(128), indexing="ij")
    maskS = np.where(jj < ii, 0.0, NEG).astype(np.float32)
    maskI = np.where(jj <= ii, 0.0, NEG).astype(np.float32)
    sel0 = np.zeros((128, 128), np.float32)
    sel0[np.arange(128) % 32 == 0, :] = 1.0
    sel1 = np.zeros((128, 128), np.float32)
    sel1[np.arange(128) % 32 == 1, :] = 1.0
    halfF = np.zeros((128, 128), np.float32)
    halfF[np.arange(128) % 32 == 0, 0:64] = 1.0
    halfF[np.arange(128) % 32 == 1, 64:128] = 1.0
    css = np.zeros((128, 4, 128), np.float32)
    for part in range(4):
        c0 = [0, 32, 64, 96][part]
        css[0:64, part, c0] = 1.0
        css[64:128, part, c0 + 1] = 1.0
    ident = np.eye(128, dtype=np.float32)
    mu = np.stack([mu_r.reshape(8, 128).T, mu_k.reshape(8, 128).T], axis=-1)
    wffn = np.hstack([W_r.T, W_k.T, W_v.T]).astype(np.float32)
    wout_t = np.ascontiguousarray(Wout.T)

    ins = []
    for c in range(NCORE):
        b, s = c // 4, c % 4
        xslice = np.empty((D, SLICE + 1), np.float32)
        xslice[:, 1:] = x[b, s * SLICE:(s + 1) * SLICE].T
        xslice[:, 0] = x[b, s * SLICE - 1] if s > 0 else 0.0
        h0 = 4 * s
        wp1c = np.zeros((D, 640), np.float32)
        wp1c[:, 0:128] = Wqkv[h0 * 64:(h0 + 2) * 64].T
        wp1c[:, 128:256] = Wqkv[(h0 + 2) * 64:(h0 + 4) * 64].T
        wp1c[:, 256:384] = Wqkv[D + h0 * 64:D + (h0 + 2) * 64].T
        wp1c[:, 384:512] = Wqkv[D + (h0 + 2) * 64:D + (h0 + 4) * 64].T
        for j in range(2):
            wp1c[:, 512 + j] = Wg[h0 + j]
            wp1c[:, 512 + 32 + j] = Wg[h0 + 2 + j]
            wp1c[:, 512 + 64 + j] = Wb[h0 + j]
            wp1c[:, 512 + 96 + j] = Wb[h0 + 2 + j]
        wtmc = np.hstack([Wqkv[D + h0 * 64:D + h0 * 64 + 256].T,
                          Wqkv[2 * D + h0 * 64:2 * D + h0 * 64 + 256].T])
        gb = np.zeros((98, 1), np.float32)
        for j in range(2):
            gb[j, 0] = bg[h0 + j]
            gb[32 + j, 0] = bg[h0 + 2 + j]
            gb[64 + j, 0] = bb[h0 + j]
            gb[96 + j, 0] = bb[h0 + 2 + j]
        ins.append(dict(
            xt=np.ascontiguousarray(x[b].T), xsl=xslice, wp1=wp1c,
            wtm=np.ascontiguousarray(wtmc), wffn=wffn, wout=wout_t,
            mu=np.ascontiguousarray(mu), gb=gb,
            ridx=np.array([[s]], np.int32),
            c_maskS=maskS, c_maskI=maskI, c_sel0=sel0, c_sel1=sel1,
            c_maskSR=np.ascontiguousarray(maskS.T),
            c_halfF=halfF, c_halfB=halfF.astype(BFNP),
            c_idB=ident.astype(BFNP), c_idF=ident,
            c_ss=css.astype(BFNP),
            c_o1b=np.ones((128, 1), BFNP),
            c_o1f=np.ones((1, 128), np.float32),
            c_eps=np.stack([np.full(128, 1e-5, np.float32),
                            np.zeros(128, np.float32)], axis=1),
        ))
    return ins


def kernel(**inputs):
    ins = _host_inputs(inputs)
    nc = _build()
    res = run_bass_kernel_spmd(nc, ins, core_ids=list(range(NCORE)))
    out = np.empty((B, T, D), np.float32)
    for c in range(NCORE):
        b, s = c // 4, c % 4
        out[b, s * SLICE:(s + 1) * SLICE, :] = np.asarray(
            res.results[c]["outp"], np.float32).T
    return out
